# version 09
# baseline (speedup 1.0000x reference)
import contextlib
import math
import numpy as np
import ml_dtypes
import concourse.bass as bass
import concourse.mybir as mybir
from concourse.bass_utils import run_bass_kernel_spmd

F32 = mybir.dt.float32
BF16 = mybir.dt.bfloat16
I32 = mybir.dt.int32
AF = mybir.ActivationFunctionType
OP = mybir.AluOpType
AX = mybir.AxisListType

T_SEQ = 16384
D = 1024
NCORES = 8
TOK = T_SEQ // NCORES
WARM = 384
QW = WARM + TOK
HFF0 = 3072
DIN = 2048
NST = 16
DTR = 64
KCV = 4
HFF1 = 4096
LN_EPS = 1e-5
FF0S = 1.0 / 127.0
NTC = QW // 128
NTO = TOK // 128

E2M1_MIDS = [0.25, 0.75, 1.25, 1.75, 2.5, 3.5, 5.0]


def _band_width(decay_logit):
    decay = 1.0 / (1.0 + math.exp(-float(decay_logit)))
    if decay <= 1e-6:
        return 512
    dstar = math.log(1.4e-45) / math.log(decay)
    w = int(math.ceil(1.2 * dstar / 512.0)) * 512
    return max(512, min(T_SEQ, w))


class P:

    RANGES = {"e": (0, 50), "h": (50, 84), "s": (84, 90)}
    SWDGE_PFX = ("wq1st", "f1xst", "f1gst", "cbst", "w0c")

    def __init__(self, nc, stack):
        self.nc = nc
        self.stack = stack
        self.pool = {}
        self.tot = {}
        self.map = {}
        self.counts = {"e": 0, "h": 0, "s": 0}
        self.marks = {}

    @classmethod
    def _cls(cls, name):
        if any(name.startswith(x) for x in cls.SWDGE_PFX):
            return "s"
        if ("ld" in name or "st" in name or
                name in ("c0b", "mfum", "wqs", "wqs2", "wqs3", "wqs4", "wqb", "cbbc")):
            return "h"
        return "e"

    def _ensure(self, name):
        if name not in self.map:
            c = self._cls(name)
            lo, hi = self.RANGES[c]
            pidx = lo + self.counts[c] % (hi - lo)
            self.counts[c] += 1
            if pidx not in self.pool:
                self.pool[pidx] = self.stack.enter_context(self.nc.semaphore("sp%d" % pidx))
                self.tot[pidx] = 0
            self.map[name] = (pidx, self.tot[pidx])
        return self.map[name]

    def s(self, name):
        pidx, _ = self._ensure(name)
        return self.pool[pidx]

    def inc(self, name, k):
        pidx, _ = self._ensure(name)
        self.tot[pidx] += k
        return self.pool[pidx], k

    def v(self, name):
        pidx, _ = self._ensure(name)
        return self.tot[pidx]

    def wge(self, eng, name, val):
        pidx, base = self._ensure(name)
        eng.wait_ge(self.pool[pidx], base + val)

    def mark(self, mname, sname):
        pidx, _ = self._ensure(sname)
        self.marks[mname] = (self.pool[pidx], self.tot[pidx])

    def wait(self, eng, mname):
        h, v = self.marks[mname]
        if v > 0:
            eng.wait_ge(h, v)

    def endphase(self, *mnames):
        self.prev_marks = list(mnames)

    def gate(self, eng):
        for m in getattr(self, "prev_marks", []):
            self.wait(eng, m)


def build_program(W):
    nc = bass.Bass("TRN2", target_bir_lowering=False)
    stack = contextlib.ExitStack()
    p = P(nc, stack)
    EXT = QW + W
    assert EXT % 128 == 0
    NB = (W + 512) // 128

    I = {}
    def din(name, shape, dtype=F32):
        I[name] = nc.dram_tensor(name, shape, dtype, kind="ExternalInput").ap()
    din("x", [EXT, D])
    din("decay_logit", [1, 1]); din("out_scale", [1, 1])
    din("q_scale", [1, D]); din("k_scale", [1, D])
    din("w_gate", [HFF0, D], I32); din("w_up", [HFF0, D], I32); din("w_down", [D, HFF0], I32)
    din("W_in", [2 * DIN, D]); din("conv_w", [DIN, KCV]); din("conv_b", [1, DIN])
    din("W_x_t", [DIN, DTR + 2 * NST]); din("W_dt_t", [DTR, DIN]); din("b_dt", [1, DIN])
    din("A_log", [DIN, NST]); din("Dp", [1, DIN]); din("W_out", [D, DIN])
    din("g1_w", [HFF1, D]); din("u1_w", [HFF1, D]); din("d1_w", [D, HFF1])
    din("s_in", [1, 1]); din("s_in_down", [1, 1])
    din("umask", [1, QW])
    out = nc.dram_tensor("out", [TOK, D], F32, kind="ExternalOutput").ap()

    S = {}
    def scr(name, shape, dtype=F32):
        S[name] = nc.dram_tensor("sc_" + name, shape, dtype, kind="Internal").ap()
    scr("xn_hi", [EXT, D], BF16); scr("xn_lo", [EXT, D], BF16)
    scr("qp_hi", [QW, D], BF16); scr("qp_lo", [QW, D], BF16)
    scr("x1", [QW, D])
    scr("row", [8, 4096])
    scr("wg_b", [HFF0, D], BF16); scr("wu_b", [HFF0, D], BF16); scr("wd_b", [D, HFF0], BF16)
    scr("win_hi", [2 * DIN, D], BF16); scr("win_lo", [2 * DIN, D], BF16)
    scr("wout_hi", [D, DIN], BF16); scr("wout_lo", [D, DIN], BF16)
    scr("wx_hi", [DIN, DTR + 2 * NST], BF16); scr("wx_lo", [DIN, DTR + 2 * NST], BF16)
    scr("wdt_hi", [DTR, DIN], BF16); scr("wdt_lo", [DTR, DIN], BF16)
    scr("uwg", [HFF1, D], BF16); scr("uwu", [HFF1, D], BF16); scr("uwd", [D, HFF1], BF16)
    scr("xc", [DIN, QW]); scr("xc_hi", [DIN, QW], BF16); scr("xc_lo", [DIN, QW], BF16)
    scr("lns", [DIN, QW])
    scr("bc", [2 * NST, QW])
    scr("siluz", [DIN, TOK])
    scr("ysz_hi", [DIN, TOK], BF16); scr("ysz_lo", [DIN, TOK], BF16)
    scr("hh_hi", [QW, HFF0], BF16); scr("hh_lo", [QW, HFF0], BF16)
    scr("xn1f", [QW, D]); scr("xn1_hi", [QW, D], BF16); scr("xn1_lo", [QW, D], BF16)
    scr("ux", [TOK, D], BF16); scr("uh", [TOK, HFF1], BF16)

    sb = {}
    def sbuf(name, shape, dtype=F32):
        sb[name] = stack.enter_context(nc.sbuf_tensor("sb_" + name, shape, dtype))
        return sb[name]
    ps = [stack.enter_context(nc.psum_tensor("ps%d" % i, [128, 512], F32)) for i in range(8)]

    cst = sbuf("cst", [128, 40])
    convw = sbuf("convw", [128, 16 * KCV])
    convb = sbuf("convb", [128, 16])
    bdtn = sbuf("bdtn", [128, 16])
    dpc = sbuf("dpc", [128, 16])
    ealog = sbuf("ealog", [128, 16 * NST])
    state = sbuf("state", [128, 16 * NST])
    astack = contextlib.ExitStack()
    def asbuf(name, shape, dtype=F32):
        sb[name] = astack.enter_context(nc.sbuf_tensor("sb_" + name, shape, dtype))
        return sb[name]
    b_tile = asbuf("b_tile", [128, 512])
    a_mat = asbuf("a_mat", [128, NB + 2])
    wts = [asbuf("wt%d" % k, [128, 512]) for k in range(4)]
    qk_b = asbuf("qk_b", [128, D])

    C_LNDEC, C_NLNDEC, C_OS, C_SIN6, C_SIN, C_SIND, C_GWG, C_GWU, C_GWD, \
        C_SGA, C_SUA, C_SDA, C_LN2, C_SIN6D = range(14)

    eng = {}

    c0stack = contextlib.ExitStack()
    with nc.named_scope("c0"), nc.Block("c0") as blk:
        t0 = c0stack.enter_context(nc.sbuf_tensor("c0_t0", [1, 4096], F32))
        t1 = c0stack.enter_context(nc.sbuf_tensor("c0_t1", [1, 4096], F32))
        io0 = c0stack.enter_context(nc.sbuf_tensor("c0_io", [128, 2144], F32))

        @blk.sync
        def _(sync):
            p.gate(sync)
            sync.dma_start(out=t0[0:1, 0:1], in_=I["decay_logit"]).then_inc(*p.inc("c0ld", 16))
            sync.dma_start(out=t0[0:1, 1:2], in_=I["out_scale"]).then_inc(*p.inc("c0ld", 16))
            sync.dma_start(out=t0[0:1, 2:3], in_=I["s_in"]).then_inc(*p.inc("c0ld", 16))
            sync.dma_start(out=t0[0:1, 3:4], in_=I["s_in_down"]).then_inc(*p.inc("c0ld", 16))
            sync.dma_start(out=t0[0:1, 16:16 + D], in_=I["q_scale"]).then_inc(*p.inc("c0ld", 16))
            sync.dma_start(out=t0[0:1, 16 + D:16 + 2 * D], in_=I["k_scale"]).then_inc(*p.inc("c0ld", 16))
            sync.dma_start(out=convw[:, :].rearrange("p (c k) -> p c k", k=KCV),
                           in_=I["conv_w"].rearrange("(c p) k -> p c k", p=128)).then_inc(*p.inc("c0ld", 16))
            sync.dma_start(out=convb[:, :].rearrange("p (o c) -> p o c", o=1),
                           in_=I["conv_b"].rearrange("o (c p) -> p o c", p=128),
                           allow_slow_non_contiguous=True).then_inc(*p.inc("c0ld", 16))
            sync.dma_start(out=bdtn[:, :].rearrange("p (o c) -> p o c", o=1),
                           in_=I["b_dt"].rearrange("o (c p) -> p o c", p=128),
                           allow_slow_non_contiguous=True).then_inc(*p.inc("c0ld", 16))
            sync.dma_start(out=dpc[:, :].rearrange("p (o c) -> p o c", o=1),
                           in_=I["Dp"].rearrange("o (c p) -> p o c", p=128),
                           allow_slow_non_contiguous=True).then_inc(*p.inc("c0ld", 16))
            sync.dma_start(out=ealog[:, :].rearrange("p (c n) -> p c n", n=NST),
                           in_=I["A_log"].rearrange("(c p) n -> p c n", p=128)).then_inc(*p.inc("c0ld", 16))

        @blk.gpsimd
        def _(g):
            p.gate(g)
            g.iota(t1[0:1, 0:512], [[1, 512]], channel_multiplier=0,
                   allow_small_or_imprecise_dtypes=True)
            g.iota(t1[0:1, 512:512 + NB + 2], [[128, NB + 2]], base=-1, channel_multiplier=0,
                   allow_small_or_imprecise_dtypes=True)
            g.iota(io0[:, 0:1], [[0, 1]], channel_multiplier=1,
                   allow_small_or_imprecise_dtypes=True).then_inc(*p.inc("c0io", 1))
            for k in range(4):
                g.iota(io0[:, 1 + k * 512:1 + (k + 1) * 512], [[-1, 512]], base=128 * k - 1,
                       channel_multiplier=1, allow_small_or_imprecise_dtypes=True)
            g.iota(io0[:, 1 + 4 * 512:1 + 4 * 512 + 1], [[0, 1]], base=0,
                   channel_multiplier=0, allow_small_or_imprecise_dtypes=True).then_inc(*p.inc("c0io", 1))

        @blk.scalar
        def _(se):
            p.gate(se)
            se.wait_ge(p.s("c0ld"), p.v("c0ld"))
            se.activation(out=t0[0:1, 4:5], in_=t0[0:1, 0:1], func=AF.Sigmoid)
            se.drain()
            se.activation(out=t0[0:1, 5:6], in_=t0[0:1, 4:5], func=AF.Ln).then_inc(*p.inc("c0a", 1))
            se.activation(out=ealog[:, :], in_=ealog[:, :], func=AF.Exp).then_inc(*p.inc("c0a", 1))

        @blk.vector
        def _(v):
            p.gate(v)
            p.wge(v, "c0a", 1)
            v.tensor_scalar(t0[0:1, 6:7], t0[0:1, 5:6], -1.0, None, OP.mult)
            v.tensor_tensor(t0[0:1, 16:16 + D], t0[0:1, 16:16 + D], t0[0:1, 16 + D:16 + 2 * D], OP.mult)
            v.tensor_scalar(t1[0:1, 1024:1025], t0[0:1, 2:3], 1.0 / 6.0, None, OP.mult)
            v.drain().then_inc(*p.inc("c0v", 1))
            v.tensor_scalar(bdtn[:, :], bdtn[:, :], -1.0, None, OP.mult).then_inc(*p.inc("c0v", 1))

        @blk.scalar
        def _(se):
            p.gate(se)
            p.wge(se, "c0v", 1)
            p.wge(se, "c0io", 1)
            se.activation(out=t1[0:1, 0:512], in_=t1[0:1, 0:512], func=AF.Exp,
                          scale=t0[0:1, 6:7]).then_inc(*p.inc("c0a", 1))
            se.activation(out=t1[0:1, 512:512 + NB + 2], in_=t1[0:1, 512:512 + NB + 2], func=AF.Exp,
                          scale=t0[0:1, 5:6]).then_inc(*p.inc("c0a", 1))

        @blk.sync
        def _(sync):
            p.gate(sync)
            sync.wait_ge(p.s("c0a"), p.v("c0a"))
            sync.dma_start(out=S["row"][0:1, 0:7], in_=t0[0:1, 0:7]).then_inc(*p.inc("c0st", 16))
            sync.dma_start(out=S["row"][0:1, 16:16 + D], in_=t0[0:1, 16:16 + D]).then_inc(*p.inc("c0st", 16))
            sync.dma_start(out=S["row"][1:2, 0:512 + NB + 2], in_=t1[0:1, 0:512 + NB + 2]).then_inc(*p.inc("c0st", 16))
            sync.dma_start(out=S["row"][1:2, 1024:1025], in_=t1[0:1, 1024:1025]).then_inc(*p.inc("c0st", 16))
            sync.wait_ge(p.s("c0st"), p.v("c0st"))
            sync.dma_start(out=cst[:, C_LNDEC:C_LNDEC + 1],
                           in_=S["row"][0:1, 5:6].partition_broadcast(128)).then_inc(*p.inc("c0b", 16))
            sync.dma_start(out=cst[:, C_NLNDEC:C_NLNDEC + 1],
                           in_=S["row"][0:1, 6:7].partition_broadcast(128)).then_inc(*p.inc("c0b", 16))
            sync.dma_start(out=cst[:, C_OS:C_OS + 1],
                           in_=S["row"][0:1, 1:2].partition_broadcast(128)).then_inc(*p.inc("c0b", 16))
            sync.dma_start(out=cst[:, C_SIN:C_SIN + 1],
                           in_=S["row"][0:1, 2:3].partition_broadcast(128)).then_inc(*p.inc("c0b", 16))
            sync.dma_start(out=cst[:, C_SIND:C_SIND + 1],
                           in_=S["row"][0:1, 3:4].partition_broadcast(128)).then_inc(*p.inc("c0b", 16))
            sync.dma_start(out=cst[:, C_SIN6:C_SIN6 + 1],
                           in_=S["row"][1:2, 1024:1025].partition_broadcast(128)).then_inc(*p.inc("c0b", 16))
            sync.dma_start(out=b_tile[:, :],
                           in_=S["row"][1:2, 0:512].partition_broadcast(128)).then_inc(*p.inc("c0b", 16))
            sync.dma_start(out=io0[:, 1 + 4 * 512 + 1:1 + 4 * 512 + 1 + NB + 2],
                           in_=S["row"][1:2, 512:512 + NB + 2].partition_broadcast(128)).then_inc(*p.inc("c0b", 16))
            sync.dma_start(out=qk_b[:, :],
                           in_=S["row"][0:1, 16:16 + D].partition_broadcast(128)).then_inc(*p.inc("c0b", 16))

        @blk.scalar
        def _(se):
            p.gate(se)
            se.wait_ge(p.s("c0b"), p.v("c0b"))
            p.wge(se, "c0io", 2)
            se.activation(out=io0[:, 0:1], in_=io0[:, 0:1], func=AF.Exp,
                          scale=cst[:, C_LNDEC:C_LNDEC + 1])
            for k in range(4):
                se.activation(out=wts[k][:, :], in_=io0[:, 1 + k * 512:1 + (k + 1) * 512],
                              func=AF.Exp, scale=cst[:, C_LNDEC:C_LNDEC + 1])
            se.drain().then_inc(*p.inc("c0a", 1))

        @blk.vector
        def _(v):
            p.gate(v)
            v.wait_ge(p.s("c0a"), p.v("c0a"))
            v.tensor_scalar(a_mat[:, :], io0[:, 1 + 4 * 512 + 1:1 + 4 * 512 + 1 + NB + 2],
                            io0[:, 0:1], None, OP.mult)
            for k in range(4):
                v.tensor_scalar(io0[:, 1 + k * 512:1 + (k + 1) * 512],
                                io0[:, 1 + k * 512:1 + (k + 1) * 512], -0.5, None, OP.is_gt)
            v.drain()
            for k in range(4):
                v.tensor_tensor(wts[k][:, :], wts[k][:, :], io0[:, 1 + k * 512:1 + (k + 1) * 512], OP.mult)
            v.memset(state[:, :], 0.0)
            v.memset(cst[:, 14:15], -10.0 * math.log(2.0))
            v.memset(cst[:, 15:16], 10.0 * math.log(2.0))
            v.memset(cst[:, 16:17], LN_EPS)
            v.drain().then_inc(*p.inc("c0v", 1))

    c0stack.close()
    p.mark("c0", "c0v")
    p.mark("c0b", "c0b")
    p.endphase("c0", "c0b")

    w0stack = contextlib.ExitStack()
    with nc.named_scope("w0"), nc.Block("w0") as blk:
        wtile = w0stack.enter_context(nc.sbuf_tensor("w0_a", [128, 2 * 4096], F32))
        whi = w0stack.enter_context(nc.sbuf_tensor("w0_hi", [128, 2 * 4096], BF16))
        wlo = w0stack.enter_context(nc.sbuf_tensor("w0_lo", [128, 2 * 4096], BF16))

        SPLITS = [("W_in", "win_hi", "win_lo", 2 * DIN, D),
                  ("W_out", "wout_hi", "wout_lo", D, DIN),
                  ("W_x_t", "wx_hi", "wx_lo", DIN, DTR + 2 * NST),
                  ("W_dt_t", "wdt_hi", "wdt_lo", DTR, DIN)]
        CASTS = [("w_gate", "wg_b", HFF0, D), ("w_up", "wu_b", HFF0, D),
                 ("w_down", "wd_b", D, HFF0)]

        def chunks(rows):
            o = 0
            while o < rows:
                yield o, min(128, rows - o)
                o += 128

        @blk.sync
        def _(sync):
            p.gate(sync)
            i = 0
            for (src, hi, lo, rows, cols) in SPLITS:
                for (o, r) in chunks(rows):
                    if i >= 2:
                        p.wge(sync, "w0v", i - 1)
                    buf = (i % 2) * 4096
                    sync.dma_start(out=wtile[0:r, buf:buf + cols],
                                   in_=I[src][o:o + r, :]).then_inc(*p.inc("w0ld%d" % (i % 2), 16))
                    i += 1

        @blk.vector
        def _(v):
            p.gate(v)
            i = 0
            for (src, hi, lo, rows, cols) in SPLITS:
                for (o, r) in chunks(rows):
                    p.wge(v, "w0ld%d" % (i % 2), 16 * (i // 2 + 1))
                    if i >= 2:
                        p.wge(v, "w0st%d" % (i % 2), 32 * (i // 2))
                    buf = (i % 2) * 4096
                    v.tensor_copy(whi[0:r, buf:buf + cols], wtile[0:r, buf:buf + cols])
                    v.drain()
                    v.tensor_tensor(wlo[0:r, buf:buf + cols], wtile[0:r, buf:buf + cols],
                                    whi[0:r, buf:buf + cols], OP.subtract)
                    v.drain().then_inc(*p.inc("w0v", 1))
                    i += 1

        @blk.scalar
        def _(se):
            p.gate(se)
            i = 0
            for (src, hi, lo, rows, cols) in SPLITS:
                for (o, r) in chunks(rows):
                    p.wge(se, "w0v", i + 1)
                    buf = (i % 2) * 4096
                    se.dma_start(out=S[hi][o:o + r, :], in_=whi[0:r, buf:buf + cols]).then_inc(*p.inc("w0st%d" % (i % 2), 16))
                    se.dma_start(out=S[lo][o:o + r, :], in_=wlo[0:r, buf:buf + cols]).then_inc(*p.inc("w0st%d" % (i % 2), 16))
                    i += 1

        @blk.gpsimd
        def _(g):
            p.gate(g)
            for (src, dst, rows, cols) in CASTS:
                g.dma_start(out=S[dst][:, :], in_=I[src][:, :]).then_inc(*p.inc("w0c", 16))

    w0stack.close()
    p.mark("w0_a", "w0st0")
    p.mark("w0_b", "w0st1")
    p.mark("w0c", "w0c")
    p.endphase("w0_a", "w0_b", "w0c")

    _emit_wquant(nc, p, stack, I, S, sb, cst, ps)

    _emit_ln(nc, p, stack, I, S, sb, src_kind="x", n_tiles=EXT // 128, qp=True,
             qk_b=qk_b, tag="ln0", beps=cst[:, 16:17])

    _emit_attn(nc, p, stack, I, S, sb, ps, cst, b_tile, a_mat, wts, W, NB, C_OS)
    astack.close()

    _emit_ffn0(nc, p, stack, I, S, sb, ps, cst)

    _emit_ln(nc, p, stack, I, S, sb, src_kind="x1", n_tiles=QW // 128, qp=False,
             qk_b=None, tag="ln1", beps=cst[:, 16:17])

    _emit_mamba_front(nc, p, stack, I, S, sb, ps, cst, convw, convb, bdtn)

    _emit_cube(nc, p, stack, I, S, sb, ps, cst, ealog, state, dpc)

    _emit_wout(nc, p, stack, I, S, sb, ps, cst)

    _emit_ffn1(nc, p, stack, I, S, sb, ps, cst,
               (C_SIN, C_SIN6, C_SGA, C_SUA, C_SDA, C_SIND, C_SIN6D))

    with nc.Block("p9") as blk:
        @blk.sync
        def _(sync):
            p.gate(sync)
            p.wait(sync, "p8_a")
            p.wait(sync, "p8_b")
            sync.dma_start(out=out[:, :], in_=S["x1"][WARM:WARM + TOK, :]).then_inc(*p.inc("p9st", 16))
            p.wge(sync, "p9st", 16)

    return nc, stack



def _emit_ln(nc, p, stack, I, S, sb, src_kind, n_tiles, qp, qk_b, tag, beps):
    with nc.named_scope(tag), nc.Block(tag) as blk:
        with contextlib.ExitStack() as ph:
            xt = [ph.enter_context(nc.sbuf_tensor(f"{tag}_x{i}", [128, D], F32)) for i in range(2)]
            xn = [ph.enter_context(nc.sbuf_tensor(f"{tag}_n{i}", [128, D], F32)) for i in range(2)]
            hi = [ph.enter_context(nc.sbuf_tensor(f"{tag}_h{i}", [128, D], BF16)) for i in range(2)]
            lo = [ph.enter_context(nc.sbuf_tensor(f"{tag}_l{i}", [128, D], BF16)) for i in range(2)]
            qh = [ph.enter_context(nc.sbuf_tensor(f"{tag}_qh{i}", [128, D], BF16)) for i in range(2)]
            ql = [ph.enter_context(nc.sbuf_tensor(f"{tag}_ql{i}", [128, D], BF16)) for i in range(2)]
            q3 = ph.enter_context(nc.sbuf_tensor(f"{tag}_q3", [128, D], F32))
            st = ph.enter_context(nc.sbuf_tensor(f"{tag}_st", [128, 24], F32))

            src = I["x"] if src_kind == "x" else S["x1"]
            hid, lod = (("xn_hi", "xn_lo") if src_kind == "x" else ("xn1_hi", "xn1_lo"))
            L, A, V, Q, ST = tag + "ld", tag + "ap", tag + "v", tag + "sq", tag + "st"

            @blk.sync
            def _(sync):
                p.gate(sync)
                if src_kind == "x1":
                    p.wait(sync, "p3_a")
                    p.wait(sync, "p3_b")
                for i in range(n_tiles):
                    if i >= 2:
                        p.wge(sync, A, i - 1)
                    sync.dma_start(out=xt[i % 2][:, :],
                                   in_=src[i * 128:(i + 1) * 128, :]).then_inc(*p.inc(L + str(i % 2), 16))

            nst_tile = [2 + (1 if src_kind == "x1" else 0) +
                        (2 if (qp and i < QW // 128) else 0) for i in range(n_tiles)]
            cum_st = [0]
            for c_ in nst_tile:
                cum_st.append(cum_st[-1] + c_)

            @blk.vector
            def _(v):
                p.gate(v)
                for i in range(n_tiles):
                    b = i % 2
                    p.wge(v, L + str(b), 16 * (i // 2 + 1))
                    if i >= 2:
                        p.wge(v, ST + str(b), 16 * sum(nst_tile[j] for j in range(b, i - 1, 2)))
                    v.bn_stats(st[:, 0:6], xt[b][:, 0:512])
                    v.drain()
                    v.bn_stats(st[:, 6:12], xt[b][:, 512:1024])
                    v.drain()
                    v.bn_aggr(st[:, 12:14], st[:, 0:12])
                    v.drain().then_inc(*p.inc(V, 1))
                    p.wge(v, Q, i + 1)
                    v.reciprocal(st[:, 15:16], st[:, 14:15])
                    v.drain()
                    v.tensor_tensor(st[:, 16:17], st[:, 15:16], st[:, 15:16], OP.mult)
                    v.tensor_scalar(st[:, 17:18], st[:, 13:14], LN_EPS, None, OP.add)
                    v.drain()
                    v.tensor_tensor(st[:, 18:19], st[:, 16:17], st[:, 17:18], OP.mult)
                    v.drain()
                    v.tensor_scalar(st[:, 18:19], st[:, 18:19], -0.5, 1.5, OP.mult, OP.add)
                    v.drain()
                    v.tensor_tensor(st[:, 19:20], st[:, 15:16], st[:, 18:19], OP.mult)
                    v.drain()
                    v.tensor_scalar(xn[b][:, :], xt[b][:, :], st[:, 12:13], st[:, 19:20],
                                    OP.subtract, OP.mult).then_inc(*p.inc(A, 1))
                    v.drain()
                    v.tensor_copy(hi[b][:, :], xn[b][:, :])
                    v.drain()
                    v.tensor_tensor(lo[b][:, :], xn[b][:, :], hi[b][:, :], OP.subtract)
                    if qp and i < QW // 128:
                        v.drain()
                        v.tensor_tensor(q3[:, :], xn[b][:, :], qk_b[:, :], OP.mult)
                        v.drain()
                        v.tensor_copy(qh[b][:, :], q3[:, :])
                        v.drain()
                        v.tensor_tensor(ql[b][:, :], q3[:, :], qh[b][:, :], OP.subtract)
                    v.drain().then_inc(*p.inc(V, 1))

            @blk.scalar
            def _(se):
                p.gate(se)
                for i in range(n_tiles):
                    b = i % 2
                    p.wge(se, V, 2 * i + 1)
                    se.activation(out=st[:, 14:15], in_=st[:, 13:14], func=AF.Sqrt,
                                  bias=beps).then_inc(*p.inc(Q, 1))
                    p.wge(se, V, 2 * i + 2)
                    se.dma_start(out=S[hid][i * 128:(i + 1) * 128, :], in_=hi[b][:, :]).then_inc(*p.inc(ST + str(b), 16))
                    se.dma_start(out=S[lod][i * 128:(i + 1) * 128, :], in_=lo[b][:, :]).then_inc(*p.inc(ST + str(b), 16))
                    if src_kind == "x1":
                        se.dma_start(out=S["xn1f"][i * 128:(i + 1) * 128, :],
                                     in_=xn[b][:, :]).then_inc(*p.inc(ST + str(b), 16))
                    if qp and i < QW // 128:
                        se.dma_start(out=S["qp_hi"][i * 128:(i + 1) * 128, :],
                                     in_=qh[b][:, :]).then_inc(*p.inc(ST + str(b), 16))
                        se.dma_start(out=S["qp_lo"][i * 128:(i + 1) * 128, :],
                                     in_=ql[b][:, :]).then_inc(*p.inc(ST + str(b), 16))

    p.mark(tag + "_a", tag + "st0")
    p.mark(tag + "_b", tag + "st1")
    p.endphase(tag + "_a", tag + "_b")


def _q_stageA(v, qsl, G, NBK, slot, x_ap, C):
    amax = qsl[:, 0 * G * NBK + slot * NBK: 0 * G * NBK + (slot + 1) * NBK]
    v.tensor_reduce(amax.rearrange("p (b o) -> p b o", o=1),
                    x_ap.rearrange("p (b k) -> p b k", k=16),
                    AX.X, OP.max, apply_absolute_value=True)
    v.drain()


def _q_stageB_v1(v, qsl, G, NBK, ng, gs6_col):
    W = ng * NBK
    GN = G * NBK
    amax = qsl[:, 0:W]
    eidx = qsl[:, GN:GN + W]
    tmp = qsl[:, 7 * GN:7 * GN + W]
    v.tensor_scalar(tmp, amax, gs6_col, None, OP.mult)
    v.drain()
    v.tensor_scalar(tmp, tmp, 448.0, None, OP.min)
    v.drain()
    v.tensor_scalar(eidx, tmp, 2.0 ** -9, None, OP.is_ge)
    for k in range(-8, 9):
        v.drain()
        v.scalar_tensor_tensor(eidx, tmp, float(2.0 ** k), eidx, OP.is_ge, OP.add)
    v.drain()


def _q_stageB_se(se, qsl, G, NBK, ng, bn10, bp10):
    W = ng * NBK
    GN = G * NBK
    LN2 = math.log(2.0)
    eidx = qsl[:, GN:GN + W]
    bexp = qsl[:, 2 * GN:2 * GN + W]
    rbexp = qsl[:, 3 * GN:3 * GN + W]
    se.activation(out=bexp, in_=eidx, func=AF.Exp, scale=LN2, bias=bn10)
    se.drain()
    se.activation(out=rbexp, in_=eidx, func=AF.Exp, scale=-LN2, bias=bp10)
    se.drain()


def _q_stageB_v2(v, qsl, G, NBK, ng, gs_col):
    W = ng * NBK
    GN = G * NBK
    bexp = qsl[:, 2 * GN:2 * GN + W]
    rbexp = qsl[:, 3 * GN:3 * GN + W]
    m = qsl[:, 4 * GN:4 * GN + W]
    sc8 = qsl[:, 5 * GN:5 * GN + W]
    rsg = qsl[:, 6 * GN:6 * GN + W]
    tmp = qsl[:, 7 * GN:7 * GN + W]
    v.tensor_scalar(bexp, bexp, 2.0 ** -6, None, OP.max)
    v.drain()
    v.tensor_scalar(rbexp, rbexp, 2.0 ** 6, None, OP.min)
    v.drain()
    v.tensor_tensor(m, tmp, rbexp, OP.mult)
    v.drain()
    v.tensor_scalar(sc8, m, 1.0 / 16.0, None, OP.is_ge)
    for j in range(1, 16):
        v.drain()
        v.scalar_tensor_tensor(sc8, m, float((2 * j + 1) / 16.0), sc8, OP.is_ge, OP.add)
    v.drain()
    v.tensor_tensor(sc8, sc8, bexp, OP.mult)
    v.drain()
    v.tensor_scalar(sc8, sc8, 0.125, None, OP.mult)
    v.drain()
    v.tensor_scalar(tmp, sc8, 1e-30, None, OP.max)
    v.drain()
    v.reciprocal(rsg, tmp)
    v.drain()
    v.tensor_scalar(rsg, rsg, gs_col, None, OP.mult)
    v.drain()


def _q_stageC_se(se, x_ap, ab, sg, C):
    se.activation(out=ab[:, 0:C], in_=x_ap, func=AF.Abs)
    se.drain()
    se.activation(out=sg[:, 0:C], in_=x_ap, func=AF.Sign)
    se.drain()


def _q_stageC_v(v, qsl, G, NBK, slot, ab, sg, qa, qb, u_out, C):
    GN = G * NBK
    sc8 = qsl[:, 5 * GN + slot * NBK:5 * GN + (slot + 1) * NBK]
    rsg = qsl[:, 6 * GN + slot * NBK:6 * GN + (slot + 1) * NBK]
    A = qa[:, 0:C]
    Cq = qb[:, 0:C]
    v.tensor_tensor(A.rearrange("p (b k) -> p b k", k=16),
                    ab[:, 0:C].rearrange("p (b k) -> p b k", k=16),
                    rsg.rearrange("p (b o) -> p b o", o=1).broadcast_to((128, NBK, 16)),
                    OP.mult)
    v.drain()
    v.tensor_scalar(Cq, A, 0.25, None, OP.is_ge)
    for mth in (0.75, 1.25, 1.75):
        v.drain()
        v.scalar_tensor_tensor(Cq, A, mth, Cq, OP.is_ge, OP.add)
    v.drain()
    v.tensor_scalar(Cq, Cq, 0.5, None, OP.mult)
    v.drain()
    v.scalar_tensor_tensor(Cq, A, 2.5, Cq, OP.is_ge, OP.add)
    v.drain()
    v.scalar_tensor_tensor(Cq, A, 3.5, Cq, OP.is_ge, OP.add)
    v.drain()
    v.tensor_scalar(A, A, 5.0, 2.0, OP.is_ge, OP.mult)
    v.drain()
    v.tensor_tensor(A, Cq, A, OP.add)
    v.drain()
    v.tensor_tensor(A, A, sg[:, 0:C], OP.mult)
    v.drain()
    v.tensor_tensor(u_out.rearrange("p (b k) -> p b k", k=16),
                    A.rearrange("p (b k) -> p b k", k=16),
                    sc8.rearrange("p (b o) -> p b o", o=1).broadcast_to((128, NBK, 16)),
                    OP.mult)
    v.drain()


def _emit_wquant(nc, p, stack, I, S, sb, cst, ps):
    C_LNDEC, C_NLNDEC, C_OS, C_SIN6, C_SIN, C_SIND, C_GWG, C_GWU, C_GWD, \
        C_SGA, C_SUA, C_SDA, C_LN2, C_SIN6D = range(14)
    WQ = [("g1_w", "uwg", HFF1, D, C_GWG),
          ("u1_w", "uwu", HFF1, D, C_GWU),
          ("d1_w", "uwd", D, HFF1, C_GWD)]

    wq0stack = contextlib.ExitStack()
    with nc.named_scope("wq0"), nc.Block("wq0") as blk:
        wt = wq0stack.enter_context(nc.sbuf_tensor("wq_a", [128, 2 * 4096], F32))
        mx = wq0stack.enter_context(nc.sbuf_tensor("wq_m", [128, 8], F32))
        sc1 = wq0stack.enter_context(nc.sbuf_tensor("wq_s", [1, 512], F32))
        nchunks = [R // 128 for (_, _, R, _, _) in WQ]

        @blk.sync
        def _(sync):
            p.gate(sync)
            i = 0
            for (src, dst, R, C, cg) in WQ:
                for o in range(0, R, 128):
                    if i >= 2:
                        p.wge(sync, "wqv", i - 1)
                    sync.dma_start(out=wt[:, (i % 2) * 4096:(i % 2) * 4096 + C],
                                   in_=I[src][o:o + 128, :]).then_inc(*p.inc("wqld%d" % (i % 2), 16))
                    i += 1

        @blk.vector
        def _(v):
            p.gate(v)
            i = 0
            for wi, (src, dst, R, C, cg) in enumerate(WQ):
                for j in range(R // 128):
                    p.wge(v, "wqld%d" % (i % 2), 16 * (i // 2 + 1))
                    b = (i % 2) * 4096
                    if j == 0:
                        v.tensor_reduce(mx[:, wi:wi + 1], wt[:, b:b + C], AX.X, OP.max,
                                        apply_absolute_value=True)
                    else:
                        v.tensor_reduce(mx[:, 4:5], wt[:, b:b + C], AX.X, OP.max,
                                        apply_absolute_value=True)
                        v.drain()
                        v.tensor_tensor(mx[:, wi:wi + 1], mx[:, wi:wi + 1], mx[:, 4:5], OP.max)
                    v.drain().then_inc(*p.inc("wqv", 1))
                    i += 1

        @blk.scalar
        def _(se):
            p.gate(se)
            p.wge(se, "wqv", sum(nchunks))
            se.dma_start(out=S["row"][2:3, 0:384].rearrange("o (c pp) -> pp (o c)", pp=128),
                         in_=mx[:, 0:3], allow_slow_non_contiguous=True).then_inc(*p.inc("wqs", 16))

        @blk.sync
        def _(sync):
            p.gate(sync)
            p.wge(sync, "wqs", 16)
            sync.dma_start(out=sc1[0:1, 0:384], in_=S["row"][2:3, 0:384]).then_inc(*p.inc("wqs2", 16))
            sync.dma_start(out=sc1[0:1, 400:401], in_=I["s_in"]).then_inc(*p.inc("wqs2", 16))
            sync.dma_start(out=sc1[0:1, 401:402], in_=I["s_in_down"]).then_inc(*p.inc("wqs2", 16))

        @blk.vector
        def _(v):
            p.gate(v)
            v.memset(sc1[0:1, 416:448], 0.0)
            v.drain()
            p.wge(v, "wqs2", 48)
            for wi in range(3):
                v.tensor_reduce(sc1[0:1, 416 + wi:417 + wi], sc1[0:1, wi * 128:(wi + 1) * 128],
                                AX.X, OP.max)
                v.drain()
            v.reciprocal(sc1[0:1, 420:423], sc1[0:1, 416:419])
            v.drain()
            v.tensor_scalar(sc1[0:1, 424:427], sc1[0:1, 420:423], 2688.0, None, OP.mult)
            v.drain()
            v.tensor_scalar(sc1[0:1, 428:431], sc1[0:1, 424:427], 1.0 / 6.0, None, OP.mult)
            v.drain()
            v.reciprocal(sc1[0:1, 432:434], sc1[0:1, 400:402])
            v.drain()
            v.tensor_scalar(sc1[0:1, 436:439], sc1[0:1, 416:419], 1.0 / 2688.0, None, OP.mult)
            v.drain()
            v.tensor_scalar(sc1[0:1, 436:438], sc1[0:1, 436:438], sc1[0:1, 432:433], None, OP.mult)
            v.drain()
            v.tensor_scalar(sc1[0:1, 438:439], sc1[0:1, 438:439], sc1[0:1, 433:434], None, OP.mult)
            v.drain()
            v.tensor_scalar(sc1[0:1, 440:441], sc1[0:1, 401:402], 1.0 / 6.0, None, OP.mult)
            v.drain().then_inc(*p.inc("wqv2", 1))

        @blk.scalar
        def _(se):
            p.gate(se)
            p.wge(se, "wqv2", 1)
            se.dma_start(out=S["row"][3:4, 416:448], in_=sc1[0:1, 416:448]).then_inc(*p.inc("wqs4", 16))

        @blk.sync
        def _(sync):
            p.gate(sync)
            p.wge(sync, "wqs4", 16)
            for (col, off) in [(C_GWG, 424), (C_GWU, 425), (C_GWD, 426),
                               (C_GWG + 16, 428), (C_GWU + 16, 429), (C_GWD + 16, 430),
                               (C_SGA, 436), (C_SUA, 437), (C_SDA, 438), (C_SIN6D, 440)]:
                sync.dma_start(out=cst[:, col:col + 1],
                               in_=S["row"][3:4, off:off + 1].partition_broadcast(128)).then_inc(*p.inc("wqb", 16))

    wq0stack.close()
    p.mark("wqscales", "wqb")
    p.endphase("wqscales")

    G = 4
    calls = []
    for (srcn, dstn, R, C, cg) in WQ:
        ch = [(srcn, dstn, o, C, cg) for o in range(0, R, 128)]
        for gs in range(0, len(ch), G):
            calls.append(ch[gs:gs + G])
    elad, cdone, exps, absg = {}, {}, {}, {}
    vc = sec = 0
    qglob = 0
    for g, grp in enumerate(calls):
        vc += 1
        elad[g] = vc
        sec += 1
        exps[g] = sec
        for k in range(len(grp)):
            sec += 1
            absg[qglob + k] = sec
            vc += 1
            cdone[qglob + k] = vc
        qglob += len(grp)
    ncalls = qglob

    with nc.named_scope("wq1"), nc.Block("wq1") as blk:
        with contextlib.ExitStack() as ph:
            xb = ph.enter_context(nc.sbuf_tensor("wq1_xb", [128, G * 4096], F32))
            ab = ph.enter_context(nc.sbuf_tensor("wq1_ab", [128, 4096], F32))
            sg = ph.enter_context(nc.sbuf_tensor("wq1_sg", [128, 4096], F32))
            qa = ph.enter_context(nc.sbuf_tensor("wq1_qa", [128, 4096], F32))
            qb = ph.enter_context(nc.sbuf_tensor("wq1_qb", [128, 4096], F32))
            qsl = ph.enter_context(nc.sbuf_tensor("wq1_qs", [128, 8 * G * 256], F32))
            uo = ph.enter_context(nc.sbuf_tensor("wq1_u", [128, 2 * 4096], BF16))

            @blk.sync
            def _(sync):
                p.gate(sync)
                p.wait(sync, "wqscales")
                q = 0
                for grp in calls:
                    for k, (srcn, dstn, o, C, cg) in enumerate(grp):
                        if q >= G:
                            p.wge(sync, "wq1se", absg[q - G])
                        sync.dma_start(out=xb[:, (q % G) * 4096:(q % G) * 4096 + C],
                                       in_=I[srcn][o:o + 128, :]).then_inc(*p.inc("wq1ld%d" % (q % 4), 16))
                        q += 1

            @blk.vector
            def _(v):
                p.gate(v)
                q = 0
                for g, grp in enumerate(calls):
                    C = grp[0][3]
                    NBK = C // 16
                    ng = len(grp)
                    cg = grp[0][4]
                    for k in range(ng):
                        p.wge(v, "wq1ld%d" % ((q + k) % 4), 16 * ((q + k) // 4 + 1))
                        _q_stageA(v, qsl, G, NBK, k, xb[:, ((q + k) % G) * 4096:((q + k) % G) * 4096 + C], C)
                    _q_stageB_v1(v, qsl, G, NBK, ng, cst[:, cg + 16:cg + 17])
                    p.inc("wq1v", 1)
                    v.drain().then_inc(p.s("wq1v"), 1)
                    p.wge(v, "wq1se", exps[g])
                    _q_stageB_v2(v, qsl, G, NBK, ng, cst[:, cg:cg + 1])
                    for k in range(ng):
                        p.wge(v, "wq1se", absg[q + k])
                        if q + k >= 2:
                            p.wge(v, "wq1st%d" % ((q + k) % 2), 16 * ((q + k) // 2))
                        _q_stageC_v(v, qsl, G, NBK, k, ab, sg, qa, qb,
                                    uo[:, ((q + k) % 2) * 4096:((q + k) % 2) * 4096 + C], C)
                        p.inc("wq1v", 1)
                        v.drain().then_inc(p.s("wq1v"), 1)
                    q += ng

            @blk.scalar
            def _(se):
                p.gate(se)
                q = 0
                for g, grp in enumerate(calls):
                    C = grp[0][3]
                    NBK = C // 16
                    ng = len(grp)
                    p.wge(se, "wq1v", elad[g])
                    _q_stageB_se(se, qsl, G, NBK, ng, cst[:, 14:15], cst[:, 15:16])
                    p.inc("wq1se", 1)
                    se.drain().then_inc(p.s("wq1se"), 1)
                    for k in range(ng):
                        if q + k >= 1:
                            p.wge(se, "wq1v", cdone[q + k - 1])
                        _q_stageC_se(se, xb[:, ((q + k) % G) * 4096:((q + k) % G) * 4096 + C], ab, sg, C)
                        p.inc("wq1se", 1)
                        se.drain().then_inc(p.s("wq1se"), 1)
                    q += ng

            @blk.gpsimd
            def _(g_):
                p.gate(g_)
                q = 0
                for grp in calls:
                    for (srcn, dstn, o, C, cg) in grp:
                        p.wge(g_, "wq1v", cdone[q])
                        g_.dma_start(out=S[dstn][o:o + 128, :],
                                     in_=uo[:, (q % 2) * 4096:(q % 2) * 4096 + C]).then_inc(*p.inc("wq1st%d" % (q % 2), 16))
                        q += 1

    p.mark("wq_a", "wq1st0")
    p.mark("wq_b", "wq1st1")
    p.endphase("wq_a", "wq_b")


def _emit_attn(nc, p, stack, I, S, sb, ps, cst, b_tile, a_mat, wts, W, NB, C_OS):
    qtiles = []
    t0 = 0
    while t0 < QW:
        qt = min(512, QW - t0)
        qtiles.append((t0, qt))
        t0 += qt

    for qi, (t0, qt) in enumerate(qtiles):
        NBq = (W + qt) // 128
        NHG = (NBq + 3) // 4
        ntc = qt // 128
        tag = "at%d" % qi
        with nc.named_scope(tag), nc.Block(tag) as blk:
            with contextlib.ExitStack() as ph:
                sw_hi = ph.enter_context(nc.sbuf_tensor(tag + "_swh", [128, NBq * 512], BF16))
                sw_lo = ph.enter_context(nc.sbuf_tensor(tag + "_swl", [128, NBq * 512], BF16))
                kf = [[ph.enter_context(nc.sbuf_tensor(tag + "_kf%d%s" % (i, h), [128, 8 * 512], BF16))
                       for h in "hl"] for i in range(2)]
                qpf = [ph.enter_context(nc.sbuf_tensor(tag + "_qp%s" % h, [128, 8 * 512], BF16))
                       for h in "hl"]
                vb = [ph.enter_context(nc.sbuf_tensor(tag + "_vb%s" % h, [128, NBq * 512], BF16))
                      for h in "hl"]
                sw32 = ph.enter_context(nc.sbuf_tensor(tag + "_s32", [128, 512], F32))
                xre = ph.enter_context(nc.sbuf_tensor(tag + "_xre", [128, 2 * 512], F32))
                x1t = ph.enter_context(nc.sbuf_tensor(tag + "_x1t", [128, 2 * 512], F32))

                KLD, QLD, VLD, SC, SWP, SW, RT, EV, XLD, ST = [tag + s for s in
                    ("kld", "qld", "vld", "sc", "swp", "sw", "rt", "ev", "xld", "st")]

                @blk.sync
                def _(sync):
                    p.gate(sync)
                    if qi == 0:
                        p.wait(sync, "ln0_a")
                        p.wait(sync, "ln0_b")
                        p.wait(sync, "c0")
                    else:
                        p.wait(sync, "at%d_a" % (qi - 1))
                        p.wait(sync, "at%d_b" % (qi - 1))
                    for hn, dst in (("qp_hi", qpf[0]), ("qp_lo", qpf[1])):
                        for vc in range(8):
                            sync.dma_start(out=dst[:, vc * 512:vc * 512 + qt],
                                           in_=S[hn][t0:t0 + qt, vc * 128:(vc + 1) * 128],
                                           transpose=True).then_inc(*p.inc(QLD, 16))
                    for hg in range(NHG):
                        if hg >= 2:
                            p.wge(sync, SC, 4 * (hg - 1))
                        s0 = t0 + 512 * hg
                        nsb = min(512, (t0 + NBq * 128) - s0)
                        for hn, dst in (("xn_hi", kf[hg % 2][0]), ("xn_lo", kf[hg % 2][1])):
                            for vc in range(8):
                                sync.dma_start(out=dst[:, vc * 512:vc * 512 + nsb],
                                               in_=S[hn][s0:s0 + nsb, vc * 128:(vc + 1) * 128],
                                               transpose=True).then_inc(*p.inc(KLD + str(hg % 2), 16))
                    for vh in range(2):
                        if vh == 1:
                            p.wge(sync, RT, ntc)
                        for hn, dst in (("xn_hi", vb[0]), ("xn_lo", vb[1])):
                            sync.dma_start(
                                out=dst[:, :].rearrange("p (j v) -> p j v", v=512),
                                in_=S[hn][t0:t0 + NBq * 128, vh * 512:(vh + 1) * 512]
                                    .rearrange("(j p) v -> p j v", p=128),
                            ).then_inc(*p.inc(VLD, 16))
                        for tc in range(ntc):
                            idx = vh * ntc + tc
                            if idx >= 2:
                                p.wge(sync, EV, idx - 1)
                            sync.dma_start(out=xre[:, (idx % 2) * 512:(idx % 2) * 512 + 512],
                                           in_=I["x"][t0 + tc * 128:t0 + tc * 128 + 128,
                                                      vh * 512:(vh + 1) * 512]).then_inc(*p.inc(XLD + str(idx % 2), 16))

                @blk.tensor
                def _(tensor):
                    p.gate(tensor)
                    for jp in range(0, NBq, 2):
                        js = [jp] + ([jp + 1] if jp + 1 < NBq else [])
                        hg = jp // 4
                        p.wge(tensor, KLD + str(hg % 2), 16 * 16 * (hg // 2 + 1))
                        if jp == 0:
                            p.wge(tensor, QLD, 16 * 16)
                        if jp >= 2:
                            p.wge(tensor, SWP, jp)
                        mms = {j: 0 for j in js}
                        for (kb, qb) in ((0, 0), (0, 1), (1, 0)):
                            for vc in range(8):
                                for j in js:
                                    jj = j % 4
                                    mms[j] += 1
                                    mm = tensor.matmul(ps[j % 2][:, 0:qt],
                                                       kf[hg % 2][kb][:, vc * 512 + jj * 128:vc * 512 + jj * 128 + 128],
                                                       qpf[qb][:, vc * 512:vc * 512 + qt],
                                                       start=(mms[j] == 1), stop=(mms[j] == 24))
                                    if mms[j] == 24:
                                        mm.then_inc(*p.inc(SC, 1))
                    for vh in range(2):
                        p.wge(tensor, VLD, 32 * (vh + 1))
                        for tcp in range(0, ntc, 2):
                            tcs = [tcp] + ([tcp + 1] if tcp + 1 < ntc else [])
                            if vh == 1:
                                p.wge(tensor, EV, ntc)
                            if vh == 0:
                                p.wge(tensor, SW, NBq)
                            mms = {tc: 0 for tc in tcs}
                            tot = {tc: 3 * (NBq - tc) for tc in tcs}
                            for j in range(tcp, NBq):
                                for (sb_, vbb) in ((0, 0), (0, 1), (1, 0)):
                                    src_sw = sw_hi if sb_ == 0 else sw_lo
                                    for tc in tcs:
                                        if not (tc <= j):
                                            continue
                                        mms[tc] += 1
                                        mm = tensor.matmul(
                                            ps[2 + tc][:, 0:512],
                                            src_sw[:, j * 512 + tc * 128:j * 512 + tc * 128 + 128],
                                            vb[vbb][:, j * 512:(j + 1) * 512],
                                            start=(mms[tc] == 1), stop=(mms[tc] == tot[tc]))
                                        if mms[tc] == tot[tc]:
                                            mm.then_inc(*p.inc(RT, 1))

                @blk.vector
                def _(v):
                    p.gate(v)
                    if qi == 0:
                        p.wait(v, "c0")
                    for j in range(NBq):
                        p.wge(v, SC, j + 1)
                        pb = ps[j % 2]
                        if j <= 3:
                            v.tensor_tensor(sw32[:, 0:qt], pb[:, 0:qt], wts[j][:, 0:qt], OP.mult)
                        else:
                            v.scalar_tensor_tensor(sw32[:, 0:qt], pb[:, 0:qt], a_mat[:, j:j + 1],
                                                   b_tile[:, 0:qt], OP.mult, OP.mult)
                        p.inc(SWP, 1)
                        v.drain().then_inc(p.s(SWP), 1)
                        v.tensor_copy(sw_hi[:, j * 512:j * 512 + qt], sw32[:, 0:qt])
                        v.drain()
                        v.tensor_tensor(sw_lo[:, j * 512:j * 512 + qt], sw32[:, 0:qt],
                                        sw_hi[:, j * 512:j * 512 + qt], OP.subtract)
                        p.inc(SW, 1)
                        v.drain().then_inc(p.s(SW), 1)
                    for vh in range(2):
                        for tc in range(ntc):
                            idx = vh * ntc + tc
                            p.wge(v, RT, idx + 1)
                            p.wge(v, XLD + str(idx % 2), 16 * (idx // 2 + 1))
                            if idx >= 2:
                                p.wge(v, ST + str(idx % 2), 16 * (idx // 2))
                            v.scalar_tensor_tensor(x1t[:, (idx % 2) * 512:(idx % 2) * 512 + 512],
                                                   ps[2 + tc][:, 0:512], cst[:, C_OS:C_OS + 1],
                                                   xre[:, (idx % 2) * 512:(idx % 2) * 512 + 512],
                                                   OP.mult, OP.add)
                            p.inc(EV, 1)
                            v.drain().then_inc(p.s(EV), 1)

                @blk.scalar
                def _(se):
                    p.gate(se)
                    for i in range(2 * ntc):
                        p.wge(se, EV, i + 1)
                        vh, tc = i // ntc, i % ntc
                        se.dma_start(out=S["x1"][t0 + tc * 128:t0 + tc * 128 + 128,
                                                 vh * 512:(vh + 1) * 512],
                                     in_=x1t[:, (i % 2) * 512:(i % 2) * 512 + 512]
                                     ).then_inc(*p.inc(ST + str(i % 2), 16))

        p.mark("at%d_a" % qi, ST + "0")
        p.mark("at%d_b" % qi, ST + "1")
        p.endphase("at%d_a" % qi, "at%d_b" % qi)
    p.mark("p2a", "at%dst0" % (len(qtiles) - 1))
    p.mark("p2b", "at%dst1" % (len(qtiles) - 1))


def _emit_ffn0(nc, p, stack, I, S, sb, ps, cst):
    with nc.named_scope("f0a"), nc.Block("f0a") as blk:
        with contextlib.ExitStack() as ph:
            xnf = [ph.enter_context(nc.sbuf_tensor("f0_xn%s" % h, [128, 8 * QW], BF16)) for h in "hl"]
            wgf = [ph.enter_context(nc.sbuf_tensor("f0_wg%d" % i, [128, 8 * 512], BF16)) for i in range(2)]
            wuf = [ph.enter_context(nc.sbuf_tensor("f0_wu%d" % i, [128, 8 * 512], BF16)) for i in range(2)]
            sg = ph.enter_context(nc.sbuf_tensor("f0_sg", [128, 2 * 512], F32))
            hh = ph.enter_context(nc.sbuf_tensor("f0_hh", [128, 2 * 512], F32))
            hhh = ph.enter_context(nc.sbuf_tensor("f0_hhh", [128, 2 * 512], BF16))
            hhl = ph.enter_context(nc.sbuf_tensor("f0_hhl", [128, 2 * 512], BF16))
            NHS = HFF0 // 512

            @blk.sync
            def _(sync):
                p.gate(sync)
                p.wait(sync, "p2a")
                p.wait(sync, "p2b")
                p.wait(sync, "w0c")
                for hn, dst in (("xn_hi", xnf[0]), ("xn_lo", xnf[1])):
                    for vc in range(8):
                        sync.dma_start(out=dst[:, vc * QW:(vc + 1) * QW],
                                       in_=S[hn][0:QW, vc * 128:(vc + 1) * 128],
                                       transpose=True).then_inc(*p.inc("f0xld", 16))
                for hs_ in range(NHS):
                    if hs_ >= 2:
                        p.wge(sync, "f0mm", 2 * NTC * (hs_ - 1))
                    for wn, dst in (("wg_b", wgf[hs_ % 2]), ("wu_b", wuf[hs_ % 2])):
                        for vc in range(8):
                            sync.dma_start(out=dst[:, vc * 512:(vc + 1) * 512],
                                           in_=S[wn][hs_ * 512:(hs_ + 1) * 512, vc * 128:(vc + 1) * 128],
                                           transpose=True).then_inc(*p.inc("f0wld%d" % (hs_ % 2), 16))

            @blk.tensor
            def _(tensor):
                p.gate(tensor)
                p.wge(tensor, "f0xld", 16 * 16)
                for hs_ in range(NHS):
                    p.wge(tensor, "f0wld%d" % (hs_ % 2), 16 * 16 * (hs_ // 2 + 1))
                    for tc in range(NTC):
                        idx = hs_ * NTC + tc
                        if idx >= 2:
                            p.wge(tensor, "f0hv", idx - 1)
                            p.wge(tensor, "f0se", idx - 1)
                        pg, pu = ps[2 * (idx % 2)], ps[2 * (idx % 2) + 1]
                        for (dst, wf) in ((pg, wgf[hs_ % 2]), (pu, wuf[hs_ % 2])):
                            nmm = 0
                            for hb in range(2):
                                xf = xnf[hb]
                                for vc in range(8):
                                    nmm += 1
                                    mm = tensor.matmul(dst[:, 0:512],
                                                       xf[:, vc * QW + tc * 128:vc * QW + tc * 128 + 128],
                                                       wf[:, vc * 512:(vc + 1) * 512],
                                                       start=(nmm == 1), stop=(nmm == 16))
                            mm.then_inc(*p.inc("f0mm", 1))

            @blk.scalar
            def _(se):
                p.gate(se)
                for hs_ in range(NHS):
                    for tc in range(NTC):
                        idx = hs_ * NTC + tc
                        p.wge(se, "f0mm", 2 * idx + 1)
                        se.activation(out=sg[:, (idx % 2) * 512:(idx % 2) * 512 + 512],
                                      in_=ps[2 * (idx % 2)][:, 0:512], func=AF.Silu,
                                      scale=FF0S)
                        se.drain().then_inc(p.s("f0se"), 1)
                        p.inc("f0se", 1)
                        p.wge(se, "f0hv", idx + 1)
                        se.dma_start(out=S["hh_hi"][tc * 128:tc * 128 + 128, hs_ * 512:(hs_ + 1) * 512],
                                     in_=hhh[:, (idx % 2) * 512:(idx % 2) * 512 + 512]).then_inc(*p.inc("f0st%d" % (idx % 2), 16))
                        se.dma_start(out=S["hh_lo"][tc * 128:tc * 128 + 128, hs_ * 512:(hs_ + 1) * 512],
                                     in_=hhl[:, (idx % 2) * 512:(idx % 2) * 512 + 512]).then_inc(*p.inc("f0st%d" % (idx % 2), 16))

            @blk.vector
            def _(v):
                p.gate(v)
                for hs_ in range(NHS):
                    for tc in range(NTC):
                        idx = hs_ * NTC + tc
                        p.wge(v, "f0mm", 2 * idx + 2)
                        p.wge(v, "f0se", idx + 1)
                        if idx >= 2:
                            p.wge(v, "f0st%d" % (idx % 2), 32 * (idx // 2))
                        b = (idx % 2) * 512
                        v.scalar_tensor_tensor(hh[:, b:b + 512], ps[2 * (idx % 2) + 1][:, 0:512],
                                               FF0S, sg[:, b:b + 512], OP.mult, OP.mult)
                        v.drain()
                        v.tensor_copy(hhh[:, b:b + 512], hh[:, b:b + 512])
                        v.drain()
                        v.tensor_tensor(hhl[:, b:b + 512], hh[:, b:b + 512], hhh[:, b:b + 512],
                                        OP.subtract)
                        p.inc("f0hv", 1)
                        v.drain().then_inc(p.s("f0hv"), 1)

    p.mark("f0hh_a", "f0st0")
    p.mark("f0hh_b", "f0st1")
    p.endphase("f0hh_a", "f0hh_b")

    with nc.named_scope("f0b"), nc.Block("f0b") as blk:
        with contextlib.ExitStack() as ph:
            hhf = [[ph.enter_context(nc.sbuf_tensor("f0b_h%d%s" % (i, h), [128, QW], BF16))
                    for h in "hl"] for i in range(12)]
            wdf = [ph.enter_context(nc.sbuf_tensor("f0b_w%d" % i, [128, 1024], BF16))
                   for i in range(12)]
            x1t = ph.enter_context(nc.sbuf_tensor("f0b_x1", [128, 2 * 512], F32))
            x1o = ph.enter_context(nc.sbuf_tensor("f0b_x1o", [128, 2 * 512], F32))

            for half in range(2):
                HB = half * 12
                tagl, tagm, tage, tags = ("f0b%dld" % half, "f0b%dmm" % half,
                                          "f0b%dev" % half, "f0b%dst" % half)

                @blk.sync
                def _(sync, half=half, HB=HB, tagl=tagl, tags=tags):
                    p.gate(sync)
                    if half == 0:
                        p.wait(sync, "f0hh_a")
                        p.wait(sync, "f0hh_b")
                    else:
                        p.wge(sync, "f0b0st0", 16 * NTC)
                        p.wge(sync, "f0b0st1", 16 * NTC)
                    for i in range(12):
                        hc = HB + i
                        for hn, dst in (("hh_hi", hhf[i][0]), ("hh_lo", hhf[i][1])):
                            sync.dma_start(out=dst[:, 0:QW],
                                           in_=S[hn][0:QW, hc * 128:(hc + 1) * 128],
                                           transpose=True).then_inc(*p.inc(tagl, 16))
                        sync.dma_start(out=wdf[i][:, :],
                                       in_=S["wd_b"][0:D, hc * 128:(hc + 1) * 128],
                                       transpose=True).then_inc(*p.inc(tagl, 16))
                    for tc in range(NTC):
                        for dh in range(2):
                            idx = tc * 2 + dh
                            if idx >= 2:
                                p.wge(sync, tage, idx - 1)
                            sync.dma_start(out=x1t[:, (idx % 2) * 512:(idx % 2) * 512 + 512],
                                           in_=S["x1"][tc * 128:tc * 128 + 128,
                                                       dh * 512:(dh + 1) * 512]).then_inc(*p.inc(tagl + "x%d" % (idx % 2), 16))

                @blk.tensor
                def _(tensor, half=half, tagl=tagl, tagm=tagm, tage=tage):
                    p.gate(tensor)
                    p.wge(tensor, tagl, 16 * 36)
                    for tc in range(NTC):
                        for dh in range(2):
                            idx = tc * 2 + dh
                            if idx >= 4:
                                p.wge(tensor, tage, idx - 3)
                            pr = ps[4 + (idx % 4)]
                            nmm = 0
                            for i in range(12):
                                for hb in range(2):
                                    nmm += 1
                                    mm = tensor.matmul(pr[:, 0:512],
                                                       hhf[i][hb][:, tc * 128:tc * 128 + 128],
                                                       wdf[i][:, dh * 512:(dh + 1) * 512],
                                                       start=(nmm == 1), stop=(nmm == 24))
                            mm.then_inc(*p.inc(tagm, 1))

                @blk.vector
                def _(v, tagm=tagm, tage=tage, tagl=tagl, tags=tags):
                    p.gate(v)
                    for tc in range(NTC):
                        for dh in range(2):
                            idx = tc * 2 + dh
                            p.wge(v, tagm, idx + 1)
                            p.wge(v, tagl + "x%d" % (idx % 2), 16 * (idx // 2 + 1))
                            if idx >= 2:
                                p.wge(v, tags + str(idx % 2), 16 * (idx // 2))
                            b = (idx % 2) * 512
                            v.scalar_tensor_tensor(x1o[:, b:b + 512], ps[4 + (idx % 4)][:, 0:512],
                                                   FF0S, x1t[:, b:b + 512], OP.mult, OP.add)
                            p.inc(tage, 1)
                            v.drain().then_inc(p.s(tage), 1)

                @blk.scalar
                def _(se, tage=tage, tags=tags, half=half):
                    p.gate(se)
                    for tc in range(NTC):
                        for dh in range(2):
                            idx = tc * 2 + dh
                            p.wge(se, tage, idx + 1)
                            se.dma_start(out=S["x1"][tc * 128:tc * 128 + 128, dh * 512:(dh + 1) * 512],
                                         in_=x1o[:, (idx % 2) * 512:(idx % 2) * 512 + 512]
                                         ).then_inc(*p.inc("f0b%dst%d" % (half, idx % 2), 16))

            p.mark("f0bh%d_a" % half, "f0b%dst0" % half)
            p.mark("f0bh%d_b" % half, "f0b%dst1" % half)
            p.endphase("f0bh%d_a" % half, "f0bh%d_b" % half)

    p.mark("p3_a", "f0b1st0")
    p.mark("p3_b", "f0b1st1")
    p.endphase("p3_a", "p3_b")


MF_TT = [(0, 512), (512, 512), (1024, 512), (1536, 512), (2048, 384)]


def _emit_mamba_front(nc, p, stack, I, S, sb, ps, cst, convw, convb, bdtn):
    with nc.named_scope("mfa"), nc.Block("mfa") as blk:
        with contextlib.ExitStack() as ph:
            xnf = [ph.enter_context(nc.sbuf_tensor("mf_xn%s" % h, [128, 8 * QW], BF16)) for h in "hl"]
            wif = [[ph.enter_context(nc.sbuf_tensor("mf_wi%d%s" % (i, h), [128, 8 * 512], BF16))
                    for h in "hl"] for i in range(2)]
            xcp = [ph.enter_context(nc.sbuf_tensor("mf_xcp%d" % i, [128, 8 + QW], F32)) for i in range(4)]
            xct = ph.enter_context(nc.sbuf_tensor("mf_xct", [128, QW], F32))
            xch = ph.enter_context(nc.sbuf_tensor("mf_xch", [128, QW], BF16))
            xcl = ph.enter_context(nc.sbuf_tensor("mf_xcl", [128, QW], BF16))
            szt = ph.enter_context(nc.sbuf_tensor("mf_sz", [128, 2 * 512], F32))
            umb = ph.enter_context(nc.sbuf_tensor("mf_um", [128, QW], F32))

            @blk.sync
            def _(sync):
                p.gate(sync)
                p.wait(sync, "ln1_a")
                p.wait(sync, "ln1_b")
                p.wait(sync, "w0_a")
                p.wait(sync, "w0_b")
                sync.dma_start(out=umb[:, :],
                               in_=I["umask"][0:1, :].partition_broadcast(128)).then_inc(*p.inc("mfum", 16))
                for hn, dst in (("xn1_hi", xnf[0]), ("xn1_lo", xnf[1])):
                    for vc in range(8):
                        sync.dma_start(out=dst[:, vc * QW:(vc + 1) * QW],
                                       in_=S[hn][0:QW, vc * 128:(vc + 1) * 128],
                                       transpose=True).then_inc(*p.inc("mfxld", 16))
                for grp in range(8):
                    if grp >= 2:
                        p.wge(sync, "mfmm", 20 * (grp - 1))
                    for hn, dst in (("win_hi", wif[grp % 2][0]), ("win_lo", wif[grp % 2][1])):
                        for vc in range(8):
                            sync.dma_start(out=dst[:, vc * 512:(vc + 1) * 512],
                                           in_=S[hn][grp * 512:(grp + 1) * 512, vc * 128:(vc + 1) * 128],
                                           transpose=True).then_inc(*p.inc("mfwld%d" % (grp % 2), 16))

            @blk.tensor
            def _(tensor):
                p.gate(tensor)
                p.wge(tensor, "mfxld", 16 * 16)
                for grp in range(8):
                    p.wge(tensor, "mfwld%d" % (grp % 2), 16 * 16 * (grp // 2 + 1))
                    for gi in range(4):
                        for ti, (tt0, tl) in enumerate(MF_TT):
                            idx = (grp * 4 + gi) * 5 + ti
                            if idx >= 4:
                                p.wge(tensor, "mfev", idx - 3)
                            pb = ps[idx % 4]
                            nmm = 0
                            for (kb, qb) in ((0, 0), (0, 1), (1, 0)):
                                for vc in range(8):
                                    nmm += 1
                                    mm = tensor.matmul(
                                        pb[:, 0:tl],
                                        wif[grp % 2][kb][:, vc * 512 + gi * 128:vc * 512 + gi * 128 + 128],
                                        xnf[qb][:, vc * QW + tt0:vc * QW + tt0 + tl],
                                        start=(nmm == 1), stop=(nmm == 24))
                            mm.then_inc(*p.inc("mfmm", 1))

            @blk.vector
            def _(v):
                p.gate(v)
                p.wge(v, "mfum", 16)
                for grp in range(8):
                    for gi in range(4):
                        dc = grp * 4 + gi
                        is_xc = dc < 16
                        if is_xc:
                            v.memset(xcp[gi][:, 0:8], 0.0)
                        for ti, (tt0, tl) in enumerate(MF_TT):
                            idx = dc * 5 + ti
                            p.wge(v, "mfmm", idx + 1)
                            if is_xc:
                                v.drain()
                                v.tensor_copy(xcp[gi][:, 3 + tt0:3 + tt0 + tl], ps[idx % 4][:, 0:tl])
                                p.inc("mfev", 1)
                                v.drain().then_inc(p.s("mfev"), 1)
                            else:
                                p.wge(v, "mfsz", (dc - 16) * 5 + ti + 1)
                                p.inc("mfev", 1)
                                v.nop().then_inc(p.s("mfev"), 1)
                        if is_xc:
                            p.wge(v, "mfst", 48 * dc)
                            v.tensor_scalar(xct[:, :], xcp[gi][:, 0:QW], convw[:, dc * 4:dc * 4 + 1],
                                            None, OP.mult)
                            for k in range(1, 4):
                                v.drain()
                                v.scalar_tensor_tensor(xct[:, :], xcp[gi][:, k:k + QW],
                                                       convw[:, dc * 4 + k:dc * 4 + k + 1],
                                                       xct[:, :], OP.mult, OP.add)
                            p.inc("mfcv", 1)
                            v.drain().then_inc(p.s("mfcv"), 1)
                            p.wge(v, "mfsil", dc + 1)
                            v.tensor_tensor(xct[:, :], xct[:, :], umb[:, :], OP.mult)
                            v.drain()
                            v.tensor_copy(xch[:, :], xct[:, :])
                            v.drain()
                            v.tensor_tensor(xcl[:, :], xct[:, :], xch[:, :], OP.subtract)
                            p.inc("mfxc", 1)
                            v.drain().then_inc(p.s("mfxc"), 1)

            @blk.scalar
            def _(se):
                p.gate(se)
                nsz = 0
                for grp in range(8):
                    for gi in range(4):
                        dc = grp * 4 + gi
                        is_xc = dc < 16
                        if is_xc:
                            p.wge(se, "mfcv", dc + 1)
                            se.activation(out=xct[:, :], in_=xct[:, :], func=AF.Silu,
                                          bias=convb[:, dc:dc + 1])
                            se.drain().then_inc(p.s("mfsil"), 1)
                            p.inc("mfsil", 1)
                            p.wge(se, "mfxc", dc + 1)
                            se.dma_start(out=S["xc"][dc * 128:(dc + 1) * 128, :],
                                         in_=xct[:, :]).then_inc(*p.inc("mfst", 16))
                            se.dma_start(out=S["xc_hi"][dc * 128:(dc + 1) * 128, :],
                                         in_=xch[:, :]).then_inc(*p.inc("mfst", 16))
                            se.dma_start(out=S["xc_lo"][dc * 128:(dc + 1) * 128, :],
                                         in_=xcl[:, :]).then_inc(*p.inc("mfst", 16))
                        else:
                            zc = dc - 16
                            for ti, (tt0, tl) in enumerate(MF_TT):
                                idx = dc * 5 + ti
                                o0, o1 = max(tt0, WARM), tt0 + tl
                                p.wge(se, "mfmm", idx + 1)
                                if o1 > o0:
                                    b = (nsz % 2) * 512
                                    if nsz >= 2:
                                        p.wge(se, "mfzst%d" % (nsz % 2), 16 * (nsz // 2))
                                    se.activation(out=szt[:, b:b + (o1 - o0)],
                                                  in_=ps[idx % 4][:, o0 - tt0:o1 - tt0], func=AF.Silu)
                                    se.drain()
                                    se.dma_start(out=S["siluz"][zc * 128:(zc + 1) * 128, o0 - WARM:o1 - WARM],
                                                 in_=szt[:, b:b + (o1 - o0)]).then_inc(*p.inc("mfzst%d" % (nsz % 2), 16))
                                    nsz += 1
                                se.drain().then_inc(p.s("mfsz"), 1)
                                p.inc("mfsz", 1)

    p.mark("mfa", "mfst")
    p.mark("mfaz", "mfzst0")
    p.mark("mfaz2", "mfzst1")
    p.endphase("mfa", "mfaz", "mfaz2")

    ph2 = contextlib.ExitStack()
    drh = ph2.enter_context(nc.sbuf_tensor("mf_drh", [64, QW], BF16))
    drl = ph2.enter_context(nc.sbuf_tensor("mf_drl", [64, QW], BF16))
    with nc.named_scope("mfb"), nc.Block("mfb") as blk:
        with contextlib.ExitStack() as ph:
            wxf = [ph.enter_context(nc.sbuf_tensor("mf_wx%s" % h, [128, 16 * 96], BF16)) for h in "hl"]
            xcf = [[ph.enter_context(nc.sbuf_tensor("mf_xcf%d%s" % (i, h), [128, QW], BF16))
                    for h in "hl"] for i in range(2)]
            dr3 = ph.enter_context(nc.sbuf_tensor("mf_dr3", [96, QW], F32))

            @blk.sync
            def _(sync):
                p.gate(sync)
                p.wait(sync, "mfa")
                for hn, dst in (("wx_hi", wxf[0]), ("wx_lo", wxf[1])):
                    for dc in range(16):
                        sync.dma_start(out=dst[:, dc * 96:(dc + 1) * 96],
                                       in_=S[hn][dc * 128:(dc + 1) * 128, 0:96]
                                       ).then_inc(*p.inc("mbwld", 16))
                for dc in range(16):
                    if dc >= 2:
                        p.wge(sync, "mbmm", 5 * (dc - 1))
                    for hn, dst in (("xc_hi", xcf[dc % 2][0]), ("xc_lo", xcf[dc % 2][1])):
                        sync.dma_start(out=dst[:, :],
                                       in_=S[hn][dc * 128:(dc + 1) * 128, :]).then_inc(*p.inc("mbxld%d" % (dc % 2), 16))

            @blk.tensor
            def _(tensor):
                p.gate(tensor)
                p.wge(tensor, "mbwld", 16 * 32)
                for dc in range(16):
                    p.wge(tensor, "mbxld%d" % (dc % 2), 32 * (dc // 2 + 1))
                    for ti, (tt0, tl) in enumerate(MF_TT):
                        nmm = 0
                        for (kb, qb) in ((0, 0), (0, 1), (1, 0)):
                            nmm += 1
                            mm = tensor.matmul(ps[ti][0:96, 0:tl],
                                               wxf[kb][:, dc * 96:(dc + 1) * 96],
                                               xcf[dc % 2][qb][:, tt0:tt0 + tl],
                                               start=(dc == 0 and nmm == 1),
                                               stop=(dc == 15 and nmm == 3))
                        mm.then_inc(*p.inc("mbmm", 1))

            @blk.vector
            def _(v):
                p.gate(v)
                p.wge(v, "mbmm", 5 * 16)
                for ti, (tt0, tl) in enumerate(MF_TT):
                    v.tensor_copy(dr3[0:96, tt0:tt0 + tl], ps[ti][0:96, 0:tl])
                v.drain()
                v.tensor_copy(drh[:, :], dr3[0:64, :])
                v.drain()
                v.tensor_tensor(drl[:, :], dr3[0:64, :], drh[:, :], OP.subtract)
                v.drain()
                v.tensor_scalar(dr3[64:80, :], dr3[64:80, :], -1.0, None, OP.mult)
                v.drain().then_inc(p.s("mbev"), 1)
                p.inc("mbev", 1)

            @blk.scalar
            def _(se):
                p.gate(se)
                p.wge(se, "mbev", 1)
                se.dma_start(out=S["bc"][0:32, :], in_=dr3[64:96, :]).then_inc(*p.inc("mbst", 16))

    p.mark("mfb", "mbst")
    p.endphase("mfb")

    with nc.named_scope("mfc"), nc.Block("mfc") as blk:
        with contextlib.ExitStack() as ph:
            wdtf = [ph.enter_context(nc.sbuf_tensor("mf_wdt%s" % h, [64, 16 * 128], BF16)) for h in "hl"]
            sgt = [ph.enter_context(nc.sbuf_tensor("mf_sgt%d" % i, [128, 512], F32)) for i in range(2)]
            lnt = [ph.enter_context(nc.sbuf_tensor("mf_lnt%d" % i, [128, 512], F32)) for i in range(2)]

            @blk.sync
            def _(sync):
                p.gate(sync)
                p.wait(sync, "mfb")
                for hn, dst in (("wdt_hi", wdtf[0]), ("wdt_lo", wdtf[1])):
                    sync.dma_start(out=dst[:, 0:DIN],
                                   in_=S[hn][0:DTR, 0:DIN]).then_inc(*p.inc("mcwld", 16))

            @blk.tensor
            def _(tensor):
                p.gate(tensor)
                p.wge(tensor, "mcwld", 16 * 2)
                p.wait(tensor, "mfb")
                for dc in range(16):
                    for ti, (tt0, tl) in enumerate(MF_TT):
                        idx = dc * 5 + ti
                        if idx >= 2:
                            p.wge(tensor, "mcsg", idx - 1)
                        pb = ps[5 + idx % 2]
                        nmm = 0
                        for (kb, qb) in ((0, 0), (0, 1), (1, 0)):
                            nmm += 1
                            drq = drh if qb == 0 else drl
                            mm = tensor.matmul(pb[:, 0:tl],
                                               wdtf[kb][0:64, dc * 128:(dc + 1) * 128],
                                               drq[0:64, tt0:tt0 + tl],
                                               start=(nmm == 1), stop=(nmm == 3))
                        mm.then_inc(*p.inc("mcmm", 1))

            @blk.scalar
            def _(se):
                p.gate(se)
                for dc in range(16):
                    for ti, (tt0, tl) in enumerate(MF_TT):
                        idx = dc * 5 + ti
                        p.wge(se, "mcmm", idx + 1)
                        b = idx % 2
                        se.activation(out=sgt[b][:, 0:tl], in_=ps[5 + idx % 2][:, 0:tl],
                                      func=AF.Sigmoid, scale=-1.0, bias=bdtn[:, dc:dc + 1])
                        se.drain().then_inc(p.s("mcsg"), 1)
                        p.inc("mcsg", 1)
                        p.wge(se, "mcgv", idx + 1)
                        if idx >= 2:
                            p.wge(se, "mcst%d" % (idx % 2), 16 * (idx // 2))
                        se.activation(out=lnt[b][:, 0:tl], in_=sgt[b][:, 0:tl], func=AF.Ln)
                        se.drain().then_inc(p.s("mcln"), 1)
                        p.inc("mcln", 1)
                        p.wge(se, "mcln", idx + 1)
                        se.dma_start(out=S["lns"][dc * 128:(dc + 1) * 128, tt0:tt0 + tl],
                                     in_=lnt[b][:, 0:tl]).then_inc(*p.inc("mcst%d" % (idx % 2), 16))

            @blk.vector
            def _(v):
                p.gate(v)
                for dc in range(16):
                    for ti, (tt0, tl) in enumerate(MF_TT):
                        idx = dc * 5 + ti
                        b = idx % 2
                        p.wge(v, "mcsg", idx + 1)
                        v.tensor_scalar(sgt[b][:, 0:tl], sgt[b][:, 0:tl], 1e-38, None, OP.max)
                        p.inc("mcgv", 1)
                        v.drain().then_inc(p.s("mcgv"), 1)

    ph2.close()
    p.mark("mfc_a", "mcst0")
    p.mark("mfc_b", "mcst1")
    p.endphase("mfc_a", "mfc_b", "mfaz", "mfaz2")


CUBE_SEGS = [(0, 608), (608, 608), (1216, 608), (1824, 608)]
CUBE_SL = 608


def _cseg(j):
    si = j // 16
    s0, sl = CUBE_SEGS[si]
    o0, o1 = max(s0, WARM), s0 + sl
    return si, s0, sl, o0, o1 - o0


def _emit_cube(nc, p, stack, I, S, sb, ps, cst, ealog, state, dpc):
    SL = CUBE_SL
    NG = 16 * len(CUBE_SEGS)
    with nc.named_scope("cb"), nc.Block("cb") as blk:
        with contextlib.ExitStack() as ph:
            Bb = ph.enter_context(nc.sbuf_tensor("cb_B", [128, NST * SL], F32))
            Cb = ph.enter_context(nc.sbuf_tensor("cb_C", [128, NST * SL], F32))
            lnst = [ph.enter_context(nc.sbuf_tensor("cb_ln%d" % i, [128, SL], F32)) for i in range(3)]
            xct = [ph.enter_context(nc.sbuf_tensor("cb_xc%d" % i, [128, SL], F32)) for i in range(3)]
            szt = [ph.enter_context(nc.sbuf_tensor("cb_sz%d" % i, [128, SL], F32)) for i in range(3)]
            ut = ph.enter_context(nc.sbuf_tensor("cb_u", [128, SL], F32))
            itmp = ph.enter_context(nc.sbuf_tensor("cb_it", [128, NST], F32))
            da = ph.enter_context(nc.sbuf_tensor("cb_da", [128, NST * SL], F32))
            bx = ph.enter_context(nc.sbuf_tensor("cb_bx", [128, NST * SL], F32))
            yh = [ph.enter_context(nc.sbuf_tensor("cb_yh%d" % i, [128, SL], BF16)) for i in range(2)]
            yl = [ph.enter_context(nc.sbuf_tensor("cb_yl%d" % i, [128, SL], BF16)) for i in range(2)]

            def grp(t, n0, n1, w):
                return t[:, n0 * SL:n1 * SL].rearrange("p (n t) -> p n t", n=n1 - n0)[:, :, 0:w]

            @blk.sync
            def _(sync):
                p.gate(sync)
                p.wait(sync, "mfc_a")
                p.wait(sync, "mfc_b")
                p.wait(sync, "mfaz")
                p.wait(sync, "mfaz2")
                for si, (s0, sl) in enumerate(CUBE_SEGS):
                    if si > 0:
                        p.wge(sync, "cbdc", 16 * si)
                    for n in range(NST):
                        sync.dma_start(out=Bb[:, n * SL:n * SL + sl],
                                       in_=S["bc"][n:n + 1, s0:s0 + sl].partition_broadcast(128)
                                       ).then_inc(*p.inc("cbbc", 16))
                        sync.dma_start(out=Cb[:, n * SL:n * SL + sl],
                                       in_=S["bc"][NST + n:NST + n + 1, s0:s0 + sl].partition_broadcast(128)
                                       ).then_inc(*p.inc("cbbc", 16))
                    o0, o1 = max(s0, WARM), s0 + sl
                    for dc in range(16):
                        g = si * 16 + dc
                        if g >= 3:
                            p.wge(sync, "cbdc", g - 2)
                        sync.dma_start(out=lnst[g % 3][:, 0:sl],
                                       in_=S["lns"][dc * 128:(dc + 1) * 128, s0:s0 + sl]
                                       ).then_inc(*p.inc("cbld%d" % (g % 3), 16))
                        sync.dma_start(out=xct[g % 3][:, 0:sl],
                                       in_=S["xc"][dc * 128:(dc + 1) * 128, s0:s0 + sl]
                                       ).then_inc(*p.inc("cbld%d" % (g % 3), 16))
                        sync.dma_start(out=szt[g % 3][:, 0:o1 - o0],
                                       in_=S["siluz"][dc * 128:(dc + 1) * 128, o0 - WARM:o1 - WARM]
                                       ).then_inc(*p.inc("cbld%d" % (g % 3), 16))

            @blk.scalar
            def _(se):
                p.gate(se)
                for g in range(NG):
                    si, s0, sl, o0, ol = _cseg(g)
                    dc = g % 16
                    p.wge(se, "cbld%d" % (g % 3), 48 * (g // 3 + 1))
                    if g >= 1:
                        p.wge(se, "cbsc", g)
                    for n in range(NST):
                        se.activation(out=da[:, n * SL:n * SL + sl], in_=lnst[g % 3][:, 0:sl],
                                      func=AF.Exp, scale=ealog[:, dc * NST + n:dc * NST + n + 1])
                    p.inc("cbse", 1)
                    se.drain().then_inc(p.s("cbse"), 1)

            @blk.vector
            def _(v):
                p.gate(v)
                da3 = da[:, :].rearrange("p (n t) -> p n t", n=NST)
                bx3 = bx[:, :].rearrange("p (n t) -> p n t", n=NST)
                it3 = itmp[:, 1:NST].rearrange("p (n o) -> p n o", o=1)
                for g in range(NG):
                    si, s0, sl, o0, ol = _cseg(g)
                    dc = g % 16
                    st3 = state[:, dc * NST:(dc + 1) * NST].rearrange("p (n o) -> p n o", o=1)
                    if dc == 0:
                        p.wge(v, "cbbc", 32 * 16 * (si + 1))
                    p.wge(v, "cbld%d" % (g % 3), 48 * (g // 3 + 1))
                    v.tensor_tensor(ut[:, 0:sl], lnst[g % 3][:, 0:sl], xct[g % 3][:, 0:sl], OP.mult)
                    v.drain()
                    v.tensor_tensor(grp(bx, 0, NST, sl),
                                    ut[:, 0:sl].unsqueeze(1).broadcast_to((128, NST, sl)),
                                    grp(Bb, 0, NST, sl), OP.mult)
                    v.drain()
                    p.wge(v, "cbse", g + 1)
                    v.tensor_tensor(it3, da3[:, 1:NST, 0:1], st3[:, 1:NST, :], OP.mult)
                    v.drain()
                    v.tensor_tensor(bx3[:, 1:NST, 0:1], bx3[:, 1:NST, 0:1], it3, OP.add)
                    v.drain()
                    v.tensor_scalar(da3[:, 1:NST, 0:1], da3[:, 1:NST, 0:1], 0.0, None, OP.mult)
                    v.drain()
                    v.tensor_tensor_scan(bx[:, :], da[:, :], bx[:, :],
                                         state[:, dc * NST:dc * NST + 1], OP.mult, OP.add)
                    v.drain()
                    v.tensor_copy(st3, bx3[:, :, sl - 1:sl])
                    p.inc("cbsc", 1)
                    v.drain().then_inc(p.s("cbsc"), 1)
                    v.tensor_tensor(bx[:, :], bx[:, :], Cb[:, :], OP.mult)
                    v.drain()
                    for h in (8, 4, 2, 1):
                        v.tensor_tensor(grp(bx, 0, h, sl), grp(bx, 0, h, sl),
                                        grp(bx, h, 2 * h, sl), OP.add)
                        v.drain()
                    v.scalar_tensor_tensor(bx[:, 0:sl], xct[g % 3][:, 0:sl], dpc[:, dc:dc + 1],
                                           bx[:, 0:sl], OP.mult, OP.add)
                    v.drain()
                    if g >= 2:
                        p.wge(v, "cbst%d" % (g % 2), 32 * (g // 2))
                    v.tensor_tensor(ut[:, 0:ol], bx[:, o0 - s0:o0 - s0 + ol],
                                    szt[g % 3][:, 0:ol], OP.mult)
                    v.drain()
                    v.tensor_copy(yh[g % 2][:, 0:ol], ut[:, 0:ol])
                    v.drain()
                    v.tensor_tensor(yl[g % 2][:, 0:ol], ut[:, 0:ol], yh[g % 2][:, 0:ol],
                                    OP.subtract)
                    p.inc("cbdc", 1)
                    v.drain().then_inc(p.s("cbdc"), 1)

            @blk.gpsimd
            def _(gp):
                p.gate(gp)
                for g in range(NG):
                    si, s0, sl, o0, ol = _cseg(g)
                    dc = g % 16
                    p.wge(gp, "cbdc", g + 1)
                    gp.dma_start(out=S["ysz_hi"][dc * 128:(dc + 1) * 128, o0 - WARM:o0 - WARM + ol],
                                 in_=yh[g % 2][:, 0:ol]).then_inc(*p.inc("cbst%d" % (g % 2), 16))
                    gp.dma_start(out=S["ysz_lo"][dc * 128:(dc + 1) * 128, o0 - WARM:o0 - WARM + ol],
                                 in_=yl[g % 2][:, 0:ol]).then_inc(*p.inc("cbst%d" % (g % 2), 16))

    p.mark("cb_a", "cbst0")
    p.mark("cb_b", "cbst1")
    p.endphase("cb_a", "cb_b")


def _emit_wout(nc, p, stack, I, S, sb, ps, cst):
    for half in range(2):
        tag = "wo%d" % half
        with nc.named_scope(tag), nc.Block(tag) as blk:
            with contextlib.ExitStack() as ph:
                yf = [[ph.enter_context(nc.sbuf_tensor(tag + "_y%d%s" % (i, h), [128, TOK], BF16))
                       for h in "hl"] for i in range(8)]
                wf = [[ph.enter_context(nc.sbuf_tensor(tag + "_w%d%s" % (i, h), [128, 1024], BF16))
                       for h in "hl"] for i in range(8)]
                x1t = ph.enter_context(nc.sbuf_tensor(tag + "_x1", [128, 2 * 512], F32))
                x2t = ph.enter_context(nc.sbuf_tensor(tag + "_x2", [128, 2 * 512], F32))
                L, M, E, ST = tag + "ld", tag + "mm", tag + "ev", tag + "st"

                @blk.sync
                def _(sync):
                    p.gate(sync)
                    if half == 0:
                        p.wait(sync, "cb_a")
                        p.wait(sync, "cb_b")
                    else:
                        p.wge(sync, "wo0st0", 16 * NTO)
                        p.wge(sync, "wo0st1", 16 * NTO)
                    for i in range(8):
                        dc = half * 8 + i
                        for hn, dst in (("ysz_hi", yf[i][0]), ("ysz_lo", yf[i][1])):
                            sync.dma_start(out=dst[:, :],
                                           in_=S[hn][dc * 128:(dc + 1) * 128, :]).then_inc(*p.inc(L, 16))
                        for hn, dst in (("wout_hi", wf[i][0]), ("wout_lo", wf[i][1])):
                            sync.dma_start(out=dst[:, :],
                                           in_=S[hn][0:D, dc * 128:(dc + 1) * 128],
                                           transpose=True).then_inc(*p.inc(L, 16))
                    for tc in range(NTO):
                        for dh in range(2):
                            idx = tc * 2 + dh
                            if idx >= 2:
                                p.wge(sync, E, idx - 1)
                            sync.dma_start(out=x1t[:, (idx % 2) * 512:(idx % 2) * 512 + 512],
                                           in_=S["x1"][WARM + tc * 128:WARM + tc * 128 + 128,
                                                       dh * 512:(dh + 1) * 512]).then_inc(*p.inc(L + "x%d" % (idx % 2), 16))

                @blk.tensor
                def _(tensor):
                    p.gate(tensor)
                    p.wge(tensor, L, 16 * 32)
                    for tc in range(NTO):
                        for dh in range(2):
                            idx = tc * 2 + dh
                            if idx >= 4:
                                p.wge(tensor, E, idx - 3)
                            pr = ps[idx % 4]
                            nmm = 0
                            for i in range(8):
                                for (yb, wb) in ((0, 0), (0, 1), (1, 0)):
                                    nmm += 1
                                    mm = tensor.matmul(pr[:, 0:512],
                                                       yf[i][yb][:, tc * 128:tc * 128 + 128],
                                                       wf[i][wb][:, dh * 512:(dh + 1) * 512],
                                                       start=(nmm == 1), stop=(nmm == 24))
                            mm.then_inc(*p.inc(M, 1))

                @blk.vector
                def _(v):
                    p.gate(v)
                    for tc in range(NTO):
                        for dh in range(2):
                            idx = tc * 2 + dh
                            p.wge(v, M, idx + 1)
                            p.wge(v, L + "x%d" % (idx % 2), 16 * (idx // 2 + 1))
                            if idx >= 2:
                                p.wge(v, ST + str(idx % 2), 16 * (idx // 2))
                            b = (idx % 2) * 512
                            v.tensor_tensor(x2t[:, b:b + 512], ps[idx % 4][:, 0:512],
                                            x1t[:, b:b + 512], OP.add)
                            p.inc(E, 1)
                            v.drain().then_inc(p.s(E), 1)

                @blk.scalar
                def _(se):
                    p.gate(se)
                    for tc in range(NTO):
                        for dh in range(2):
                            idx = tc * 2 + dh
                            p.wge(se, E, idx + 1)
                            se.dma_start(out=S["x1"][WARM + tc * 128:WARM + tc * 128 + 128,
                                                     dh * 512:(dh + 1) * 512],
                                         in_=x2t[:, (idx % 2) * 512:(idx % 2) * 512 + 512]
                                         ).then_inc(*p.inc(ST + str(idx % 2), 16))

                p.mark("woh%d_a" % half, ST + "0")
                p.mark("woh%d_b" % half, ST + "1")
                p.endphase("woh%d_a" % half, "woh%d_b" % half)

    p.mark("p7_a", "wo1st0")
    p.mark("p7_b", "wo1st1")
    p.endphase("p7_a", "p7_b")


def _emit_ffn1(nc, p, stack, I, S, sb, ps, cst, cols):
    C_SIN, C_SIN6, C_SGA, C_SUA, C_SDA, C_SIND, C_SIN6D = cols

    G = 4
    NQX = NTO
    elad, cdone, exps, absg = {}, {}, {}, {}
    vc = sec = 0
    for g in range(NQX // G):
        vc += 1; elad[g] = vc
        sec += 1; exps[g] = sec
        for k in range(G):
            sec += 1; absg[g * G + k] = sec
            vc += 1; cdone[g * G + k] = vc

    with nc.named_scope("f1x"), nc.Block("f1x") as blk:
        with contextlib.ExitStack() as ph:
            xb = ph.enter_context(nc.sbuf_tensor("f1x_xb", [128, G * D], F32))
            ab = ph.enter_context(nc.sbuf_tensor("f1x_ab", [128, D], F32))
            sg = ph.enter_context(nc.sbuf_tensor("f1x_sg", [128, D], F32))
            qa = ph.enter_context(nc.sbuf_tensor("f1x_qa", [128, D], F32))
            qb = ph.enter_context(nc.sbuf_tensor("f1x_qb", [128, D], F32))
            qsl = ph.enter_context(nc.sbuf_tensor("f1x_qs", [128, 8 * G * 64], F32))
            uo = [ph.enter_context(nc.sbuf_tensor("f1x_u%d" % i, [128, D], BF16)) for i in range(2)]
            NBK = D // 16

            @blk.sync
            def _(sync):
                p.gate(sync)
                p.wait(sync, "ln1_a")
                p.wait(sync, "ln1_b")
                p.wait(sync, "wqscales")
                for tc in range(NQX):
                    if tc >= G:
                        p.wge(sync, "f1xse", absg[tc - G])
                    sync.dma_start(out=xb[:, (tc % G) * D:(tc % G) * D + D],
                                   in_=S["xn1f"][WARM + tc * 128:WARM + tc * 128 + 128, :]
                                   ).then_inc(*p.inc("f1xld%d" % (tc % 4), 16))

            @blk.vector
            def _(v):
                p.gate(v)
                for g in range(NQX // G):
                    for k in range(G):
                        tc = g * G + k
                        p.wge(v, "f1xld%d" % (tc % 4), 16 * (tc // 4 + 1))
                        _q_stageA(v, qsl, G, NBK, k, xb[:, (tc % G) * D:(tc % G) * D + D], D)
                    _q_stageB_v1(v, qsl, G, NBK, G, cst[:, C_SIN6:C_SIN6 + 1])
                    p.inc("f1xv", 1)
                    v.drain().then_inc(p.s("f1xv"), 1)
                    p.wge(v, "f1xse", exps[g])
                    _q_stageB_v2(v, qsl, G, NBK, G, cst[:, C_SIN:C_SIN + 1])
                    for k in range(G):
                        tc = g * G + k
                        p.wge(v, "f1xse", absg[tc])
                        if tc >= 2:
                            p.wge(v, "f1xst%d" % (tc % 2), 16 * (tc // 2))
                        _q_stageC_v(v, qsl, G, NBK, k, ab, sg, qa, qb, uo[tc % 2][:, :], D)
                        p.inc("f1xv", 1)
                        v.drain().then_inc(p.s("f1xv"), 1)

            @blk.scalar
            def _(se):
                p.gate(se)
                for g in range(NQX // G):
                    p.wge(se, "f1xv", elad[g])
                    _q_stageB_se(se, qsl, G, NBK, G, cst[:, 14:15], cst[:, 15:16])
                    p.inc("f1xse", 1)
                    se.drain().then_inc(p.s("f1xse"), 1)
                    for k in range(G):
                        tc = g * G + k
                        if tc >= 1:
                            p.wge(se, "f1xv", cdone[tc - 1])
                        _q_stageC_se(se, xb[:, (tc % G) * D:(tc % G) * D + D], ab, sg, D)
                        p.inc("f1xse", 1)
                        se.drain().then_inc(p.s("f1xse"), 1)

            @blk.gpsimd
            def _(g_):
                p.gate(g_)
                for tc in range(NQX):
                    p.wge(g_, "f1xv", cdone[tc])
                    g_.dma_start(out=S["ux"][tc * 128:(tc + 1) * 128, :],
                                 in_=uo[tc % 2][:, :]).then_inc(*p.inc("f1xst%d" % (tc % 2), 16))

    p.mark("f1x_a", "f1xst0")
    p.mark("f1x_b", "f1xst1")
    p.endphase("f1x_a", "f1x_b")

    G = 4
    NQH = 8 * NTO
    elad, cdone, exps, absg = {}, {}, {}, {}
    vc = sec = 0
    for g in range(NQH // G):
        vc += 1; elad[g] = vc
        sec += 1; exps[g] = sec
        for k in range(G):
            sec += 1; absg[g * G + k] = sec
            vc += 1; cdone[g * G + k] = vc

    with nc.named_scope("f1g"), nc.Block("f1g") as blk:
        with contextlib.ExitStack() as ph:
            uxf = ph.enter_context(nc.sbuf_tensor("f1g_ux", [128, 8 * TOK], BF16))
            wgf = [ph.enter_context(nc.sbuf_tensor("f1g_wg%d" % i, [128, 8 * 512], BF16)) for i in range(2)]
            wuf = [ph.enter_context(nc.sbuf_tensor("f1g_wu%d" % i, [128, 8 * 512], BF16)) for i in range(2)]
            sgt = [ph.enter_context(nc.sbuf_tensor("f1g_sg%d" % i, [128, 512], F32)) for i in range(2)]
            hht = ph.enter_context(nc.sbuf_tensor("f1g_hh", [128, G * 512], F32))
            ab = ph.enter_context(nc.sbuf_tensor("f1g_ab", [128, 512], F32))
            sgn = ph.enter_context(nc.sbuf_tensor("f1g_sn", [128, 512], F32))
            qa = ph.enter_context(nc.sbuf_tensor("f1g_qa", [128, 512], F32))
            qb = ph.enter_context(nc.sbuf_tensor("f1g_qb", [128, 512], F32))
            qsl = ph.enter_context(nc.sbuf_tensor("f1g_qs", [128, 8 * G * 32], F32))
            uh = [ph.enter_context(nc.sbuf_tensor("f1g_uh%d" % i, [128, 512], BF16)) for i in range(2)]
            NBK = 512 // 16

            @blk.sync
            def _(sync):
                p.gate(sync)
                p.wait(sync, "f1x_a")
                p.wait(sync, "f1x_b")
                p.wait(sync, "wq_a")
                p.wait(sync, "wq_b")
                for hc in range(8):
                    sync.dma_start(out=uxf[:, hc * TOK:(hc + 1) * TOK],
                                   in_=S["ux"][0:TOK, hc * 128:(hc + 1) * 128],
                                   transpose=True).then_inc(*p.inc("f1gxl", 16))
                for os_ in range(8):
                    if os_ >= 2:
                        p.wge(sync, "f1gmm", 2 * NTO * (os_ - 1))
                    for wn, dst in (("uwg", wgf[os_ % 2]), ("uwu", wuf[os_ % 2])):
                        for hc in range(8):
                            sync.dma_start(out=dst[:, hc * 512:(hc + 1) * 512],
                                           in_=S[wn][os_ * 512:(os_ + 1) * 512, hc * 128:(hc + 1) * 128],
                                           transpose=True).then_inc(*p.inc("f1gwl%d" % (os_ % 2), 16))

            @blk.tensor
            def _(tensor):
                p.gate(tensor)
                p.wge(tensor, "f1gxl", 16 * 8)
                for os_ in range(8):
                    p.wge(tensor, "f1gwl%d" % (os_ % 2), 16 * 16 * (os_ // 2 + 1))
                    for tc in range(NTO):
                        idx = os_ * NTO + tc
                        if idx >= 2:
                            p.wge(tensor, "f1gsg", idx - 1)
                        if idx >= 2:
                            p.wge(tensor, "f1ghh", idx - 1)
                        pg, pu = ps[2 * (idx % 2)], ps[2 * (idx % 2) + 1]
                        for (dst, wf) in ((pg, wgf[os_ % 2]), (pu, wuf[os_ % 2])):
                            for hc in range(8):
                                mm = tensor.matmul(dst[:, 0:512],
                                                   uxf[:, hc * TOK + tc * 128:hc * TOK + tc * 128 + 128],
                                                   wf[:, hc * 512:(hc + 1) * 512],
                                                   start=(hc == 0), stop=(hc == 7))
                            mm.then_inc(*p.inc("f1gmm", 1))

            @blk.vector
            def _(v):
                p.gate(v)
                for g in range(NQH // G):
                    for k in range(G):
                        idx = g * G + k
                        p.wge(v, "f1gmm", 2 * idx + 2)
                        p.wge(v, "f1gsg", idx + 1)
                        if idx >= G:
                            p.wge(v, "f1hse", absg[idx - G])
                        v.scalar_tensor_tensor(hht[:, (idx % G) * 512:(idx % G) * 512 + 512],
                                               ps[2 * (idx % 2) + 1][:, 0:512],
                                               cst[:, C_SUA:C_SUA + 1], sgt[idx % 2][:, :],
                                               OP.mult, OP.mult)
                        p.inc("f1ghh", 1)
                        v.drain().then_inc(p.s("f1ghh"), 1)
                        _q_stageA(v, qsl, G, NBK, k, hht[:, (idx % G) * 512:(idx % G) * 512 + 512], 512)
                    _q_stageB_v1(v, qsl, G, NBK, G, cst[:, C_SIN6D:C_SIN6D + 1])
                    p.inc("f1hv", 1)
                    v.drain().then_inc(p.s("f1hv"), 1)
                    p.wge(v, "f1hse", exps[g])
                    _q_stageB_v2(v, qsl, G, NBK, G, cst[:, C_SIND:C_SIND + 1])
                    for k in range(G):
                        idx = g * G + k
                        p.wge(v, "f1hse", absg[idx])
                        if idx >= 2:
                            p.wge(v, "f1gst%d" % (idx % 2), 16 * (idx // 2))
                        _q_stageC_v(v, qsl, G, NBK, k, ab, sgn, qa, qb, uh[idx % 2][:, :], 512)
                        p.inc("f1hv", 1)
                        v.drain().then_inc(p.s("f1hv"), 1)

            @blk.scalar
            def _(se):
                p.gate(se)
                for g in range(NQH // G):
                    for k in range(G):
                        idx = g * G + k
                        p.wge(se, "f1gmm", 2 * idx + 1)
                        if idx >= 2:
                            p.wge(se, "f1ghh", idx - 1)
                        se.activation(out=sgt[idx % 2][:, :], in_=ps[2 * (idx % 2)][:, 0:512],
                                      func=AF.Silu, scale=cst[:, C_SGA:C_SGA + 1])
                        se.drain().then_inc(p.s("f1gsg"), 1)
                        p.inc("f1gsg", 1)
                    p.wge(se, "f1hv", elad[g])
                    _q_stageB_se(se, qsl, G, NBK, G, cst[:, 14:15], cst[:, 15:16])
                    p.inc("f1hse", 1)
                    se.drain().then_inc(p.s("f1hse"), 1)
                    for k in range(G):
                        idx = g * G + k
                        p.wge(se, "f1ghh", idx + 1)
                        if idx >= 1:
                            p.wge(se, "f1hv", cdone[idx - 1])
                        _q_stageC_se(se, hht[:, (idx % G) * 512:(idx % G) * 512 + 512], ab, sgn, 512)
                        p.inc("f1hse", 1)
                        se.drain().then_inc(p.s("f1hse"), 1)

            @blk.gpsimd
            def _(g_):
                p.gate(g_)
                for os_ in range(8):
                    for tc in range(NTO):
                        idx = os_ * NTO + tc
                        p.wge(g_, "f1hv", cdone[idx])
                        g_.dma_start(out=S["uh"][tc * 128:(tc + 1) * 128, os_ * 512:(os_ + 1) * 512],
                                     in_=uh[idx % 2][:, :]).then_inc(*p.inc("f1gst%d" % (idx % 2), 16))

    p.mark("f1g_a", "f1gst0")
    p.mark("f1g_b", "f1gst1")
    p.endphase("f1g_a", "f1g_b")

    for half in range(2):
        tag = "f1d%d" % half
        with nc.named_scope(tag), nc.Block(tag) as blk:
            with contextlib.ExitStack() as ph:
                uhf = [ph.enter_context(nc.sbuf_tensor(tag + "_uh%d" % i, [128, TOK], BF16))
                       for i in range(16)]
                wdf = [ph.enter_context(nc.sbuf_tensor(tag + "_wd%d" % i, [128, 1024], BF16))
                       for i in range(16)]
                x1t = ph.enter_context(nc.sbuf_tensor(tag + "_x1", [128, 2 * 512], F32))
                x2t = ph.enter_context(nc.sbuf_tensor(tag + "_x2", [128, 2 * 512], F32))
                L, M, E, ST = tag + "ld", tag + "mm", tag + "ev", tag + "st"

                @blk.sync
                def _(sync):
                    p.gate(sync)
                    p.wait(sync, "f1g_a")
                    p.wait(sync, "f1g_b")
                    if half == 0:
                        p.wait(sync, "p7_a")
                        p.wait(sync, "p7_b")
                    else:
                        p.wge(sync, "f1d0st0", 16 * NTO)
                        p.wge(sync, "f1d0st1", 16 * NTO)
                    for i in range(16):
                        oc = half * 16 + i
                        sync.dma_start(out=uhf[i][:, :],
                                       in_=S["uh"][0:TOK, oc * 128:(oc + 1) * 128],
                                       transpose=True).then_inc(*p.inc(L, 16))
                        sync.dma_start(out=wdf[i][:, :],
                                       in_=S["uwd"][0:D, oc * 128:(oc + 1) * 128],
                                       transpose=True).then_inc(*p.inc(L, 16))
                    for tc in range(NTO):
                        for dh in range(2):
                            idx = tc * 2 + dh
                            if idx >= 2:
                                p.wge(sync, E, idx - 1)
                            sync.dma_start(out=x1t[:, (idx % 2) * 512:(idx % 2) * 512 + 512],
                                           in_=S["x1"][WARM + tc * 128:WARM + tc * 128 + 128,
                                                       dh * 512:(dh + 1) * 512]).then_inc(*p.inc(L + "x%d" % (idx % 2), 16))

                @blk.tensor
                def _(tensor):
                    p.gate(tensor)
                    p.wge(tensor, L, 16 * 32)
                    for tc in range(NTO):
                        for dh in range(2):
                            idx = tc * 2 + dh
                            if idx >= 4:
                                p.wge(tensor, E, idx - 3)
                            pr = ps[4 + idx % 4]
                            for i in range(16):
                                mm = tensor.matmul(pr[:, 0:512],
                                                   uhf[i][:, tc * 128:tc * 128 + 128],
                                                   wdf[i][:, dh * 512:(dh + 1) * 512],
                                                   start=(i == 0), stop=(i == 15))
                            mm.then_inc(*p.inc(M, 1))

                @blk.vector
                def _(v):
                    p.gate(v)
                    for tc in range(NTO):
                        for dh in range(2):
                            idx = tc * 2 + dh
                            p.wge(v, M, idx + 1)
                            p.wge(v, L + "x%d" % (idx % 2), 16 * (idx // 2 + 1))
                            if idx >= 2:
                                p.wge(v, ST + str(idx % 2), 16 * (idx // 2))
                            b = (idx % 2) * 512
                            v.scalar_tensor_tensor(x2t[:, b:b + 512], ps[4 + idx % 4][:, 0:512],
                                                   cst[:, C_SDA:C_SDA + 1], x1t[:, b:b + 512],
                                                   OP.mult, OP.add)
                            p.inc(E, 1)
                            v.drain().then_inc(p.s(E), 1)

                @blk.scalar
                def _(se):
                    p.gate(se)
                    for tc in range(NTO):
                        for dh in range(2):
                            idx = tc * 2 + dh
                            p.wge(se, E, idx + 1)
                            se.dma_start(out=S["x1"][WARM + tc * 128:WARM + tc * 128 + 128,
                                                     dh * 512:(dh + 1) * 512],
                                         in_=x2t[:, (idx % 2) * 512:(idx % 2) * 512 + 512]
                                         ).then_inc(*p.inc("f1d%dst%d" % (half, idx % 2), 16))

                p.mark("f1dh%d_a" % half, "f1d%dst0" % half)
                p.mark("f1dh%d_b" % half, "f1d%dst1" % half)
                p.endphase("f1dh%d_a" % half, "f1dh%d_b" % half)

    p.mark("p8_a", "f1d1st0")
    p.mark("p8_b", "f1d1st1")



def _stage_inputs(inputs, W):
    EXT = QW + W
    x = np.asarray(inputs["L0_attn_x"]).reshape(T_SEQ, D)
    maps = []
    for c in range(NCORES):
        g0 = c * TOK - WARM
        xl = np.zeros((EXT, D), dtype=np.float32)
        s0, s1 = max(0, g0), min(T_SEQ, g0 + EXT)
        xl[s0 - g0:s1 - g0, :] = x[s0:s1, :]
        um = np.zeros((1, QW), dtype=np.float32)
        um[0, max(0, -g0):] = 1.0
        m = {
            "x": xl,
            "decay_logit": np.asarray(inputs["L0_attn_decay_logit"], np.float32).reshape(1, 1),
            "out_scale": np.asarray(inputs["L0_attn_out_scale"], np.float32).reshape(1, 1),
            "q_scale": np.asarray(inputs["L0_attn_q_scale"], np.float32).reshape(1, D),
            "k_scale": np.asarray(inputs["L0_attn_k_scale"], np.float32).reshape(1, D),
            "w_gate": np.asarray(inputs["L0_ffn_w_gate"], np.int32),
            "w_up": np.asarray(inputs["L0_ffn_w_up"], np.int32),
            "w_down": np.asarray(inputs["L0_ffn_w_down"], np.int32),
            "W_in": np.asarray(inputs["L1_attn_W_in"], np.float32),
            "conv_w": np.asarray(inputs["L1_attn_conv_w"], np.float32),
            "conv_b": np.asarray(inputs["L1_attn_conv_b"], np.float32).reshape(1, DIN),
            "W_x_t": np.ascontiguousarray(np.asarray(inputs["L1_attn_W_x"], np.float32).T),
            "W_dt_t": np.ascontiguousarray(np.asarray(inputs["L1_attn_W_dt"], np.float32).T),
            "b_dt": np.asarray(inputs["L1_attn_b_dt"], np.float32).reshape(1, DIN),
            "A_log": np.asarray(inputs["L1_attn_A_log"], np.float32),
            "Dp": np.asarray(inputs["L1_attn_Dp"], np.float32).reshape(1, DIN),
            "W_out": np.asarray(inputs["L1_attn_W_out"], np.float32),
            "g1_w": np.asarray(inputs["L1_ffn_gate_w"], np.float32),
            "u1_w": np.asarray(inputs["L1_ffn_up_w"], np.float32),
            "d1_w": np.asarray(inputs["L1_ffn_down_w"], np.float32),
            "s_in": np.asarray(inputs["L1_ffn_s_in"], np.float32).reshape(1, 1),
            "s_in_down": np.asarray(inputs["L1_ffn_s_in_down"], np.float32).reshape(1, 1),
            "umask": um,
        }
        maps.append(m)
    return maps


def kernel(**inputs):
    W = _band_width(np.asarray(inputs["L0_attn_decay_logit"]).reshape(-1)[0])
    nc, stack = build_program(W)
    try:
        maps = _stage_inputs(inputs, W)
        res = run_bass_kernel_spmd(nc, maps, core_ids=list(range(NCORES)))
    finally:
        stack.close()
    y = np.concatenate([np.asarray(r["out"]) for r in res.results], axis=0)
    return y.reshape(1, T_SEQ, D).astype(np.float32)


# version 10
# speedup vs baseline: 1.0157x; 1.0157x over previous
import contextlib
import math
import numpy as np
import ml_dtypes
import concourse.bass as bass
import concourse.mybir as mybir
from concourse.bass_utils import run_bass_kernel_spmd

F32 = mybir.dt.float32
BF16 = mybir.dt.bfloat16
I32 = mybir.dt.int32
AF = mybir.ActivationFunctionType
OP = mybir.AluOpType
AX = mybir.AxisListType

T_SEQ = 16384
D = 1024
NCORES = 8
TOK = T_SEQ // NCORES
WARM = 384
QW = WARM + TOK
HFF0 = 3072
DIN = 2048
NST = 16
DTR = 64
KCV = 4
HFF1 = 4096
LN_EPS = 1e-5
FF0S = 1.0 / 127.0
NTC = QW // 128
NTO = TOK // 128

E2M1_MIDS = [0.25, 0.75, 1.25, 1.75, 2.5, 3.5, 5.0]


def _band_width(decay_logit):
    decay = 1.0 / (1.0 + math.exp(-float(decay_logit)))
    if decay <= 1e-6:
        return 512
    dstar = math.log(1.4e-45) / math.log(decay)
    w = int(math.ceil(1.2 * dstar / 512.0)) * 512
    return max(512, min(T_SEQ, w))


class P:

    RANGES = {"e": (0, 50), "h": (50, 84), "s": (84, 90)}
    SWDGE_PFX = ("wq1st", "f1xst", "f1gst", "cbst", "w0c")

    def __init__(self, nc, stack):
        self.nc = nc
        self.stack = stack
        self.pool = {}
        self.tot = {}
        self.map = {}
        self.counts = {"e": 0, "h": 0, "s": 0}
        self.marks = {}

    @classmethod
    def _cls(cls, name):
        if any(name.startswith(x) for x in cls.SWDGE_PFX):
            return "s"
        if ("ld" in name or "st" in name or
                name in ("c0b", "mfum", "wqs", "wqs2", "wqs3", "wqs4", "wqb", "cbbc")):
            return "h"
        return "e"

    def _ensure(self, name):
        if name not in self.map:
            c = self._cls(name)
            lo, hi = self.RANGES[c]
            pidx = lo + self.counts[c] % (hi - lo)
            self.counts[c] += 1
            if pidx not in self.pool:
                self.pool[pidx] = self.stack.enter_context(self.nc.semaphore("sp%d" % pidx))
                self.tot[pidx] = 0
            self.map[name] = (pidx, self.tot[pidx])
        return self.map[name]

    def s(self, name):
        pidx, _ = self._ensure(name)
        return self.pool[pidx]

    def inc(self, name, k):
        pidx, _ = self._ensure(name)
        self.tot[pidx] += k
        return self.pool[pidx], k

    def v(self, name):
        pidx, _ = self._ensure(name)
        return self.tot[pidx]

    def wge(self, eng, name, val):
        pidx, base = self._ensure(name)
        eng.wait_ge(self.pool[pidx], base + val)

    def mark(self, mname, sname):
        pidx, _ = self._ensure(sname)
        self.marks[mname] = (self.pool[pidx], self.tot[pidx])

    def wait(self, eng, mname):
        h, v = self.marks[mname]
        if v > 0:
            eng.wait_ge(h, v)

    def endphase(self, *mnames):
        self.prev_marks = list(mnames)

    def gate(self, eng):
        for m in getattr(self, "prev_marks", []):
            self.wait(eng, m)


def build_program(W):
    nc = bass.Bass("TRN2", target_bir_lowering=False)
    stack = contextlib.ExitStack()
    p = P(nc, stack)
    EXT = QW + W
    assert EXT % 128 == 0
    NB = (W + 512) // 128

    I = {}
    def din(name, shape, dtype=F32):
        I[name] = nc.dram_tensor(name, shape, dtype, kind="ExternalInput").ap()
    din("x", [EXT, D])
    din("decay_logit", [1, 1]); din("out_scale", [1, 1])
    din("q_scale", [1, D]); din("k_scale", [1, D])
    din("w_gate", [HFF0, D], I32); din("w_up", [HFF0, D], I32); din("w_down", [D, HFF0], I32)
    din("W_in", [2 * DIN, D]); din("conv_w", [DIN, KCV]); din("conv_b", [1, DIN])
    din("W_x_t", [DIN, DTR + 2 * NST]); din("W_dt_t", [DTR, DIN]); din("b_dt", [1, DIN])
    din("A_log", [DIN, NST]); din("Dp", [1, DIN]); din("W_out", [D, DIN])
    din("g1_w", [HFF1, D]); din("u1_w", [HFF1, D]); din("d1_w", [D, HFF1])
    din("s_in", [1, 1]); din("s_in_down", [1, 1])
    din("umask", [1, QW])
    out = nc.dram_tensor("out", [TOK, D], F32, kind="ExternalOutput").ap()

    S = {}
    def scr(name, shape, dtype=F32):
        S[name] = nc.dram_tensor("sc_" + name, shape, dtype, kind="Internal").ap()
    scr("xn_hi", [EXT, D], BF16); scr("xn_lo", [EXT, D], BF16)
    scr("qp_hi", [QW, D], BF16); scr("qp_lo", [QW, D], BF16)
    scr("x1", [QW, D])
    scr("row", [8, 4096])
    scr("wg_b", [HFF0, D], BF16); scr("wu_b", [HFF0, D], BF16); scr("wd_b", [D, HFF0], BF16)
    scr("win_hi", [2 * DIN, D], BF16); scr("win_lo", [2 * DIN, D], BF16)
    scr("wout_hi", [D, DIN], BF16); scr("wout_lo", [D, DIN], BF16)
    scr("wx_hi", [DIN, DTR + 2 * NST], BF16); scr("wx_lo", [DIN, DTR + 2 * NST], BF16)
    scr("wdt_hi", [DTR, DIN], BF16); scr("wdt_lo", [DTR, DIN], BF16)
    scr("uwg", [HFF1, D], BF16); scr("uwu", [HFF1, D], BF16); scr("uwd", [D, HFF1], BF16)
    scr("xc", [DIN, QW]); scr("xc_hi", [DIN, QW], BF16); scr("xc_lo", [DIN, QW], BF16)
    scr("lns", [DIN, QW])
    scr("bc", [2 * NST, QW])
    scr("siluz", [DIN, TOK])
    scr("ysz_hi", [DIN, TOK], BF16); scr("ysz_lo", [DIN, TOK], BF16)
    scr("hh_hi", [QW, HFF0], BF16); scr("hh_lo", [QW, HFF0], BF16)
    scr("xn1f", [QW, D]); scr("xn1_hi", [QW, D], BF16); scr("xn1_lo", [QW, D], BF16)
    scr("ux", [TOK, D], BF16); scr("uh", [TOK, HFF1], BF16)

    sb = {}
    def sbuf(name, shape, dtype=F32):
        sb[name] = stack.enter_context(nc.sbuf_tensor("sb_" + name, shape, dtype))
        return sb[name]
    ps = [stack.enter_context(nc.psum_tensor("ps%d" % i, [128, 512], F32)) for i in range(8)]

    cst = sbuf("cst", [128, 40])
    convw = sbuf("convw", [128, 16 * KCV])
    convb = sbuf("convb", [128, 16])
    bdtn = sbuf("bdtn", [128, 16])
    dpc = sbuf("dpc", [128, 16])
    ealog = sbuf("ealog", [128, 16 * NST])
    state = sbuf("state", [128, 16 * NST])
    astack = contextlib.ExitStack()
    def asbuf(name, shape, dtype=F32):
        sb[name] = astack.enter_context(nc.sbuf_tensor("sb_" + name, shape, dtype))
        return sb[name]
    b_tile = asbuf("b_tile", [128, 512])
    a_mat = asbuf("a_mat", [128, NB + 2])
    wts = [asbuf("wt%d" % k, [128, 512]) for k in range(4)]
    qk_b = asbuf("qk_b", [128, D])

    C_LNDEC, C_NLNDEC, C_OS, C_SIN6, C_SIN, C_SIND, C_GWG, C_GWU, C_GWD, \
        C_SGA, C_SUA, C_SDA, C_LN2, C_SIN6D = range(14)

    eng = {}

    c0stack = contextlib.ExitStack()
    with nc.named_scope("c0"), nc.Block("c0") as blk:
        t0 = c0stack.enter_context(nc.sbuf_tensor("c0_t0", [1, 4096], F32))
        t1 = c0stack.enter_context(nc.sbuf_tensor("c0_t1", [1, 4096], F32))
        io0 = c0stack.enter_context(nc.sbuf_tensor("c0_io", [128, 2144], F32))

        @blk.sync
        def _(sync):
            p.gate(sync)
            sync.dma_start(out=t0[0:1, 0:1], in_=I["decay_logit"]).then_inc(*p.inc("c0ld", 16))
            sync.dma_start(out=t0[0:1, 1:2], in_=I["out_scale"]).then_inc(*p.inc("c0ld", 16))
            sync.dma_start(out=t0[0:1, 2:3], in_=I["s_in"]).then_inc(*p.inc("c0ld", 16))
            sync.dma_start(out=t0[0:1, 3:4], in_=I["s_in_down"]).then_inc(*p.inc("c0ld", 16))
            sync.dma_start(out=t0[0:1, 16:16 + D], in_=I["q_scale"]).then_inc(*p.inc("c0ld", 16))
            sync.dma_start(out=t0[0:1, 16 + D:16 + 2 * D], in_=I["k_scale"]).then_inc(*p.inc("c0ld", 16))
            sync.dma_start(out=convw[:, :].rearrange("p (c k) -> p c k", k=KCV),
                           in_=I["conv_w"].rearrange("(c p) k -> p c k", p=128)).then_inc(*p.inc("c0ld", 16))
            sync.dma_start(out=convb[:, :].rearrange("p (o c) -> p o c", o=1),
                           in_=I["conv_b"].rearrange("o (c p) -> p o c", p=128),
                           allow_slow_non_contiguous=True).then_inc(*p.inc("c0ld", 16))
            sync.dma_start(out=bdtn[:, :].rearrange("p (o c) -> p o c", o=1),
                           in_=I["b_dt"].rearrange("o (c p) -> p o c", p=128),
                           allow_slow_non_contiguous=True).then_inc(*p.inc("c0ld", 16))
            sync.dma_start(out=dpc[:, :].rearrange("p (o c) -> p o c", o=1),
                           in_=I["Dp"].rearrange("o (c p) -> p o c", p=128),
                           allow_slow_non_contiguous=True).then_inc(*p.inc("c0ld", 16))
            sync.dma_start(out=ealog[:, :].rearrange("p (c n) -> p c n", n=NST),
                           in_=I["A_log"].rearrange("(c p) n -> p c n", p=128)).then_inc(*p.inc("c0ld", 16))

        @blk.gpsimd
        def _(g):
            p.gate(g)
            g.iota(t1[0:1, 0:512], [[1, 512]], channel_multiplier=0,
                   allow_small_or_imprecise_dtypes=True)
            g.iota(t1[0:1, 512:512 + NB + 2], [[128, NB + 2]], base=-1, channel_multiplier=0,
                   allow_small_or_imprecise_dtypes=True)
            g.iota(io0[:, 0:1], [[0, 1]], channel_multiplier=1,
                   allow_small_or_imprecise_dtypes=True).then_inc(*p.inc("c0io", 1))
            for k in range(4):
                g.iota(io0[:, 1 + k * 512:1 + (k + 1) * 512], [[-1, 512]], base=128 * k - 1,
                       channel_multiplier=1, allow_small_or_imprecise_dtypes=True)
            g.iota(io0[:, 1 + 4 * 512:1 + 4 * 512 + 1], [[0, 1]], base=0,
                   channel_multiplier=0, allow_small_or_imprecise_dtypes=True).then_inc(*p.inc("c0io", 1))

        @blk.scalar
        def _(se):
            p.gate(se)
            se.wait_ge(p.s("c0ld"), p.v("c0ld"))
            se.activation(out=t0[0:1, 4:5], in_=t0[0:1, 0:1], func=AF.Sigmoid)
            se.drain()
            se.activation(out=t0[0:1, 5:6], in_=t0[0:1, 4:5], func=AF.Ln).then_inc(*p.inc("c0a", 1))
            se.activation(out=ealog[:, :], in_=ealog[:, :], func=AF.Exp).then_inc(*p.inc("c0a", 1))

        @blk.vector
        def _(v):
            p.gate(v)
            p.wge(v, "c0a", 1)
            v.tensor_scalar(t0[0:1, 6:7], t0[0:1, 5:6], -1.0, None, OP.mult)
            v.tensor_tensor(t0[0:1, 16:16 + D], t0[0:1, 16:16 + D], t0[0:1, 16 + D:16 + 2 * D], OP.mult)
            v.tensor_scalar(t1[0:1, 1024:1025], t0[0:1, 2:3], 1.0 / 6.0, None, OP.mult)
            v.drain().then_inc(*p.inc("c0v", 1))
            v.tensor_scalar(bdtn[:, :], bdtn[:, :], -1.0, None, OP.mult).then_inc(*p.inc("c0v", 1))

        @blk.scalar
        def _(se):
            p.gate(se)
            p.wge(se, "c0v", 1)
            p.wge(se, "c0io", 1)
            se.activation(out=t1[0:1, 0:512], in_=t1[0:1, 0:512], func=AF.Exp,
                          scale=t0[0:1, 6:7]).then_inc(*p.inc("c0a", 1))
            se.activation(out=t1[0:1, 512:512 + NB + 2], in_=t1[0:1, 512:512 + NB + 2], func=AF.Exp,
                          scale=t0[0:1, 5:6]).then_inc(*p.inc("c0a", 1))

        @blk.sync
        def _(sync):
            p.gate(sync)
            sync.wait_ge(p.s("c0a"), p.v("c0a"))
            sync.dma_start(out=S["row"][0:1, 0:7], in_=t0[0:1, 0:7]).then_inc(*p.inc("c0st", 16))
            sync.dma_start(out=S["row"][0:1, 16:16 + D], in_=t0[0:1, 16:16 + D]).then_inc(*p.inc("c0st", 16))
            sync.dma_start(out=S["row"][1:2, 0:512 + NB + 2], in_=t1[0:1, 0:512 + NB + 2]).then_inc(*p.inc("c0st", 16))
            sync.dma_start(out=S["row"][1:2, 1024:1025], in_=t1[0:1, 1024:1025]).then_inc(*p.inc("c0st", 16))
            sync.wait_ge(p.s("c0st"), p.v("c0st"))
            sync.dma_start(out=cst[:, C_LNDEC:C_LNDEC + 1],
                           in_=S["row"][0:1, 5:6].partition_broadcast(128)).then_inc(*p.inc("c0b", 16))
            sync.dma_start(out=cst[:, C_NLNDEC:C_NLNDEC + 1],
                           in_=S["row"][0:1, 6:7].partition_broadcast(128)).then_inc(*p.inc("c0b", 16))
            sync.dma_start(out=cst[:, C_OS:C_OS + 1],
                           in_=S["row"][0:1, 1:2].partition_broadcast(128)).then_inc(*p.inc("c0b", 16))
            sync.dma_start(out=cst[:, C_SIN:C_SIN + 1],
                           in_=S["row"][0:1, 2:3].partition_broadcast(128)).then_inc(*p.inc("c0b", 16))
            sync.dma_start(out=cst[:, C_SIND:C_SIND + 1],
                           in_=S["row"][0:1, 3:4].partition_broadcast(128)).then_inc(*p.inc("c0b", 16))
            sync.dma_start(out=cst[:, C_SIN6:C_SIN6 + 1],
                           in_=S["row"][1:2, 1024:1025].partition_broadcast(128)).then_inc(*p.inc("c0b", 16))
            sync.dma_start(out=b_tile[:, :],
                           in_=S["row"][1:2, 0:512].partition_broadcast(128)).then_inc(*p.inc("c0b", 16))
            sync.dma_start(out=io0[:, 1 + 4 * 512 + 1:1 + 4 * 512 + 1 + NB + 2],
                           in_=S["row"][1:2, 512:512 + NB + 2].partition_broadcast(128)).then_inc(*p.inc("c0b", 16))
            sync.dma_start(out=qk_b[:, :],
                           in_=S["row"][0:1, 16:16 + D].partition_broadcast(128)).then_inc(*p.inc("c0b", 16))

        @blk.scalar
        def _(se):
            p.gate(se)
            se.wait_ge(p.s("c0b"), p.v("c0b"))
            p.wge(se, "c0io", 2)
            se.activation(out=io0[:, 0:1], in_=io0[:, 0:1], func=AF.Exp,
                          scale=cst[:, C_LNDEC:C_LNDEC + 1])
            for k in range(4):
                se.activation(out=wts[k][:, :], in_=io0[:, 1 + k * 512:1 + (k + 1) * 512],
                              func=AF.Exp, scale=cst[:, C_LNDEC:C_LNDEC + 1])
            se.drain().then_inc(*p.inc("c0a", 1))

        @blk.vector
        def _(v):
            p.gate(v)
            v.wait_ge(p.s("c0a"), p.v("c0a"))
            v.tensor_scalar(a_mat[:, :], io0[:, 1 + 4 * 512 + 1:1 + 4 * 512 + 1 + NB + 2],
                            io0[:, 0:1], None, OP.mult)
            for k in range(4):
                v.tensor_scalar(io0[:, 1 + k * 512:1 + (k + 1) * 512],
                                io0[:, 1 + k * 512:1 + (k + 1) * 512], -0.5, None, OP.is_gt)
            v.drain()
            for k in range(4):
                v.tensor_tensor(wts[k][:, :], wts[k][:, :], io0[:, 1 + k * 512:1 + (k + 1) * 512], OP.mult)
            v.memset(state[:, :], 0.0)
            v.memset(cst[:, 14:15], -10.0 * math.log(2.0))
            v.memset(cst[:, 15:16], 10.0 * math.log(2.0))
            v.memset(cst[:, 16:17], LN_EPS)
            v.drain().then_inc(*p.inc("c0v", 1))

    c0stack.close()
    p.mark("c0", "c0v")
    p.mark("c0b", "c0b")
    p.endphase("c0", "c0b")

    w0stack = contextlib.ExitStack()
    with nc.named_scope("w0"), nc.Block("w0") as blk:
        wtile = w0stack.enter_context(nc.sbuf_tensor("w0_a", [128, 2 * 4096], F32))
        whi = w0stack.enter_context(nc.sbuf_tensor("w0_hi", [128, 2 * 4096], BF16))
        wlo = w0stack.enter_context(nc.sbuf_tensor("w0_lo", [128, 2 * 4096], BF16))

        SPLITS = [("W_in", "win_hi", "win_lo", 2 * DIN, D),
                  ("W_out", "wout_hi", "wout_lo", D, DIN),
                  ("W_x_t", "wx_hi", "wx_lo", DIN, DTR + 2 * NST),
                  ("W_dt_t", "wdt_hi", "wdt_lo", DTR, DIN)]
        CASTS = [("w_gate", "wg_b", HFF0, D), ("w_up", "wu_b", HFF0, D),
                 ("w_down", "wd_b", D, HFF0)]

        def chunks(rows):
            o = 0
            while o < rows:
                yield o, min(128, rows - o)
                o += 128

        @blk.sync
        def _(sync):
            p.gate(sync)
            i = 0
            for (src, hi, lo, rows, cols) in SPLITS:
                for (o, r) in chunks(rows):
                    if i >= 2:
                        p.wge(sync, "w0v", i - 1)
                    buf = (i % 2) * 4096
                    sync.dma_start(out=wtile[0:r, buf:buf + cols],
                                   in_=I[src][o:o + r, :]).then_inc(*p.inc("w0ld%d" % (i % 2), 16))
                    i += 1

        @blk.vector
        def _(v):
            p.gate(v)
            i = 0
            for (src, hi, lo, rows, cols) in SPLITS:
                for (o, r) in chunks(rows):
                    p.wge(v, "w0ld%d" % (i % 2), 16 * (i // 2 + 1))
                    if i >= 2:
                        p.wge(v, "w0st%d" % (i % 2), 32 * (i // 2))
                    buf = (i % 2) * 4096
                    v.tensor_copy(whi[0:r, buf:buf + cols], wtile[0:r, buf:buf + cols])
                    v.drain()
                    v.tensor_tensor(wlo[0:r, buf:buf + cols], wtile[0:r, buf:buf + cols],
                                    whi[0:r, buf:buf + cols], OP.subtract)
                    v.drain().then_inc(*p.inc("w0v", 1))
                    i += 1

        @blk.scalar
        def _(se):
            p.gate(se)
            i = 0
            for (src, hi, lo, rows, cols) in SPLITS:
                for (o, r) in chunks(rows):
                    p.wge(se, "w0v", i + 1)
                    buf = (i % 2) * 4096
                    se.dma_start(out=S[hi][o:o + r, :], in_=whi[0:r, buf:buf + cols]).then_inc(*p.inc("w0st%d" % (i % 2), 16))
                    se.dma_start(out=S[lo][o:o + r, :], in_=wlo[0:r, buf:buf + cols]).then_inc(*p.inc("w0st%d" % (i % 2), 16))
                    i += 1

        @blk.gpsimd
        def _(g):
            p.gate(g)
            for (src, dst, rows, cols) in CASTS:
                g.dma_start(out=S[dst][:, :], in_=I[src][:, :]).then_inc(*p.inc("w0c", 16))

    w0stack.close()
    p.mark("w0_a", "w0st0")
    p.mark("w0_b", "w0st1")
    p.mark("w0c", "w0c")
    p.endphase("w0_a", "w0_b", "w0c")

    _emit_wquant(nc, p, stack, I, S, sb, cst, ps)

    _emit_ln(nc, p, stack, I, S, sb, src_kind="x", n_tiles=EXT // 128, qp=True,
             qk_b=qk_b, tag="ln0", beps=cst[:, 16:17])

    _emit_attn(nc, p, stack, I, S, sb, ps, cst, b_tile, a_mat, wts, W, NB, C_OS)
    astack.close()

    _emit_ffn0(nc, p, stack, I, S, sb, ps, cst)

    _emit_ln(nc, p, stack, I, S, sb, src_kind="x1", n_tiles=QW // 128, qp=False,
             qk_b=None, tag="ln1", beps=cst[:, 16:17])

    _emit_mamba_front(nc, p, stack, I, S, sb, ps, cst, convw, convb, bdtn)

    _emit_cube(nc, p, stack, I, S, sb, ps, cst, ealog, state, dpc)

    _emit_wout(nc, p, stack, I, S, sb, ps, cst)

    _emit_ffn1(nc, p, stack, I, S, sb, ps, cst,
               (C_SIN, C_SIN6, C_SGA, C_SUA, C_SDA, C_SIND, C_SIN6D))

    with nc.Block("p9") as blk:
        @blk.sync
        def _(sync):
            p.gate(sync)
            p.wait(sync, "p8_a")
            p.wait(sync, "p8_b")
            sync.dma_start(out=out[:, :], in_=S["x1"][WARM:WARM + TOK, :]).then_inc(*p.inc("p9st", 16))
            p.wge(sync, "p9st", 16)

    return nc, stack



def _emit_ln(nc, p, stack, I, S, sb, src_kind, n_tiles, qp, qk_b, tag, beps):
    with nc.named_scope(tag), nc.Block(tag) as blk:
        with contextlib.ExitStack() as ph:
            xt = [ph.enter_context(nc.sbuf_tensor(f"{tag}_x{i}", [128, D], F32)) for i in range(2)]
            xn = [ph.enter_context(nc.sbuf_tensor(f"{tag}_n{i}", [128, D], F32)) for i in range(2)]
            hi = [ph.enter_context(nc.sbuf_tensor(f"{tag}_h{i}", [128, D], BF16)) for i in range(2)]
            lo = [ph.enter_context(nc.sbuf_tensor(f"{tag}_l{i}", [128, D], BF16)) for i in range(2)]
            qh = [ph.enter_context(nc.sbuf_tensor(f"{tag}_qh{i}", [128, D], BF16)) for i in range(2)]
            ql = [ph.enter_context(nc.sbuf_tensor(f"{tag}_ql{i}", [128, D], BF16)) for i in range(2)]
            q3 = ph.enter_context(nc.sbuf_tensor(f"{tag}_q3", [128, D], F32))
            st = ph.enter_context(nc.sbuf_tensor(f"{tag}_st", [128, 24], F32))

            src = I["x"] if src_kind == "x" else S["x1"]
            hid, lod = (("xn_hi", "xn_lo") if src_kind == "x" else ("xn1_hi", "xn1_lo"))
            L, A, V, Q, ST = tag + "ld", tag + "ap", tag + "v", tag + "sq", tag + "st"

            @blk.sync
            def _(sync):
                p.gate(sync)
                if src_kind == "x1":
                    p.wait(sync, "p3_a")
                    p.wait(sync, "p3_b")
                for i in range(n_tiles):
                    if i >= 2:
                        p.wge(sync, A, i - 1)
                    sync.dma_start(out=xt[i % 2][:, :],
                                   in_=src[i * 128:(i + 1) * 128, :]).then_inc(*p.inc(L + str(i % 2), 16))

            nst_tile = [2 + (1 if src_kind == "x1" else 0) +
                        (2 if (qp and i < QW // 128) else 0) for i in range(n_tiles)]
            cum_st = [0]
            for c_ in nst_tile:
                cum_st.append(cum_st[-1] + c_)

            @blk.vector
            def _(v):
                p.gate(v)
                for i in range(n_tiles):
                    b = i % 2
                    p.wge(v, L + str(b), 16 * (i // 2 + 1))
                    if i >= 2:
                        p.wge(v, ST + str(b), 16 * sum(nst_tile[j] for j in range(b, i - 1, 2)))
                    v.bn_stats(st[:, 0:6], xt[b][:, 0:512])
                    v.drain()
                    v.bn_stats(st[:, 6:12], xt[b][:, 512:1024])
                    v.drain()
                    v.bn_aggr(st[:, 12:14], st[:, 0:12])
                    v.drain().then_inc(*p.inc(V, 1))
                    p.wge(v, Q, i + 1)
                    v.reciprocal(st[:, 15:16], st[:, 14:15])
                    v.drain()
                    v.tensor_tensor(st[:, 16:17], st[:, 15:16], st[:, 15:16], OP.mult)
                    v.tensor_scalar(st[:, 17:18], st[:, 13:14], LN_EPS, None, OP.add)
                    v.drain()
                    v.tensor_tensor(st[:, 18:19], st[:, 16:17], st[:, 17:18], OP.mult)
                    v.drain()
                    v.tensor_scalar(st[:, 18:19], st[:, 18:19], -0.5, 1.5, OP.mult, OP.add)
                    v.drain()
                    v.tensor_tensor(st[:, 19:20], st[:, 15:16], st[:, 18:19], OP.mult)
                    v.drain()
                    v.tensor_scalar(xn[b][:, :], xt[b][:, :], st[:, 12:13], st[:, 19:20],
                                    OP.subtract, OP.mult).then_inc(*p.inc(A, 1))
                    v.drain()
                    v.tensor_copy(hi[b][:, :], xn[b][:, :])
                    v.drain()
                    v.tensor_tensor(lo[b][:, :], xn[b][:, :], hi[b][:, :], OP.subtract)
                    if qp and i < QW // 128:
                        v.drain()
                        v.tensor_tensor(q3[:, :], xn[b][:, :], qk_b[:, :], OP.mult)
                        v.drain()
                        v.tensor_copy(qh[b][:, :], q3[:, :])
                        v.drain()
                        v.tensor_tensor(ql[b][:, :], q3[:, :], qh[b][:, :], OP.subtract)
                    v.drain().then_inc(*p.inc(V, 1))

            @blk.scalar
            def _(se):
                p.gate(se)
                for i in range(n_tiles):
                    b = i % 2
                    p.wge(se, V, 2 * i + 1)
                    se.activation(out=st[:, 14:15], in_=st[:, 13:14], func=AF.Sqrt,
                                  bias=beps).then_inc(*p.inc(Q, 1))
                    p.wge(se, V, 2 * i + 2)
                    se.dma_start(out=S[hid][i * 128:(i + 1) * 128, :], in_=hi[b][:, :]).then_inc(*p.inc(ST + str(b), 16))
                    se.dma_start(out=S[lod][i * 128:(i + 1) * 128, :], in_=lo[b][:, :]).then_inc(*p.inc(ST + str(b), 16))
                    if src_kind == "x1":
                        se.dma_start(out=S["xn1f"][i * 128:(i + 1) * 128, :],
                                     in_=xn[b][:, :]).then_inc(*p.inc(ST + str(b), 16))
                    if qp and i < QW // 128:
                        se.dma_start(out=S["qp_hi"][i * 128:(i + 1) * 128, :],
                                     in_=qh[b][:, :]).then_inc(*p.inc(ST + str(b), 16))
                        se.dma_start(out=S["qp_lo"][i * 128:(i + 1) * 128, :],
                                     in_=ql[b][:, :]).then_inc(*p.inc(ST + str(b), 16))

    p.mark(tag + "_a", tag + "st0")
    p.mark(tag + "_b", tag + "st1")
    p.endphase(tag + "_a", tag + "_b")


def _q_stageA(v, qsl, G, NBK, slot, x_ap, C):
    amax = qsl[:, 0 * G * NBK + slot * NBK: 0 * G * NBK + (slot + 1) * NBK]
    v.tensor_reduce(amax.rearrange("p (b o) -> p b o", o=1),
                    x_ap.rearrange("p (b k) -> p b k", k=16),
                    AX.X, OP.max, apply_absolute_value=True)
    v.drain()


def _q_stageB_v1(v, qsl, G, NBK, ng, gs6_col):
    W = ng * NBK
    GN = G * NBK
    amax = qsl[:, 0:W]
    eidx = qsl[:, GN:GN + W]
    tmp = qsl[:, 7 * GN:7 * GN + W]
    v.tensor_scalar(tmp, amax, gs6_col, None, OP.mult)
    v.drain()
    v.tensor_scalar(tmp, tmp, 448.0, None, OP.min)
    v.drain()
    v.tensor_scalar(eidx, tmp, 2.0 ** -9, None, OP.is_ge)
    for k in range(-8, 9):
        v.drain()
        v.scalar_tensor_tensor(eidx, tmp, float(2.0 ** k), eidx, OP.is_ge, OP.add)
    v.drain()


def _q_stageB_se(se, qsl, G, NBK, ng, bn10, bp10):
    W = ng * NBK
    GN = G * NBK
    LN2 = math.log(2.0)
    eidx = qsl[:, GN:GN + W]
    bexp = qsl[:, 2 * GN:2 * GN + W]
    rbexp = qsl[:, 3 * GN:3 * GN + W]
    se.activation(out=bexp, in_=eidx, func=AF.Exp, scale=LN2, bias=bn10)
    se.drain()
    se.activation(out=rbexp, in_=eidx, func=AF.Exp, scale=-LN2, bias=bp10)
    se.drain()


def _q_stageB_v2(v, qsl, G, NBK, ng, gs_col):
    W = ng * NBK
    GN = G * NBK
    bexp = qsl[:, 2 * GN:2 * GN + W]
    rbexp = qsl[:, 3 * GN:3 * GN + W]
    m = qsl[:, 4 * GN:4 * GN + W]
    sc8 = qsl[:, 5 * GN:5 * GN + W]
    rsg = qsl[:, 6 * GN:6 * GN + W]
    tmp = qsl[:, 7 * GN:7 * GN + W]
    v.tensor_scalar(bexp, bexp, 2.0 ** -6, None, OP.max)
    v.drain()
    v.tensor_scalar(rbexp, rbexp, 2.0 ** 6, None, OP.min)
    v.drain()
    v.tensor_tensor(m, tmp, rbexp, OP.mult)
    v.drain()
    v.tensor_scalar(sc8, m, 1.0 / 16.0, None, OP.is_ge)
    for j in range(1, 16):
        v.drain()
        v.scalar_tensor_tensor(sc8, m, float((2 * j + 1) / 16.0), sc8, OP.is_ge, OP.add)
    v.drain()
    v.tensor_tensor(sc8, sc8, bexp, OP.mult)
    v.drain()
    v.tensor_scalar(sc8, sc8, 0.125, None, OP.mult)
    v.drain()
    v.tensor_scalar(tmp, sc8, 1e-30, None, OP.max)
    v.drain()
    v.reciprocal(rsg, tmp)
    v.drain()
    v.tensor_scalar(rsg, rsg, gs_col, None, OP.mult)
    v.drain()


def _q_stageC_se(se, x_ap, ab, sg, C):
    se.activation(out=ab[:, 0:C], in_=x_ap, func=AF.Abs)
    se.drain()
    se.activation(out=sg[:, 0:C], in_=x_ap, func=AF.Sign)
    se.drain()


def _q_stageC_v(v, qsl, G, NBK, slot, ab, sg, qa, qb, u_out, C):
    GN = G * NBK
    sc8 = qsl[:, 5 * GN + slot * NBK:5 * GN + (slot + 1) * NBK]
    rsg = qsl[:, 6 * GN + slot * NBK:6 * GN + (slot + 1) * NBK]
    A = qa[:, 0:C]
    Cq = qb[:, 0:C]
    v.tensor_tensor(A.rearrange("p (b k) -> p b k", k=16),
                    ab[:, 0:C].rearrange("p (b k) -> p b k", k=16),
                    rsg.rearrange("p (b o) -> p b o", o=1).broadcast_to((128, NBK, 16)),
                    OP.mult)
    v.drain()
    v.tensor_scalar(Cq, A, 0.25, None, OP.is_ge)
    for mth in (0.75, 1.25, 1.75):
        v.drain()
        v.scalar_tensor_tensor(Cq, A, mth, Cq, OP.is_ge, OP.add)
    v.drain()
    v.tensor_scalar(Cq, Cq, 0.5, None, OP.mult)
    v.drain()
    v.scalar_tensor_tensor(Cq, A, 2.5, Cq, OP.is_ge, OP.add)
    v.drain()
    v.scalar_tensor_tensor(Cq, A, 3.5, Cq, OP.is_ge, OP.add)
    v.drain()
    v.tensor_scalar(A, A, 5.0, 2.0, OP.is_ge, OP.mult)
    v.drain()
    v.tensor_tensor(A, Cq, A, OP.add)
    v.drain()
    v.tensor_tensor(A, A, sg[:, 0:C], OP.mult)
    v.drain()
    v.tensor_tensor(u_out.rearrange("p (b k) -> p b k", k=16),
                    A.rearrange("p (b k) -> p b k", k=16),
                    sc8.rearrange("p (b o) -> p b o", o=1).broadcast_to((128, NBK, 16)),
                    OP.mult)
    v.drain()


def _emit_wquant(nc, p, stack, I, S, sb, cst, ps):
    C_LNDEC, C_NLNDEC, C_OS, C_SIN6, C_SIN, C_SIND, C_GWG, C_GWU, C_GWD, \
        C_SGA, C_SUA, C_SDA, C_LN2, C_SIN6D = range(14)
    WQ = [("g1_w", "uwg", HFF1, D, C_GWG),
          ("u1_w", "uwu", HFF1, D, C_GWU),
          ("d1_w", "uwd", D, HFF1, C_GWD)]

    wq0stack = contextlib.ExitStack()
    with nc.named_scope("wq0"), nc.Block("wq0") as blk:
        wt = wq0stack.enter_context(nc.sbuf_tensor("wq_a", [128, 2 * 4096], F32))
        mx = wq0stack.enter_context(nc.sbuf_tensor("wq_m", [128, 8], F32))
        sc1 = wq0stack.enter_context(nc.sbuf_tensor("wq_s", [1, 512], F32))
        nchunks = [R // 128 for (_, _, R, _, _) in WQ]

        @blk.sync
        def _(sync):
            p.gate(sync)
            i = 0
            for (src, dst, R, C, cg) in WQ:
                for o in range(0, R, 128):
                    if i >= 2:
                        p.wge(sync, "wqv", i - 1)
                    sync.dma_start(out=wt[:, (i % 2) * 4096:(i % 2) * 4096 + C],
                                   in_=I[src][o:o + 128, :]).then_inc(*p.inc("wqld%d" % (i % 2), 16))
                    i += 1

        @blk.vector
        def _(v):
            p.gate(v)
            i = 0
            for wi, (src, dst, R, C, cg) in enumerate(WQ):
                for j in range(R // 128):
                    p.wge(v, "wqld%d" % (i % 2), 16 * (i // 2 + 1))
                    b = (i % 2) * 4096
                    if j == 0:
                        v.tensor_reduce(mx[:, wi:wi + 1], wt[:, b:b + C], AX.X, OP.max,
                                        apply_absolute_value=True)
                    else:
                        v.tensor_reduce(mx[:, 4:5], wt[:, b:b + C], AX.X, OP.max,
                                        apply_absolute_value=True)
                        v.drain()
                        v.tensor_tensor(mx[:, wi:wi + 1], mx[:, wi:wi + 1], mx[:, 4:5], OP.max)
                    v.drain().then_inc(*p.inc("wqv", 1))
                    i += 1

        @blk.scalar
        def _(se):
            p.gate(se)
            p.wge(se, "wqv", sum(nchunks))
            se.dma_start(out=S["row"][2:3, 0:384].rearrange("o (c pp) -> pp (o c)", pp=128),
                         in_=mx[:, 0:3], allow_slow_non_contiguous=True).then_inc(*p.inc("wqs", 16))

        @blk.sync
        def _(sync):
            p.gate(sync)
            p.wge(sync, "wqs", 16)
            sync.dma_start(out=sc1[0:1, 0:384], in_=S["row"][2:3, 0:384]).then_inc(*p.inc("wqs2", 16))
            sync.dma_start(out=sc1[0:1, 400:401], in_=I["s_in"]).then_inc(*p.inc("wqs2", 16))
            sync.dma_start(out=sc1[0:1, 401:402], in_=I["s_in_down"]).then_inc(*p.inc("wqs2", 16))

        @blk.vector
        def _(v):
            p.gate(v)
            v.memset(sc1[0:1, 416:448], 0.0)
            v.drain()
            p.wge(v, "wqs2", 48)
            for wi in range(3):
                v.tensor_reduce(sc1[0:1, 416 + wi:417 + wi], sc1[0:1, wi * 128:(wi + 1) * 128],
                                AX.X, OP.max)
                v.drain()
            v.reciprocal(sc1[0:1, 420:423], sc1[0:1, 416:419])
            v.drain()
            v.tensor_scalar(sc1[0:1, 424:427], sc1[0:1, 420:423], 2688.0, None, OP.mult)
            v.drain()
            v.tensor_scalar(sc1[0:1, 428:431], sc1[0:1, 424:427], 1.0 / 6.0, None, OP.mult)
            v.drain()
            v.reciprocal(sc1[0:1, 432:434], sc1[0:1, 400:402])
            v.drain()
            v.tensor_scalar(sc1[0:1, 436:439], sc1[0:1, 416:419], 1.0 / 2688.0, None, OP.mult)
            v.drain()
            v.tensor_scalar(sc1[0:1, 436:438], sc1[0:1, 436:438], sc1[0:1, 432:433], None, OP.mult)
            v.drain()
            v.tensor_scalar(sc1[0:1, 438:439], sc1[0:1, 438:439], sc1[0:1, 433:434], None, OP.mult)
            v.drain()
            v.tensor_scalar(sc1[0:1, 440:441], sc1[0:1, 401:402], 1.0 / 6.0, None, OP.mult)
            v.drain().then_inc(*p.inc("wqv2", 1))

        @blk.scalar
        def _(se):
            p.gate(se)
            p.wge(se, "wqv2", 1)
            se.dma_start(out=S["row"][3:4, 416:448], in_=sc1[0:1, 416:448]).then_inc(*p.inc("wqs4", 16))

        @blk.sync
        def _(sync):
            p.gate(sync)
            p.wge(sync, "wqs4", 16)
            for (col, off) in [(C_GWG, 424), (C_GWU, 425), (C_GWD, 426),
                               (C_GWG + 16, 428), (C_GWU + 16, 429), (C_GWD + 16, 430),
                               (C_SGA, 436), (C_SUA, 437), (C_SDA, 438), (C_SIN6D, 440)]:
                sync.dma_start(out=cst[:, col:col + 1],
                               in_=S["row"][3:4, off:off + 1].partition_broadcast(128)).then_inc(*p.inc("wqb", 16))

    wq0stack.close()
    p.mark("wqscales", "wqb")
    p.endphase("wqscales")

    G = 4
    calls = []
    for (srcn, dstn, R, C, cg) in WQ:
        ch = [(srcn, dstn, o, C, cg) for o in range(0, R, 128)]
        for gs in range(0, len(ch), G):
            calls.append(ch[gs:gs + G])
    elad, cdone, exps, absg = {}, {}, {}, {}
    vc = sec = 0
    qglob = 0
    for g, grp in enumerate(calls):
        vc += 1
        elad[g] = vc
        sec += 1
        exps[g] = sec
        for k in range(len(grp)):
            sec += 1
            absg[qglob + k] = sec
            vc += 1
            cdone[qglob + k] = vc
        qglob += len(grp)
    ncalls = qglob

    with nc.named_scope("wq1"), nc.Block("wq1") as blk:
        with contextlib.ExitStack() as ph:
            xb = ph.enter_context(nc.sbuf_tensor("wq1_xb", [128, G * 4096], F32))
            ab = ph.enter_context(nc.sbuf_tensor("wq1_ab", [128, 4096], F32))
            sg = ph.enter_context(nc.sbuf_tensor("wq1_sg", [128, 4096], F32))
            qa = ph.enter_context(nc.sbuf_tensor("wq1_qa", [128, 4096], F32))
            qb = ph.enter_context(nc.sbuf_tensor("wq1_qb", [128, 4096], F32))
            qsl = ph.enter_context(nc.sbuf_tensor("wq1_qs", [128, 8 * G * 256], F32))
            uo = ph.enter_context(nc.sbuf_tensor("wq1_u", [128, 2 * 4096], BF16))

            @blk.sync
            def _(sync):
                p.gate(sync)
                p.wait(sync, "wqscales")
                q = 0
                for grp in calls:
                    for k, (srcn, dstn, o, C, cg) in enumerate(grp):
                        if q >= G:
                            p.wge(sync, "wq1se", absg[q - G])
                        sync.dma_start(out=xb[:, (q % G) * 4096:(q % G) * 4096 + C],
                                       in_=I[srcn][o:o + 128, :]).then_inc(*p.inc("wq1ld%d" % (q % 4), 16))
                        q += 1

            @blk.vector
            def _(v):
                p.gate(v)
                q = 0
                for g, grp in enumerate(calls):
                    C = grp[0][3]
                    NBK = C // 16
                    ng = len(grp)
                    cg = grp[0][4]
                    for k in range(ng):
                        p.wge(v, "wq1ld%d" % ((q + k) % 4), 16 * ((q + k) // 4 + 1))
                        _q_stageA(v, qsl, G, NBK, k, xb[:, ((q + k) % G) * 4096:((q + k) % G) * 4096 + C], C)
                    _q_stageB_v1(v, qsl, G, NBK, ng, cst[:, cg + 16:cg + 17])
                    p.inc("wq1v", 1)
                    v.drain().then_inc(p.s("wq1v"), 1)
                    p.wge(v, "wq1se", exps[g])
                    _q_stageB_v2(v, qsl, G, NBK, ng, cst[:, cg:cg + 1])
                    for k in range(ng):
                        p.wge(v, "wq1se", absg[q + k])
                        if q + k >= 2:
                            p.wge(v, "wq1st%d" % ((q + k) % 2), 16 * ((q + k) // 2))
                        _q_stageC_v(v, qsl, G, NBK, k, ab, sg, qa, qb,
                                    uo[:, ((q + k) % 2) * 4096:((q + k) % 2) * 4096 + C], C)
                        p.inc("wq1v", 1)
                        v.drain().then_inc(p.s("wq1v"), 1)
                    q += ng

            @blk.scalar
            def _(se):
                p.gate(se)
                q = 0
                for g, grp in enumerate(calls):
                    C = grp[0][3]
                    NBK = C // 16
                    ng = len(grp)
                    p.wge(se, "wq1v", elad[g])
                    _q_stageB_se(se, qsl, G, NBK, ng, cst[:, 14:15], cst[:, 15:16])
                    p.inc("wq1se", 1)
                    se.drain().then_inc(p.s("wq1se"), 1)
                    for k in range(ng):
                        if q + k >= 1:
                            p.wge(se, "wq1v", cdone[q + k - 1])
                        _q_stageC_se(se, xb[:, ((q + k) % G) * 4096:((q + k) % G) * 4096 + C], ab, sg, C)
                        p.inc("wq1se", 1)
                        se.drain().then_inc(p.s("wq1se"), 1)
                    q += ng

            @blk.gpsimd
            def _(g_):
                p.gate(g_)
                q = 0
                for grp in calls:
                    for (srcn, dstn, o, C, cg) in grp:
                        p.wge(g_, "wq1v", cdone[q])
                        g_.dma_start(out=S[dstn][o:o + 128, :],
                                     in_=uo[:, (q % 2) * 4096:(q % 2) * 4096 + C]).then_inc(*p.inc("wq1st%d" % (q % 2), 16))
                        q += 1

    p.mark("wq_a", "wq1st0")
    p.mark("wq_b", "wq1st1")
    p.endphase("wq_a", "wq_b")


def _emit_attn(nc, p, stack, I, S, sb, ps, cst, b_tile, a_mat, wts, W, NB, C_OS):
    qtiles = []
    t0 = 0
    while t0 < QW:
        qt = min(512, QW - t0)
        qtiles.append((t0, qt))
        t0 += qt

    for qi, (t0, qt) in enumerate(qtiles):
        NBq = (W + qt) // 128
        NHG = (NBq + 3) // 4
        ntc = qt // 128
        tag = "at%d" % qi
        with nc.named_scope(tag), nc.Block(tag) as blk:
            with contextlib.ExitStack() as ph:
                sw_hi = ph.enter_context(nc.sbuf_tensor(tag + "_swh", [128, NBq * 512], BF16))
                sw_lo = ph.enter_context(nc.sbuf_tensor(tag + "_swl", [128, NBq * 512], BF16))
                kf = [[ph.enter_context(nc.sbuf_tensor(tag + "_kf%d%s" % (i, h), [128, 8 * 512], BF16))
                       for h in "hl"] for i in range(2)]
                qpf = [ph.enter_context(nc.sbuf_tensor(tag + "_qp%s" % h, [128, 8 * 512], BF16))
                       for h in "hl"]
                vb = [ph.enter_context(nc.sbuf_tensor(tag + "_vb%s" % h, [128, NBq * 512], BF16))
                      for h in "hl"]
                sw32 = ph.enter_context(nc.sbuf_tensor(tag + "_s32", [128, 512], F32))
                xre = ph.enter_context(nc.sbuf_tensor(tag + "_xre", [128, 2 * 512], F32))
                x1t = ph.enter_context(nc.sbuf_tensor(tag + "_x1t", [128, 2 * 512], F32))

                KLD, QLD, VLD, SC, SWP, SW, RT, EV, XLD, ST = [tag + s for s in
                    ("kld", "qld", "vld", "sc", "swp", "sw", "rt", "ev", "xld", "st")]

                @blk.sync
                def _(sync):
                    p.gate(sync)
                    if qi == 0:
                        p.wait(sync, "ln0_a")
                        p.wait(sync, "ln0_b")
                        p.wait(sync, "c0")
                    else:
                        p.wait(sync, "at%d_a" % (qi - 1))
                        p.wait(sync, "at%d_b" % (qi - 1))
                    for hn, dst in (("qp_hi", qpf[0]), ("qp_lo", qpf[1])):
                        for vc in range(8):
                            sync.dma_start(out=dst[:, vc * 512:vc * 512 + qt],
                                           in_=S[hn][t0:t0 + qt, vc * 128:(vc + 1) * 128],
                                           transpose=True).then_inc(*p.inc(QLD, 16))
                    for hg in range(NHG):
                        if hg >= 2:
                            p.wge(sync, SC, 4 * (hg - 1))
                        s0 = t0 + 512 * hg
                        nsb = min(512, (t0 + NBq * 128) - s0)
                        for hn, dst in (("xn_hi", kf[hg % 2][0]), ("xn_lo", kf[hg % 2][1])):
                            for vc in range(8):
                                sync.dma_start(out=dst[:, vc * 512:vc * 512 + nsb],
                                               in_=S[hn][s0:s0 + nsb, vc * 128:(vc + 1) * 128],
                                               transpose=True).then_inc(*p.inc(KLD + str(hg % 2), 16))
                    for vh in range(2):
                        if vh == 1:
                            p.wge(sync, RT, ntc)
                        for hn, dst in (("xn_hi", vb[0]), ("xn_lo", vb[1])):
                            sync.dma_start(
                                out=dst[:, :].rearrange("p (j v) -> p j v", v=512),
                                in_=S[hn][t0:t0 + NBq * 128, vh * 512:(vh + 1) * 512]
                                    .rearrange("(j p) v -> p j v", p=128),
                            ).then_inc(*p.inc(VLD, 16))
                        for tc in range(ntc):
                            idx = vh * ntc + tc
                            if idx >= 2:
                                p.wge(sync, EV, idx - 1)
                            sync.dma_start(out=xre[:, (idx % 2) * 512:(idx % 2) * 512 + 512],
                                           in_=I["x"][t0 + tc * 128:t0 + tc * 128 + 128,
                                                      vh * 512:(vh + 1) * 512]).then_inc(*p.inc(XLD + str(idx % 2), 16))

                @blk.tensor
                def _(tensor):
                    p.gate(tensor)
                    for j in range(NBq):
                        hg, jj = j // 4, j % 4
                        p.wge(tensor, KLD + str(hg % 2), 16 * 16 * (hg // 2 + 1))
                        if j == 0:
                            p.wge(tensor, QLD, 16 * 16)
                        if j >= 2:
                            p.wge(tensor, SWP, j - 1)
                        pb = ps[j % 2]
                        nmm = 0
                        for (kb, qb) in ((0, 0), (0, 1), (1, 0)):
                            for vc in range(8):
                                nmm += 1
                                mm = tensor.matmul(pb[:, 0:qt],
                                                   kf[hg % 2][kb][:, vc * 512 + jj * 128:vc * 512 + jj * 128 + 128],
                                                   qpf[qb][:, vc * 512:vc * 512 + qt],
                                                   start=(nmm == 1), stop=(nmm == 24))
                        mm.then_inc(*p.inc(SC, 1))
                    for vh in range(2):
                        p.wge(tensor, VLD, 32 * (vh + 1))
                        for tc in range(ntc):
                            if vh == 1:
                                p.wge(tensor, EV, ntc + tc)
                            pr = ps[2 + tc]
                            nmm = 0
                            nj = min(tc + W // 128 + 1, NBq)
                            if vh == 0:
                                p.wge(tensor, SW, nj)
                            for j in range(tc, nj):
                                for (sb_, vbb) in ((0, 0), (0, 1), (1, 0)):
                                    nmm += 1
                                    src_sw = sw_hi if sb_ == 0 else sw_lo
                                    mm = tensor.matmul(
                                        pr[:, 0:512],
                                        src_sw[:, j * 512 + tc * 128:j * 512 + tc * 128 + 128],
                                        vb[vbb][:, j * 512:(j + 1) * 512],
                                        start=(nmm == 1), stop=(nmm == 3 * (nj - tc)))
                            mm.then_inc(*p.inc(RT, 1))

                @blk.vector
                def _(v):
                    p.gate(v)
                    if qi == 0:
                        p.wait(v, "c0")
                    for j in range(NBq):
                        p.wge(v, SC, j + 1)
                        pb = ps[j % 2]
                        if j <= 3:
                            v.tensor_tensor(sw32[:, 0:qt], pb[:, 0:qt], wts[j][:, 0:qt], OP.mult)
                        else:
                            v.scalar_tensor_tensor(sw32[:, 0:qt], pb[:, 0:qt], a_mat[:, j:j + 1],
                                                   b_tile[:, 0:qt], OP.mult, OP.mult)
                        p.inc(SWP, 1)
                        v.drain().then_inc(p.s(SWP), 1)
                        v.tensor_copy(sw_hi[:, j * 512:j * 512 + qt], sw32[:, 0:qt])
                        v.drain()
                        v.tensor_tensor(sw_lo[:, j * 512:j * 512 + qt], sw32[:, 0:qt],
                                        sw_hi[:, j * 512:j * 512 + qt], OP.subtract)
                        p.inc(SW, 1)
                        v.drain().then_inc(p.s(SW), 1)
                    for vh in range(2):
                        for tc in range(ntc):
                            idx = vh * ntc + tc
                            p.wge(v, RT, idx + 1)
                            p.wge(v, XLD + str(idx % 2), 16 * (idx // 2 + 1))
                            if idx >= 2:
                                p.wge(v, ST + str(idx % 2), 16 * (idx // 2))
                            v.scalar_tensor_tensor(x1t[:, (idx % 2) * 512:(idx % 2) * 512 + 512],
                                                   ps[2 + tc][:, 0:512], cst[:, C_OS:C_OS + 1],
                                                   xre[:, (idx % 2) * 512:(idx % 2) * 512 + 512],
                                                   OP.mult, OP.add)
                            p.inc(EV, 1)
                            v.drain().then_inc(p.s(EV), 1)

                @blk.scalar
                def _(se):
                    p.gate(se)
                    for i in range(2 * ntc):
                        p.wge(se, EV, i + 1)
                        vh, tc = i // ntc, i % ntc
                        se.dma_start(out=S["x1"][t0 + tc * 128:t0 + tc * 128 + 128,
                                                 vh * 512:(vh + 1) * 512],
                                     in_=x1t[:, (i % 2) * 512:(i % 2) * 512 + 512]
                                     ).then_inc(*p.inc(ST + str(i % 2), 16))

        p.mark("at%d_a" % qi, ST + "0")
        p.mark("at%d_b" % qi, ST + "1")
        p.endphase("at%d_a" % qi, "at%d_b" % qi)
    p.mark("p2a", "at%dst0" % (len(qtiles) - 1))
    p.mark("p2b", "at%dst1" % (len(qtiles) - 1))


def _emit_ffn0(nc, p, stack, I, S, sb, ps, cst):
    with nc.named_scope("f0a"), nc.Block("f0a") as blk:
        with contextlib.ExitStack() as ph:
            xnf = [ph.enter_context(nc.sbuf_tensor("f0_xn%s" % h, [128, 8 * QW], BF16)) for h in "hl"]
            wgf = [ph.enter_context(nc.sbuf_tensor("f0_wg%d" % i, [128, 8 * 512], BF16)) for i in range(2)]
            wuf = [ph.enter_context(nc.sbuf_tensor("f0_wu%d" % i, [128, 8 * 512], BF16)) for i in range(2)]
            sg = ph.enter_context(nc.sbuf_tensor("f0_sg", [128, 2 * 512], F32))
            hh = ph.enter_context(nc.sbuf_tensor("f0_hh", [128, 2 * 512], F32))
            hhh = ph.enter_context(nc.sbuf_tensor("f0_hhh", [128, 2 * 512], BF16))
            hhl = ph.enter_context(nc.sbuf_tensor("f0_hhl", [128, 2 * 512], BF16))
            NHS = HFF0 // 512

            @blk.sync
            def _(sync):
                p.gate(sync)
                p.wait(sync, "p2a")
                p.wait(sync, "p2b")
                p.wait(sync, "w0c")
                for hn, dst in (("xn_hi", xnf[0]), ("xn_lo", xnf[1])):
                    for vc in range(8):
                        sync.dma_start(out=dst[:, vc * QW:(vc + 1) * QW],
                                       in_=S[hn][0:QW, vc * 128:(vc + 1) * 128],
                                       transpose=True).then_inc(*p.inc("f0xld", 16))
                for hs_ in range(NHS):
                    if hs_ >= 2:
                        p.wge(sync, "f0mm", 2 * NTC * (hs_ - 1))
                    for wn, dst in (("wg_b", wgf[hs_ % 2]), ("wu_b", wuf[hs_ % 2])):
                        for vc in range(8):
                            sync.dma_start(out=dst[:, vc * 512:(vc + 1) * 512],
                                           in_=S[wn][hs_ * 512:(hs_ + 1) * 512, vc * 128:(vc + 1) * 128],
                                           transpose=True).then_inc(*p.inc("f0wld%d" % (hs_ % 2), 16))

            @blk.tensor
            def _(tensor):
                p.gate(tensor)
                p.wge(tensor, "f0xld", 16 * 16)
                for hs_ in range(NHS):
                    p.wge(tensor, "f0wld%d" % (hs_ % 2), 16 * 16 * (hs_ // 2 + 1))
                    for tc in range(NTC):
                        idx = hs_ * NTC + tc
                        if idx >= 2:
                            p.wge(tensor, "f0hv", idx - 1)
                            p.wge(tensor, "f0se", idx - 1)
                        pg, pu = ps[2 * (idx % 2)], ps[2 * (idx % 2) + 1]
                        for (dst, wf) in ((pg, wgf[hs_ % 2]), (pu, wuf[hs_ % 2])):
                            nmm = 0
                            for hb in range(2):
                                xf = xnf[hb]
                                for vc in range(8):
                                    nmm += 1
                                    mm = tensor.matmul(dst[:, 0:512],
                                                       xf[:, vc * QW + tc * 128:vc * QW + tc * 128 + 128],
                                                       wf[:, vc * 512:(vc + 1) * 512],
                                                       start=(nmm == 1), stop=(nmm == 16))
                            mm.then_inc(*p.inc("f0mm", 1))

            @blk.scalar
            def _(se):
                p.gate(se)
                for hs_ in range(NHS):
                    for tc in range(NTC):
                        idx = hs_ * NTC + tc
                        p.wge(se, "f0mm", 2 * idx + 1)
                        se.activation(out=sg[:, (idx % 2) * 512:(idx % 2) * 512 + 512],
                                      in_=ps[2 * (idx % 2)][:, 0:512], func=AF.Silu,
                                      scale=FF0S)
                        se.drain().then_inc(p.s("f0se"), 1)
                        p.inc("f0se", 1)
                        p.wge(se, "f0hv", idx + 1)
                        se.dma_start(out=S["hh_hi"][tc * 128:tc * 128 + 128, hs_ * 512:(hs_ + 1) * 512],
                                     in_=hhh[:, (idx % 2) * 512:(idx % 2) * 512 + 512]).then_inc(*p.inc("f0st%d" % (idx % 2), 16))
                        se.dma_start(out=S["hh_lo"][tc * 128:tc * 128 + 128, hs_ * 512:(hs_ + 1) * 512],
                                     in_=hhl[:, (idx % 2) * 512:(idx % 2) * 512 + 512]).then_inc(*p.inc("f0st%d" % (idx % 2), 16))

            @blk.vector
            def _(v):
                p.gate(v)
                for hs_ in range(NHS):
                    for tc in range(NTC):
                        idx = hs_ * NTC + tc
                        p.wge(v, "f0mm", 2 * idx + 2)
                        p.wge(v, "f0se", idx + 1)
                        if idx >= 2:
                            p.wge(v, "f0st%d" % (idx % 2), 32 * (idx // 2))
                        b = (idx % 2) * 512
                        v.scalar_tensor_tensor(hh[:, b:b + 512], ps[2 * (idx % 2) + 1][:, 0:512],
                                               FF0S, sg[:, b:b + 512], OP.mult, OP.mult)
                        v.drain()
                        v.tensor_copy(hhh[:, b:b + 512], hh[:, b:b + 512])
                        v.drain()
                        v.tensor_tensor(hhl[:, b:b + 512], hh[:, b:b + 512], hhh[:, b:b + 512],
                                        OP.subtract)
                        p.inc("f0hv", 1)
                        v.drain().then_inc(p.s("f0hv"), 1)

    p.mark("f0hh_a", "f0st0")
    p.mark("f0hh_b", "f0st1")
    p.endphase("f0hh_a", "f0hh_b")

    with nc.named_scope("f0b"), nc.Block("f0b") as blk:
        with contextlib.ExitStack() as ph:
            hhf = [[ph.enter_context(nc.sbuf_tensor("f0b_h%d%s" % (i, h), [128, QW], BF16))
                    for h in "hl"] for i in range(12)]
            wdf = [ph.enter_context(nc.sbuf_tensor("f0b_w%d" % i, [128, 1024], BF16))
                   for i in range(12)]
            x1t = ph.enter_context(nc.sbuf_tensor("f0b_x1", [128, 2 * 512], F32))
            x1o = ph.enter_context(nc.sbuf_tensor("f0b_x1o", [128, 2 * 512], F32))

            for half in range(2):
                HB = half * 12
                tagl, tagm, tage, tags = ("f0b%dld" % half, "f0b%dmm" % half,
                                          "f0b%dev" % half, "f0b%dst" % half)

                @blk.sync
                def _(sync, half=half, HB=HB, tagl=tagl, tags=tags):
                    p.gate(sync)
                    if half == 0:
                        p.wait(sync, "f0hh_a")
                        p.wait(sync, "f0hh_b")
                    else:
                        p.wge(sync, "f0b0st0", 16 * NTC)
                        p.wge(sync, "f0b0st1", 16 * NTC)
                    for i in range(12):
                        hc = HB + i
                        for hn, dst in (("hh_hi", hhf[i][0]), ("hh_lo", hhf[i][1])):
                            sync.dma_start(out=dst[:, 0:QW],
                                           in_=S[hn][0:QW, hc * 128:(hc + 1) * 128],
                                           transpose=True).then_inc(*p.inc(tagl, 16))
                        sync.dma_start(out=wdf[i][:, :],
                                       in_=S["wd_b"][0:D, hc * 128:(hc + 1) * 128],
                                       transpose=True).then_inc(*p.inc(tagl, 16))
                    for tc in range(NTC):
                        for dh in range(2):
                            idx = tc * 2 + dh
                            if idx >= 2:
                                p.wge(sync, tage, idx - 1)
                            sync.dma_start(out=x1t[:, (idx % 2) * 512:(idx % 2) * 512 + 512],
                                           in_=S["x1"][tc * 128:tc * 128 + 128,
                                                       dh * 512:(dh + 1) * 512]).then_inc(*p.inc(tagl + "x%d" % (idx % 2), 16))

                @blk.tensor
                def _(tensor, half=half, tagl=tagl, tagm=tagm, tage=tage):
                    p.gate(tensor)
                    p.wge(tensor, tagl, 16 * 36)
                    for tc in range(NTC):
                        for dh in range(2):
                            idx = tc * 2 + dh
                            if idx >= 4:
                                p.wge(tensor, tage, idx - 3)
                            pr = ps[4 + (idx % 4)]
                            nmm = 0
                            for i in range(12):
                                for hb in range(2):
                                    nmm += 1
                                    mm = tensor.matmul(pr[:, 0:512],
                                                       hhf[i][hb][:, tc * 128:tc * 128 + 128],
                                                       wdf[i][:, dh * 512:(dh + 1) * 512],
                                                       start=(nmm == 1), stop=(nmm == 24))
                            mm.then_inc(*p.inc(tagm, 1))

                @blk.vector
                def _(v, tagm=tagm, tage=tage, tagl=tagl, tags=tags):
                    p.gate(v)
                    for tc in range(NTC):
                        for dh in range(2):
                            idx = tc * 2 + dh
                            p.wge(v, tagm, idx + 1)
                            p.wge(v, tagl + "x%d" % (idx % 2), 16 * (idx // 2 + 1))
                            if idx >= 2:
                                p.wge(v, tags + str(idx % 2), 16 * (idx // 2))
                            b = (idx % 2) * 512
                            v.scalar_tensor_tensor(x1o[:, b:b + 512], ps[4 + (idx % 4)][:, 0:512],
                                                   FF0S, x1t[:, b:b + 512], OP.mult, OP.add)
                            p.inc(tage, 1)
                            v.drain().then_inc(p.s(tage), 1)

                @blk.scalar
                def _(se, tage=tage, tags=tags, half=half):
                    p.gate(se)
                    for tc in range(NTC):
                        for dh in range(2):
                            idx = tc * 2 + dh
                            p.wge(se, tage, idx + 1)
                            se.dma_start(out=S["x1"][tc * 128:tc * 128 + 128, dh * 512:(dh + 1) * 512],
                                         in_=x1o[:, (idx % 2) * 512:(idx % 2) * 512 + 512]
                                         ).then_inc(*p.inc("f0b%dst%d" % (half, idx % 2), 16))

            p.mark("f0bh%d_a" % half, "f0b%dst0" % half)
            p.mark("f0bh%d_b" % half, "f0b%dst1" % half)
            p.endphase("f0bh%d_a" % half, "f0bh%d_b" % half)

    p.mark("p3_a", "f0b1st0")
    p.mark("p3_b", "f0b1st1")
    p.endphase("p3_a", "p3_b")


MF_TT = [(0, 512), (512, 512), (1024, 512), (1536, 512), (2048, 384)]


def _emit_mamba_front(nc, p, stack, I, S, sb, ps, cst, convw, convb, bdtn):
    with nc.named_scope("mfa"), nc.Block("mfa") as blk:
        with contextlib.ExitStack() as ph:
            xnf = [ph.enter_context(nc.sbuf_tensor("mf_xn%s" % h, [128, 8 * QW], BF16)) for h in "hl"]
            wif = [[ph.enter_context(nc.sbuf_tensor("mf_wi%d%s" % (i, h), [128, 8 * 512], BF16))
                    for h in "hl"] for i in range(2)]
            xcp = [ph.enter_context(nc.sbuf_tensor("mf_xcp%d" % i, [128, 8 + QW], F32)) for i in range(4)]
            xct = ph.enter_context(nc.sbuf_tensor("mf_xct", [128, QW], F32))
            xch = ph.enter_context(nc.sbuf_tensor("mf_xch", [128, QW], BF16))
            xcl = ph.enter_context(nc.sbuf_tensor("mf_xcl", [128, QW], BF16))
            szt = ph.enter_context(nc.sbuf_tensor("mf_sz", [128, 2 * 512], F32))
            umb = ph.enter_context(nc.sbuf_tensor("mf_um", [128, QW], F32))

            @blk.sync
            def _(sync):
                p.gate(sync)
                p.wait(sync, "ln1_a")
                p.wait(sync, "ln1_b")
                p.wait(sync, "w0_a")
                p.wait(sync, "w0_b")
                sync.dma_start(out=umb[:, :],
                               in_=I["umask"][0:1, :].partition_broadcast(128)).then_inc(*p.inc("mfum", 16))
                for hn, dst in (("xn1_hi", xnf[0]), ("xn1_lo", xnf[1])):
                    for vc in range(8):
                        sync.dma_start(out=dst[:, vc * QW:(vc + 1) * QW],
                                       in_=S[hn][0:QW, vc * 128:(vc + 1) * 128],
                                       transpose=True).then_inc(*p.inc("mfxld", 16))
                for grp in range(8):
                    if grp >= 2:
                        p.wge(sync, "mfmm", 20 * (grp - 1))
                    for hn, dst in (("win_hi", wif[grp % 2][0]), ("win_lo", wif[grp % 2][1])):
                        for vc in range(8):
                            sync.dma_start(out=dst[:, vc * 512:(vc + 1) * 512],
                                           in_=S[hn][grp * 512:(grp + 1) * 512, vc * 128:(vc + 1) * 128],
                                           transpose=True).then_inc(*p.inc("mfwld%d" % (grp % 2), 16))

            @blk.tensor
            def _(tensor):
                p.gate(tensor)
                p.wge(tensor, "mfxld", 16 * 16)
                for grp in range(8):
                    p.wge(tensor, "mfwld%d" % (grp % 2), 16 * 16 * (grp // 2 + 1))
                    for gi in range(4):
                        for ti, (tt0, tl) in enumerate(MF_TT):
                            idx = (grp * 4 + gi) * 5 + ti
                            if idx >= 4:
                                p.wge(tensor, "mfev", idx - 3)
                            pb = ps[idx % 4]
                            nmm = 0
                            for (kb, qb) in ((0, 0), (0, 1), (1, 0)):
                                for vc in range(8):
                                    nmm += 1
                                    mm = tensor.matmul(
                                        pb[:, 0:tl],
                                        wif[grp % 2][kb][:, vc * 512 + gi * 128:vc * 512 + gi * 128 + 128],
                                        xnf[qb][:, vc * QW + tt0:vc * QW + tt0 + tl],
                                        start=(nmm == 1), stop=(nmm == 24))
                            mm.then_inc(*p.inc("mfmm", 1))

            @blk.vector
            def _(v):
                p.gate(v)
                p.wge(v, "mfum", 16)
                for grp in range(8):
                    for gi in range(4):
                        dc = grp * 4 + gi
                        is_xc = dc < 16
                        if is_xc:
                            v.memset(xcp[gi][:, 0:8], 0.0)
                        for ti, (tt0, tl) in enumerate(MF_TT):
                            idx = dc * 5 + ti
                            p.wge(v, "mfmm", idx + 1)
                            if is_xc:
                                v.drain()
                                v.tensor_copy(xcp[gi][:, 3 + tt0:3 + tt0 + tl], ps[idx % 4][:, 0:tl])
                                p.inc("mfev", 1)
                                v.drain().then_inc(p.s("mfev"), 1)
                            else:
                                p.wge(v, "mfsz", (dc - 16) * 5 + ti + 1)
                                p.inc("mfev", 1)
                                v.nop().then_inc(p.s("mfev"), 1)
                        if is_xc:
                            p.wge(v, "mfst", 48 * dc)
                            v.tensor_scalar(xct[:, :], xcp[gi][:, 0:QW], convw[:, dc * 4:dc * 4 + 1],
                                            None, OP.mult)
                            for k in range(1, 4):
                                v.drain()
                                v.scalar_tensor_tensor(xct[:, :], xcp[gi][:, k:k + QW],
                                                       convw[:, dc * 4 + k:dc * 4 + k + 1],
                                                       xct[:, :], OP.mult, OP.add)
                            p.inc("mfcv", 1)
                            v.drain().then_inc(p.s("mfcv"), 1)
                            p.wge(v, "mfsil", dc + 1)
                            v.tensor_tensor(xct[:, :], xct[:, :], umb[:, :], OP.mult)
                            v.drain()
                            v.tensor_copy(xch[:, :], xct[:, :])
                            v.drain()
                            v.tensor_tensor(xcl[:, :], xct[:, :], xch[:, :], OP.subtract)
                            p.inc("mfxc", 1)
                            v.drain().then_inc(p.s("mfxc"), 1)

            @blk.scalar
            def _(se):
                p.gate(se)
                nsz = 0
                for grp in range(8):
                    for gi in range(4):
                        dc = grp * 4 + gi
                        is_xc = dc < 16
                        if is_xc:
                            p.wge(se, "mfcv", dc + 1)
                            se.activation(out=xct[:, :], in_=xct[:, :], func=AF.Silu,
                                          bias=convb[:, dc:dc + 1])
                            se.drain().then_inc(p.s("mfsil"), 1)
                            p.inc("mfsil", 1)
                            p.wge(se, "mfxc", dc + 1)
                            se.dma_start(out=S["xc"][dc * 128:(dc + 1) * 128, :],
                                         in_=xct[:, :]).then_inc(*p.inc("mfst", 16))
                            se.dma_start(out=S["xc_hi"][dc * 128:(dc + 1) * 128, :],
                                         in_=xch[:, :]).then_inc(*p.inc("mfst", 16))
                            se.dma_start(out=S["xc_lo"][dc * 128:(dc + 1) * 128, :],
                                         in_=xcl[:, :]).then_inc(*p.inc("mfst", 16))
                        else:
                            zc = dc - 16
                            for ti, (tt0, tl) in enumerate(MF_TT):
                                idx = dc * 5 + ti
                                o0, o1 = max(tt0, WARM), tt0 + tl
                                p.wge(se, "mfmm", idx + 1)
                                if o1 > o0:
                                    b = (nsz % 2) * 512
                                    if nsz >= 2:
                                        p.wge(se, "mfzst%d" % (nsz % 2), 16 * (nsz // 2))
                                    se.activation(out=szt[:, b:b + (o1 - o0)],
                                                  in_=ps[idx % 4][:, o0 - tt0:o1 - tt0], func=AF.Silu)
                                    se.drain()
                                    se.dma_start(out=S["siluz"][zc * 128:(zc + 1) * 128, o0 - WARM:o1 - WARM],
                                                 in_=szt[:, b:b + (o1 - o0)]).then_inc(*p.inc("mfzst%d" % (nsz % 2), 16))
                                    nsz += 1
                                se.drain().then_inc(p.s("mfsz"), 1)
                                p.inc("mfsz", 1)

    p.mark("mfa", "mfst")
    p.mark("mfaz", "mfzst0")
    p.mark("mfaz2", "mfzst1")
    p.endphase("mfa", "mfaz", "mfaz2")

    ph2 = contextlib.ExitStack()
    drh = ph2.enter_context(nc.sbuf_tensor("mf_drh", [64, QW], BF16))
    drl = ph2.enter_context(nc.sbuf_tensor("mf_drl", [64, QW], BF16))
    with nc.named_scope("mfb"), nc.Block("mfb") as blk:
        with contextlib.ExitStack() as ph:
            wxf = [ph.enter_context(nc.sbuf_tensor("mf_wx%s" % h, [128, 16 * 96], BF16)) for h in "hl"]
            xcf = [[ph.enter_context(nc.sbuf_tensor("mf_xcf%d%s" % (i, h), [128, QW], BF16))
                    for h in "hl"] for i in range(2)]
            dr3 = ph.enter_context(nc.sbuf_tensor("mf_dr3", [96, QW], F32))

            @blk.sync
            def _(sync):
                p.gate(sync)
                p.wait(sync, "mfa")
                for hn, dst in (("wx_hi", wxf[0]), ("wx_lo", wxf[1])):
                    for dc in range(16):
                        sync.dma_start(out=dst[:, dc * 96:(dc + 1) * 96],
                                       in_=S[hn][dc * 128:(dc + 1) * 128, 0:96]
                                       ).then_inc(*p.inc("mbwld", 16))
                for dc in range(16):
                    if dc >= 2:
                        p.wge(sync, "mbmm", 5 * (dc - 1))
                    for hn, dst in (("xc_hi", xcf[dc % 2][0]), ("xc_lo", xcf[dc % 2][1])):
                        sync.dma_start(out=dst[:, :],
                                       in_=S[hn][dc * 128:(dc + 1) * 128, :]).then_inc(*p.inc("mbxld%d" % (dc % 2), 16))

            @blk.tensor
            def _(tensor):
                p.gate(tensor)
                p.wge(tensor, "mbwld", 16 * 32)
                for dc in range(16):
                    p.wge(tensor, "mbxld%d" % (dc % 2), 32 * (dc // 2 + 1))
                    for ti, (tt0, tl) in enumerate(MF_TT):
                        nmm = 0
                        for (kb, qb) in ((0, 0), (0, 1), (1, 0)):
                            nmm += 1
                            mm = tensor.matmul(ps[ti][0:96, 0:tl],
                                               wxf[kb][:, dc * 96:(dc + 1) * 96],
                                               xcf[dc % 2][qb][:, tt0:tt0 + tl],
                                               start=(dc == 0 and nmm == 1),
                                               stop=(dc == 15 and nmm == 3))
                        mm.then_inc(*p.inc("mbmm", 1))

            @blk.vector
            def _(v):
                p.gate(v)
                p.wge(v, "mbmm", 5 * 16)
                for ti, (tt0, tl) in enumerate(MF_TT):
                    v.tensor_copy(dr3[0:96, tt0:tt0 + tl], ps[ti][0:96, 0:tl])
                v.drain()
                v.tensor_copy(drh[:, :], dr3[0:64, :])
                v.drain()
                v.tensor_tensor(drl[:, :], dr3[0:64, :], drh[:, :], OP.subtract)
                v.drain()
                v.tensor_scalar(dr3[64:80, :], dr3[64:80, :], -1.0, None, OP.mult)
                v.drain().then_inc(p.s("mbev"), 1)
                p.inc("mbev", 1)

            @blk.scalar
            def _(se):
                p.gate(se)
                p.wge(se, "mbev", 1)
                se.dma_start(out=S["bc"][0:32, :], in_=dr3[64:96, :]).then_inc(*p.inc("mbst", 16))

    p.mark("mfb", "mbst")
    p.endphase("mfb")

    with nc.named_scope("mfc"), nc.Block("mfc") as blk:
        with contextlib.ExitStack() as ph:
            wdtf = [ph.enter_context(nc.sbuf_tensor("mf_wdt%s" % h, [64, 16 * 128], BF16)) for h in "hl"]
            sgt = [ph.enter_context(nc.sbuf_tensor("mf_sgt%d" % i, [128, 512], F32)) for i in range(2)]
            lnt = [ph.enter_context(nc.sbuf_tensor("mf_lnt%d" % i, [128, 512], F32)) for i in range(2)]

            @blk.sync
            def _(sync):
                p.gate(sync)
                p.wait(sync, "mfb")
                for hn, dst in (("wdt_hi", wdtf[0]), ("wdt_lo", wdtf[1])):
                    sync.dma_start(out=dst[:, 0:DIN],
                                   in_=S[hn][0:DTR, 0:DIN]).then_inc(*p.inc("mcwld", 16))

            @blk.tensor
            def _(tensor):
                p.gate(tensor)
                p.wge(tensor, "mcwld", 16 * 2)
                p.wait(tensor, "mfb")
                for dc in range(16):
                    for ti, (tt0, tl) in enumerate(MF_TT):
                        idx = dc * 5 + ti
                        if idx >= 2:
                            p.wge(tensor, "mcsg", idx - 1)
                        pb = ps[5 + idx % 2]
                        nmm = 0
                        for (kb, qb) in ((0, 0), (0, 1), (1, 0)):
                            nmm += 1
                            drq = drh if qb == 0 else drl
                            mm = tensor.matmul(pb[:, 0:tl],
                                               wdtf[kb][0:64, dc * 128:(dc + 1) * 128],
                                               drq[0:64, tt0:tt0 + tl],
                                               start=(nmm == 1), stop=(nmm == 3))
                        mm.then_inc(*p.inc("mcmm", 1))

            @blk.scalar
            def _(se):
                p.gate(se)
                for dc in range(16):
                    for ti, (tt0, tl) in enumerate(MF_TT):
                        idx = dc * 5 + ti
                        p.wge(se, "mcmm", idx + 1)
                        b = idx % 2
                        se.activation(out=sgt[b][:, 0:tl], in_=ps[5 + idx % 2][:, 0:tl],
                                      func=AF.Sigmoid, scale=-1.0, bias=bdtn[:, dc:dc + 1])
                        se.drain().then_inc(p.s("mcsg"), 1)
                        p.inc("mcsg", 1)
                        p.wge(se, "mcgv", idx + 1)
                        if idx >= 2:
                            p.wge(se, "mcst%d" % (idx % 2), 16 * (idx // 2))
                        se.activation(out=lnt[b][:, 0:tl], in_=sgt[b][:, 0:tl], func=AF.Ln)
                        se.drain().then_inc(p.s("mcln"), 1)
                        p.inc("mcln", 1)
                        p.wge(se, "mcln", idx + 1)
                        se.dma_start(out=S["lns"][dc * 128:(dc + 1) * 128, tt0:tt0 + tl],
                                     in_=lnt[b][:, 0:tl]).then_inc(*p.inc("mcst%d" % (idx % 2), 16))

            @blk.vector
            def _(v):
                p.gate(v)
                for dc in range(16):
                    for ti, (tt0, tl) in enumerate(MF_TT):
                        idx = dc * 5 + ti
                        b = idx % 2
                        p.wge(v, "mcsg", idx + 1)
                        v.tensor_scalar(sgt[b][:, 0:tl], sgt[b][:, 0:tl], 1e-38, None, OP.max)
                        p.inc("mcgv", 1)
                        v.drain().then_inc(p.s("mcgv"), 1)

    ph2.close()
    p.mark("mfc_a", "mcst0")
    p.mark("mfc_b", "mcst1")
    p.endphase("mfc_a", "mfc_b", "mfaz", "mfaz2")


CUBE_SEGS = [(0, 608), (608, 608), (1216, 608), (1824, 608)]
CUBE_SL = 608


def _cseg(j):
    si = j // 16
    s0, sl = CUBE_SEGS[si]
    o0, o1 = max(s0, WARM), s0 + sl
    return si, s0, sl, o0, o1 - o0


def _emit_cube(nc, p, stack, I, S, sb, ps, cst, ealog, state, dpc):
    SL = CUBE_SL
    NG = 16 * len(CUBE_SEGS)
    with nc.named_scope("cb"), nc.Block("cb") as blk:
        with contextlib.ExitStack() as ph:
            Bb = ph.enter_context(nc.sbuf_tensor("cb_B", [128, NST * SL], F32))
            Cb = ph.enter_context(nc.sbuf_tensor("cb_C", [128, NST * SL], F32))
            lnst = [ph.enter_context(nc.sbuf_tensor("cb_ln%d" % i, [128, SL], F32)) for i in range(3)]
            xct = [ph.enter_context(nc.sbuf_tensor("cb_xc%d" % i, [128, SL], F32)) for i in range(3)]
            szt = [ph.enter_context(nc.sbuf_tensor("cb_sz%d" % i, [128, SL], F32)) for i in range(3)]
            ut = ph.enter_context(nc.sbuf_tensor("cb_u", [128, SL], F32))
            itmp = ph.enter_context(nc.sbuf_tensor("cb_it", [128, NST], F32))
            da = ph.enter_context(nc.sbuf_tensor("cb_da", [128, NST * SL], F32))
            bx = ph.enter_context(nc.sbuf_tensor("cb_bx", [128, NST * SL], F32))
            yh = [ph.enter_context(nc.sbuf_tensor("cb_yh%d" % i, [128, SL], BF16)) for i in range(2)]
            yl = [ph.enter_context(nc.sbuf_tensor("cb_yl%d" % i, [128, SL], BF16)) for i in range(2)]

            def grp(t, n0, n1, w):
                return t[:, n0 * SL:n1 * SL].rearrange("p (n t) -> p n t", n=n1 - n0)[:, :, 0:w]

            @blk.sync
            def _(sync):
                p.gate(sync)
                p.wait(sync, "mfc_a")
                p.wait(sync, "mfc_b")
                p.wait(sync, "mfaz")
                p.wait(sync, "mfaz2")
                for si, (s0, sl) in enumerate(CUBE_SEGS):
                    if si > 0:
                        p.wge(sync, "cbdc", 16 * si)
                    for n in range(NST):
                        sync.dma_start(out=Bb[:, n * SL:n * SL + sl],
                                       in_=S["bc"][n:n + 1, s0:s0 + sl].partition_broadcast(128)
                                       ).then_inc(*p.inc("cbbc", 16))
                        sync.dma_start(out=Cb[:, n * SL:n * SL + sl],
                                       in_=S["bc"][NST + n:NST + n + 1, s0:s0 + sl].partition_broadcast(128)
                                       ).then_inc(*p.inc("cbbc", 16))
                    o0, o1 = max(s0, WARM), s0 + sl
                    for dc in range(16):
                        g = si * 16 + dc
                        if g >= 3:
                            p.wge(sync, "cbdc", g - 2)
                        sync.dma_start(out=lnst[g % 3][:, 0:sl],
                                       in_=S["lns"][dc * 128:(dc + 1) * 128, s0:s0 + sl]
                                       ).then_inc(*p.inc("cbld%d" % (g % 3), 16))
                        sync.dma_start(out=xct[g % 3][:, 0:sl],
                                       in_=S["xc"][dc * 128:(dc + 1) * 128, s0:s0 + sl]
                                       ).then_inc(*p.inc("cbld%d" % (g % 3), 16))
                        sync.dma_start(out=szt[g % 3][:, 0:o1 - o0],
                                       in_=S["siluz"][dc * 128:(dc + 1) * 128, o0 - WARM:o1 - WARM]
                                       ).then_inc(*p.inc("cbld%d" % (g % 3), 16))

            @blk.scalar
            def _(se):
                p.gate(se)
                for g in range(NG):
                    si, s0, sl, o0, ol = _cseg(g)
                    dc = g % 16
                    p.wge(se, "cbld%d" % (g % 3), 48 * (g // 3 + 1))
                    if g >= 1:
                        p.wge(se, "cbsc", g)
                    for n in range(NST):
                        se.activation(out=da[:, n * SL:n * SL + sl], in_=lnst[g % 3][:, 0:sl],
                                      func=AF.Exp, scale=ealog[:, dc * NST + n:dc * NST + n + 1])
                    p.inc("cbse", 1)
                    se.drain().then_inc(p.s("cbse"), 1)

            @blk.vector
            def _(v):
                p.gate(v)
                da3 = da[:, :].rearrange("p (n t) -> p n t", n=NST)
                bx3 = bx[:, :].rearrange("p (n t) -> p n t", n=NST)
                it3 = itmp[:, 1:NST].rearrange("p (n o) -> p n o", o=1)
                for g in range(NG):
                    si, s0, sl, o0, ol = _cseg(g)
                    dc = g % 16
                    st3 = state[:, dc * NST:(dc + 1) * NST].rearrange("p (n o) -> p n o", o=1)
                    if dc == 0:
                        p.wge(v, "cbbc", 32 * 16 * (si + 1))
                    p.wge(v, "cbld%d" % (g % 3), 48 * (g // 3 + 1))
                    v.tensor_tensor(ut[:, 0:sl], lnst[g % 3][:, 0:sl], xct[g % 3][:, 0:sl], OP.mult)
                    v.drain()
                    v.tensor_tensor(grp(bx, 0, NST, sl),
                                    ut[:, 0:sl].unsqueeze(1).broadcast_to((128, NST, sl)),
                                    grp(Bb, 0, NST, sl), OP.mult)
                    v.drain()
                    p.wge(v, "cbse", g + 1)
                    v.tensor_tensor(it3, da3[:, 1:NST, 0:1], st3[:, 1:NST, :], OP.mult)
                    v.drain()
                    v.tensor_tensor(bx3[:, 1:NST, 0:1], bx3[:, 1:NST, 0:1], it3, OP.add)
                    v.drain()
                    v.tensor_scalar(da3[:, 1:NST, 0:1], da3[:, 1:NST, 0:1], 0.0, None, OP.mult)
                    v.drain()
                    v.tensor_tensor_scan(bx[:, :], da[:, :], bx[:, :],
                                         state[:, dc * NST:dc * NST + 1], OP.mult, OP.add)
                    v.drain()
                    v.tensor_copy(st3, bx3[:, :, sl - 1:sl])
                    p.inc("cbsc", 1)
                    v.drain().then_inc(p.s("cbsc"), 1)
                    v.tensor_tensor(bx[:, :], bx[:, :], Cb[:, :], OP.mult)
                    v.drain()
                    for h in (8, 4, 2, 1):
                        v.tensor_tensor(grp(bx, 0, h, sl), grp(bx, 0, h, sl),
                                        grp(bx, h, 2 * h, sl), OP.add)
                        v.drain()
                    v.scalar_tensor_tensor(bx[:, 0:sl], xct[g % 3][:, 0:sl], dpc[:, dc:dc + 1],
                                           bx[:, 0:sl], OP.mult, OP.add)
                    v.drain()
                    if g >= 2:
                        p.wge(v, "cbst%d" % (g % 2), 32 * (g // 2))
                    v.tensor_tensor(ut[:, 0:ol], bx[:, o0 - s0:o0 - s0 + ol],
                                    szt[g % 3][:, 0:ol], OP.mult)
                    v.drain()
                    v.tensor_copy(yh[g % 2][:, 0:ol], ut[:, 0:ol])
                    v.drain()
                    v.tensor_tensor(yl[g % 2][:, 0:ol], ut[:, 0:ol], yh[g % 2][:, 0:ol],
                                    OP.subtract)
                    p.inc("cbdc", 1)
                    v.drain().then_inc(p.s("cbdc"), 1)

            @blk.gpsimd
            def _(gp):
                p.gate(gp)
                for g in range(NG):
                    si, s0, sl, o0, ol = _cseg(g)
                    dc = g % 16
                    p.wge(gp, "cbdc", g + 1)
                    gp.dma_start(out=S["ysz_hi"][dc * 128:(dc + 1) * 128, o0 - WARM:o0 - WARM + ol],
                                 in_=yh[g % 2][:, 0:ol]).then_inc(*p.inc("cbst%d" % (g % 2), 16))
                    gp.dma_start(out=S["ysz_lo"][dc * 128:(dc + 1) * 128, o0 - WARM:o0 - WARM + ol],
                                 in_=yl[g % 2][:, 0:ol]).then_inc(*p.inc("cbst%d" % (g % 2), 16))

    p.mark("cb_a", "cbst0")
    p.mark("cb_b", "cbst1")
    p.endphase("cb_a", "cb_b")


def _emit_wout(nc, p, stack, I, S, sb, ps, cst):
    for half in range(2):
        tag = "wo%d" % half
        with nc.named_scope(tag), nc.Block(tag) as blk:
            with contextlib.ExitStack() as ph:
                yf = [[ph.enter_context(nc.sbuf_tensor(tag + "_y%d%s" % (i, h), [128, TOK], BF16))
                       for h in "hl"] for i in range(8)]
                wf = [[ph.enter_context(nc.sbuf_tensor(tag + "_w%d%s" % (i, h), [128, 1024], BF16))
                       for h in "hl"] for i in range(8)]
                x1t = ph.enter_context(nc.sbuf_tensor(tag + "_x1", [128, 2 * 512], F32))
                x2t = ph.enter_context(nc.sbuf_tensor(tag + "_x2", [128, 2 * 512], F32))
                L, M, E, ST = tag + "ld", tag + "mm", tag + "ev", tag + "st"

                @blk.sync
                def _(sync):
                    p.gate(sync)
                    if half == 0:
                        p.wait(sync, "cb_a")
                        p.wait(sync, "cb_b")
                    else:
                        p.wge(sync, "wo0st0", 16 * NTO)
                        p.wge(sync, "wo0st1", 16 * NTO)
                    for i in range(8):
                        dc = half * 8 + i
                        for hn, dst in (("ysz_hi", yf[i][0]), ("ysz_lo", yf[i][1])):
                            sync.dma_start(out=dst[:, :],
                                           in_=S[hn][dc * 128:(dc + 1) * 128, :]).then_inc(*p.inc(L, 16))
                        for hn, dst in (("wout_hi", wf[i][0]), ("wout_lo", wf[i][1])):
                            sync.dma_start(out=dst[:, :],
                                           in_=S[hn][0:D, dc * 128:(dc + 1) * 128],
                                           transpose=True).then_inc(*p.inc(L, 16))
                    for tc in range(NTO):
                        for dh in range(2):
                            idx = tc * 2 + dh
                            if idx >= 2:
                                p.wge(sync, E, idx - 1)
                            sync.dma_start(out=x1t[:, (idx % 2) * 512:(idx % 2) * 512 + 512],
                                           in_=S["x1"][WARM + tc * 128:WARM + tc * 128 + 128,
                                                       dh * 512:(dh + 1) * 512]).then_inc(*p.inc(L + "x%d" % (idx % 2), 16))

                @blk.tensor
                def _(tensor):
                    p.gate(tensor)
                    p.wge(tensor, L, 16 * 32)
                    for tc in range(NTO):
                        for dh in range(2):
                            idx = tc * 2 + dh
                            if idx >= 4:
                                p.wge(tensor, E, idx - 3)
                            pr = ps[idx % 4]
                            nmm = 0
                            for i in range(8):
                                for (yb, wb) in ((0, 0), (0, 1), (1, 0)):
                                    nmm += 1
                                    mm = tensor.matmul(pr[:, 0:512],
                                                       yf[i][yb][:, tc * 128:tc * 128 + 128],
                                                       wf[i][wb][:, dh * 512:(dh + 1) * 512],
                                                       start=(nmm == 1), stop=(nmm == 24))
                            mm.then_inc(*p.inc(M, 1))

                @blk.vector
                def _(v):
                    p.gate(v)
                    for tc in range(NTO):
                        for dh in range(2):
                            idx = tc * 2 + dh
                            p.wge(v, M, idx + 1)
                            p.wge(v, L + "x%d" % (idx % 2), 16 * (idx // 2 + 1))
                            if idx >= 2:
                                p.wge(v, ST + str(idx % 2), 16 * (idx // 2))
                            b = (idx % 2) * 512
                            v.tensor_tensor(x2t[:, b:b + 512], ps[idx % 4][:, 0:512],
                                            x1t[:, b:b + 512], OP.add)
                            p.inc(E, 1)
                            v.drain().then_inc(p.s(E), 1)

                @blk.scalar
                def _(se):
                    p.gate(se)
                    for tc in range(NTO):
                        for dh in range(2):
                            idx = tc * 2 + dh
                            p.wge(se, E, idx + 1)
                            se.dma_start(out=S["x1"][WARM + tc * 128:WARM + tc * 128 + 128,
                                                     dh * 512:(dh + 1) * 512],
                                         in_=x2t[:, (idx % 2) * 512:(idx % 2) * 512 + 512]
                                         ).then_inc(*p.inc(ST + str(idx % 2), 16))

                p.mark("woh%d_a" % half, ST + "0")
                p.mark("woh%d_b" % half, ST + "1")
                p.endphase("woh%d_a" % half, "woh%d_b" % half)

    p.mark("p7_a", "wo1st0")
    p.mark("p7_b", "wo1st1")
    p.endphase("p7_a", "p7_b")


def _emit_ffn1(nc, p, stack, I, S, sb, ps, cst, cols):
    C_SIN, C_SIN6, C_SGA, C_SUA, C_SDA, C_SIND, C_SIN6D = cols

    G = 4
    NQX = NTO
    elad, cdone, exps, absg = {}, {}, {}, {}
    vc = sec = 0
    for g in range(NQX // G):
        vc += 1; elad[g] = vc
        sec += 1; exps[g] = sec
        for k in range(G):
            sec += 1; absg[g * G + k] = sec
            vc += 1; cdone[g * G + k] = vc

    with nc.named_scope("f1x"), nc.Block("f1x") as blk:
        with contextlib.ExitStack() as ph:
            xb = ph.enter_context(nc.sbuf_tensor("f1x_xb", [128, G * D], F32))
            ab = ph.enter_context(nc.sbuf_tensor("f1x_ab", [128, D], F32))
            sg = ph.enter_context(nc.sbuf_tensor("f1x_sg", [128, D], F32))
            qa = ph.enter_context(nc.sbuf_tensor("f1x_qa", [128, D], F32))
            qb = ph.enter_context(nc.sbuf_tensor("f1x_qb", [128, D], F32))
            qsl = ph.enter_context(nc.sbuf_tensor("f1x_qs", [128, 8 * G * 64], F32))
            uo = [ph.enter_context(nc.sbuf_tensor("f1x_u%d" % i, [128, D], BF16)) for i in range(2)]
            NBK = D // 16

            @blk.sync
            def _(sync):
                p.gate(sync)
                p.wait(sync, "ln1_a")
                p.wait(sync, "ln1_b")
                p.wait(sync, "wqscales")
                for tc in range(NQX):
                    if tc >= G:
                        p.wge(sync, "f1xse", absg[tc - G])
                    sync.dma_start(out=xb[:, (tc % G) * D:(tc % G) * D + D],
                                   in_=S["xn1f"][WARM + tc * 128:WARM + tc * 128 + 128, :]
                                   ).then_inc(*p.inc("f1xld%d" % (tc % 4), 16))

            @blk.vector
            def _(v):
                p.gate(v)
                for g in range(NQX // G):
                    for k in range(G):
                        tc = g * G + k
                        p.wge(v, "f1xld%d" % (tc % 4), 16 * (tc // 4 + 1))
                        _q_stageA(v, qsl, G, NBK, k, xb[:, (tc % G) * D:(tc % G) * D + D], D)
                    _q_stageB_v1(v, qsl, G, NBK, G, cst[:, C_SIN6:C_SIN6 + 1])
                    p.inc("f1xv", 1)
                    v.drain().then_inc(p.s("f1xv"), 1)
                    p.wge(v, "f1xse", exps[g])
                    _q_stageB_v2(v, qsl, G, NBK, G, cst[:, C_SIN:C_SIN + 1])
                    for k in range(G):
                        tc = g * G + k
                        p.wge(v, "f1xse", absg[tc])
                        if tc >= 2:
                            p.wge(v, "f1xst%d" % (tc % 2), 16 * (tc // 2))
                        _q_stageC_v(v, qsl, G, NBK, k, ab, sg, qa, qb, uo[tc % 2][:, :], D)
                        p.inc("f1xv", 1)
                        v.drain().then_inc(p.s("f1xv"), 1)

            @blk.scalar
            def _(se):
                p.gate(se)
                for g in range(NQX // G):
                    p.wge(se, "f1xv", elad[g])
                    _q_stageB_se(se, qsl, G, NBK, G, cst[:, 14:15], cst[:, 15:16])
                    p.inc("f1xse", 1)
                    se.drain().then_inc(p.s("f1xse"), 1)
                    for k in range(G):
                        tc = g * G + k
                        if tc >= 1:
                            p.wge(se, "f1xv", cdone[tc - 1])
                        _q_stageC_se(se, xb[:, (tc % G) * D:(tc % G) * D + D], ab, sg, D)
                        p.inc("f1xse", 1)
                        se.drain().then_inc(p.s("f1xse"), 1)

            @blk.gpsimd
            def _(g_):
                p.gate(g_)
                for tc in range(NQX):
                    p.wge(g_, "f1xv", cdone[tc])
                    g_.dma_start(out=S["ux"][tc * 128:(tc + 1) * 128, :],
                                 in_=uo[tc % 2][:, :]).then_inc(*p.inc("f1xst%d" % (tc % 2), 16))

    p.mark("f1x_a", "f1xst0")
    p.mark("f1x_b", "f1xst1")
    p.endphase("f1x_a", "f1x_b")

    G = 4
    NQH = 8 * NTO
    elad, cdone, exps, absg = {}, {}, {}, {}
    vc = sec = 0
    for g in range(NQH // G):
        vc += 1; elad[g] = vc
        sec += 1; exps[g] = sec
        for k in range(G):
            sec += 1; absg[g * G + k] = sec
            vc += 1; cdone[g * G + k] = vc

    with nc.named_scope("f1g"), nc.Block("f1g") as blk:
        with contextlib.ExitStack() as ph:
            uxf = ph.enter_context(nc.sbuf_tensor("f1g_ux", [128, 8 * TOK], BF16))
            wgf = [ph.enter_context(nc.sbuf_tensor("f1g_wg%d" % i, [128, 8 * 512], BF16)) for i in range(2)]
            wuf = [ph.enter_context(nc.sbuf_tensor("f1g_wu%d" % i, [128, 8 * 512], BF16)) for i in range(2)]
            sgt = [ph.enter_context(nc.sbuf_tensor("f1g_sg%d" % i, [128, 512], F32)) for i in range(2)]
            hht = ph.enter_context(nc.sbuf_tensor("f1g_hh", [128, G * 512], F32))
            ab = ph.enter_context(nc.sbuf_tensor("f1g_ab", [128, 512], F32))
            sgn = ph.enter_context(nc.sbuf_tensor("f1g_sn", [128, 512], F32))
            qa = ph.enter_context(nc.sbuf_tensor("f1g_qa", [128, 512], F32))
            qb = ph.enter_context(nc.sbuf_tensor("f1g_qb", [128, 512], F32))
            qsl = ph.enter_context(nc.sbuf_tensor("f1g_qs", [128, 8 * G * 32], F32))
            uh = [ph.enter_context(nc.sbuf_tensor("f1g_uh%d" % i, [128, 512], BF16)) for i in range(2)]
            NBK = 512 // 16

            @blk.sync
            def _(sync):
                p.gate(sync)
                p.wait(sync, "f1x_a")
                p.wait(sync, "f1x_b")
                p.wait(sync, "wq_a")
                p.wait(sync, "wq_b")
                for hc in range(8):
                    sync.dma_start(out=uxf[:, hc * TOK:(hc + 1) * TOK],
                                   in_=S["ux"][0:TOK, hc * 128:(hc + 1) * 128],
                                   transpose=True).then_inc(*p.inc("f1gxl", 16))
                for os_ in range(8):
                    if os_ >= 2:
                        p.wge(sync, "f1gmm", 2 * NTO * (os_ - 1))
                    for wn, dst in (("uwg", wgf[os_ % 2]), ("uwu", wuf[os_ % 2])):
                        for hc in range(8):
                            sync.dma_start(out=dst[:, hc * 512:(hc + 1) * 512],
                                           in_=S[wn][os_ * 512:(os_ + 1) * 512, hc * 128:(hc + 1) * 128],
                                           transpose=True).then_inc(*p.inc("f1gwl%d" % (os_ % 2), 16))

            @blk.tensor
            def _(tensor):
                p.gate(tensor)
                p.wge(tensor, "f1gxl", 16 * 8)
                for os_ in range(8):
                    p.wge(tensor, "f1gwl%d" % (os_ % 2), 16 * 16 * (os_ // 2 + 1))
                    for tc in range(NTO):
                        idx = os_ * NTO + tc
                        if idx >= 2:
                            p.wge(tensor, "f1gsg", idx - 1)
                        if idx >= 2:
                            p.wge(tensor, "f1ghh", idx - 1)
                        pg, pu = ps[2 * (idx % 2)], ps[2 * (idx % 2) + 1]
                        for (dst, wf) in ((pg, wgf[os_ % 2]), (pu, wuf[os_ % 2])):
                            for hc in range(8):
                                mm = tensor.matmul(dst[:, 0:512],
                                                   uxf[:, hc * TOK + tc * 128:hc * TOK + tc * 128 + 128],
                                                   wf[:, hc * 512:(hc + 1) * 512],
                                                   start=(hc == 0), stop=(hc == 7))
                            mm.then_inc(*p.inc("f1gmm", 1))

            @blk.vector
            def _(v):
                p.gate(v)
                for g in range(NQH // G):
                    for k in range(G):
                        idx = g * G + k
                        p.wge(v, "f1gmm", 2 * idx + 2)
                        p.wge(v, "f1gsg", idx + 1)
                        if idx >= G:
                            p.wge(v, "f1hse", absg[idx - G])
                        v.scalar_tensor_tensor(hht[:, (idx % G) * 512:(idx % G) * 512 + 512],
                                               ps[2 * (idx % 2) + 1][:, 0:512],
                                               cst[:, C_SUA:C_SUA + 1], sgt[idx % 2][:, :],
                                               OP.mult, OP.mult)
                        p.inc("f1ghh", 1)
                        v.drain().then_inc(p.s("f1ghh"), 1)
                        _q_stageA(v, qsl, G, NBK, k, hht[:, (idx % G) * 512:(idx % G) * 512 + 512], 512)
                    _q_stageB_v1(v, qsl, G, NBK, G, cst[:, C_SIN6D:C_SIN6D + 1])
                    p.inc("f1hv", 1)
                    v.drain().then_inc(p.s("f1hv"), 1)
                    p.wge(v, "f1hse", exps[g])
                    _q_stageB_v2(v, qsl, G, NBK, G, cst[:, C_SIND:C_SIND + 1])
                    for k in range(G):
                        idx = g * G + k
                        p.wge(v, "f1hse", absg[idx])
                        if idx >= 2:
                            p.wge(v, "f1gst%d" % (idx % 2), 16 * (idx // 2))
                        _q_stageC_v(v, qsl, G, NBK, k, ab, sgn, qa, qb, uh[idx % 2][:, :], 512)
                        p.inc("f1hv", 1)
                        v.drain().then_inc(p.s("f1hv"), 1)

            @blk.scalar
            def _(se):
                p.gate(se)
                for g in range(NQH // G):
                    for k in range(G):
                        idx = g * G + k
                        p.wge(se, "f1gmm", 2 * idx + 1)
                        if idx >= 2:
                            p.wge(se, "f1ghh", idx - 1)
                        se.activation(out=sgt[idx % 2][:, :], in_=ps[2 * (idx % 2)][:, 0:512],
                                      func=AF.Silu, scale=cst[:, C_SGA:C_SGA + 1])
                        se.drain().then_inc(p.s("f1gsg"), 1)
                        p.inc("f1gsg", 1)
                    p.wge(se, "f1hv", elad[g])
                    _q_stageB_se(se, qsl, G, NBK, G, cst[:, 14:15], cst[:, 15:16])
                    p.inc("f1hse", 1)
                    se.drain().then_inc(p.s("f1hse"), 1)
                    for k in range(G):
                        idx = g * G + k
                        p.wge(se, "f1ghh", idx + 1)
                        if idx >= 1:
                            p.wge(se, "f1hv", cdone[idx - 1])
                        _q_stageC_se(se, hht[:, (idx % G) * 512:(idx % G) * 512 + 512], ab, sgn, 512)
                        p.inc("f1hse", 1)
                        se.drain().then_inc(p.s("f1hse"), 1)

            @blk.gpsimd
            def _(g_):
                p.gate(g_)
                for os_ in range(8):
                    for tc in range(NTO):
                        idx = os_ * NTO + tc
                        p.wge(g_, "f1hv", cdone[idx])
                        g_.dma_start(out=S["uh"][tc * 128:(tc + 1) * 128, os_ * 512:(os_ + 1) * 512],
                                     in_=uh[idx % 2][:, :]).then_inc(*p.inc("f1gst%d" % (idx % 2), 16))

    p.mark("f1g_a", "f1gst0")
    p.mark("f1g_b", "f1gst1")
    p.endphase("f1g_a", "f1g_b")

    for half in range(2):
        tag = "f1d%d" % half
        with nc.named_scope(tag), nc.Block(tag) as blk:
            with contextlib.ExitStack() as ph:
                uhf = [ph.enter_context(nc.sbuf_tensor(tag + "_uh%d" % i, [128, TOK], BF16))
                       for i in range(16)]
                wdf = [ph.enter_context(nc.sbuf_tensor(tag + "_wd%d" % i, [128, 1024], BF16))
                       for i in range(16)]
                x1t = ph.enter_context(nc.sbuf_tensor(tag + "_x1", [128, 2 * 512], F32))
                x2t = ph.enter_context(nc.sbuf_tensor(tag + "_x2", [128, 2 * 512], F32))
                L, M, E, ST = tag + "ld", tag + "mm", tag + "ev", tag + "st"

                @blk.sync
                def _(sync):
                    p.gate(sync)
                    p.wait(sync, "f1g_a")
                    p.wait(sync, "f1g_b")
                    if half == 0:
                        p.wait(sync, "p7_a")
                        p.wait(sync, "p7_b")
                    else:
                        p.wge(sync, "f1d0st0", 16 * NTO)
                        p.wge(sync, "f1d0st1", 16 * NTO)
                    for i in range(16):
                        oc = half * 16 + i
                        sync.dma_start(out=uhf[i][:, :],
                                       in_=S["uh"][0:TOK, oc * 128:(oc + 1) * 128],
                                       transpose=True).then_inc(*p.inc(L, 16))
                        sync.dma_start(out=wdf[i][:, :],
                                       in_=S["uwd"][0:D, oc * 128:(oc + 1) * 128],
                                       transpose=True).then_inc(*p.inc(L, 16))
                    for tc in range(NTO):
                        for dh in range(2):
                            idx = tc * 2 + dh
                            if idx >= 2:
                                p.wge(sync, E, idx - 1)
                            sync.dma_start(out=x1t[:, (idx % 2) * 512:(idx % 2) * 512 + 512],
                                           in_=S["x1"][WARM + tc * 128:WARM + tc * 128 + 128,
                                                       dh * 512:(dh + 1) * 512]).then_inc(*p.inc(L + "x%d" % (idx % 2), 16))

                @blk.tensor
                def _(tensor):
                    p.gate(tensor)
                    p.wge(tensor, L, 16 * 32)
                    for tc in range(NTO):
                        for dh in range(2):
                            idx = tc * 2 + dh
                            if idx >= 4:
                                p.wge(tensor, E, idx - 3)
                            pr = ps[4 + idx % 4]
                            for i in range(16):
                                mm = tensor.matmul(pr[:, 0:512],
                                                   uhf[i][:, tc * 128:tc * 128 + 128],
                                                   wdf[i][:, dh * 512:(dh + 1) * 512],
                                                   start=(i == 0), stop=(i == 15))
                            mm.then_inc(*p.inc(M, 1))

                @blk.vector
                def _(v):
                    p.gate(v)
                    for tc in range(NTO):
                        for dh in range(2):
                            idx = tc * 2 + dh
                            p.wge(v, M, idx + 1)
                            p.wge(v, L + "x%d" % (idx % 2), 16 * (idx // 2 + 1))
                            if idx >= 2:
                                p.wge(v, ST + str(idx % 2), 16 * (idx // 2))
                            b = (idx % 2) * 512
                            v.scalar_tensor_tensor(x2t[:, b:b + 512], ps[4 + idx % 4][:, 0:512],
                                                   cst[:, C_SDA:C_SDA + 1], x1t[:, b:b + 512],
                                                   OP.mult, OP.add)
                            p.inc(E, 1)
                            v.drain().then_inc(p.s(E), 1)

                @blk.scalar
                def _(se):
                    p.gate(se)
                    for tc in range(NTO):
                        for dh in range(2):
                            idx = tc * 2 + dh
                            p.wge(se, E, idx + 1)
                            se.dma_start(out=S["x1"][WARM + tc * 128:WARM + tc * 128 + 128,
                                                     dh * 512:(dh + 1) * 512],
                                         in_=x2t[:, (idx % 2) * 512:(idx % 2) * 512 + 512]
                                         ).then_inc(*p.inc("f1d%dst%d" % (half, idx % 2), 16))

                p.mark("f1dh%d_a" % half, "f1d%dst0" % half)
                p.mark("f1dh%d_b" % half, "f1d%dst1" % half)
                p.endphase("f1dh%d_a" % half, "f1dh%d_b" % half)

    p.mark("p8_a", "f1d1st0")
    p.mark("p8_b", "f1d1st1")



def _stage_inputs(inputs, W):
    EXT = QW + W
    x = np.asarray(inputs["L0_attn_x"]).reshape(T_SEQ, D)
    maps = []
    for c in range(NCORES):
        g0 = c * TOK - WARM
        xl = np.zeros((EXT, D), dtype=np.float32)
        s0, s1 = max(0, g0), min(T_SEQ, g0 + EXT)
        xl[s0 - g0:s1 - g0, :] = x[s0:s1, :]
        um = np.zeros((1, QW), dtype=np.float32)
        um[0, max(0, -g0):] = 1.0
        m = {
            "x": xl,
            "decay_logit": np.asarray(inputs["L0_attn_decay_logit"], np.float32).reshape(1, 1),
            "out_scale": np.asarray(inputs["L0_attn_out_scale"], np.float32).reshape(1, 1),
            "q_scale": np.asarray(inputs["L0_attn_q_scale"], np.float32).reshape(1, D),
            "k_scale": np.asarray(inputs["L0_attn_k_scale"], np.float32).reshape(1, D),
            "w_gate": np.asarray(inputs["L0_ffn_w_gate"], np.int32),
            "w_up": np.asarray(inputs["L0_ffn_w_up"], np.int32),
            "w_down": np.asarray(inputs["L0_ffn_w_down"], np.int32),
            "W_in": np.asarray(inputs["L1_attn_W_in"], np.float32),
            "conv_w": np.asarray(inputs["L1_attn_conv_w"], np.float32),
            "conv_b": np.asarray(inputs["L1_attn_conv_b"], np.float32).reshape(1, DIN),
            "W_x_t": np.ascontiguousarray(np.asarray(inputs["L1_attn_W_x"], np.float32).T),
            "W_dt_t": np.ascontiguousarray(np.asarray(inputs["L1_attn_W_dt"], np.float32).T),
            "b_dt": np.asarray(inputs["L1_attn_b_dt"], np.float32).reshape(1, DIN),
            "A_log": np.asarray(inputs["L1_attn_A_log"], np.float32),
            "Dp": np.asarray(inputs["L1_attn_Dp"], np.float32).reshape(1, DIN),
            "W_out": np.asarray(inputs["L1_attn_W_out"], np.float32),
            "g1_w": np.asarray(inputs["L1_ffn_gate_w"], np.float32),
            "u1_w": np.asarray(inputs["L1_ffn_up_w"], np.float32),
            "d1_w": np.asarray(inputs["L1_ffn_down_w"], np.float32),
            "s_in": np.asarray(inputs["L1_ffn_s_in"], np.float32).reshape(1, 1),
            "s_in_down": np.asarray(inputs["L1_ffn_s_in_down"], np.float32).reshape(1, 1),
            "umask": um,
        }
        maps.append(m)
    return maps


def kernel(**inputs):
    W = _band_width(np.asarray(inputs["L0_attn_decay_logit"]).reshape(-1)[0])
    nc, stack = build_program(W)
    try:
        maps = _stage_inputs(inputs, W)
        res = run_bass_kernel_spmd(nc, maps, core_ids=list(range(NCORES)))
    finally:
        stack.close()
    y = np.concatenate([np.asarray(r["out"]) for r in res.results], axis=0)
    return y.reshape(1, T_SEQ, D).astype(np.float32)


# version 11 (final)
# speedup vs baseline: 1.0179x; 1.0021x over previous
import contextlib
import math
import numpy as np
import ml_dtypes
import concourse.bass as bass
import concourse.mybir as mybir
from concourse.bass_utils import run_bass_kernel_spmd

F32 = mybir.dt.float32
BF16 = mybir.dt.bfloat16
I32 = mybir.dt.int32
AF = mybir.ActivationFunctionType
OP = mybir.AluOpType
AX = mybir.AxisListType

T_SEQ = 16384
D = 1024
NCORES = 8
TOK = T_SEQ // NCORES
WARM = 384
QW = WARM + TOK
HFF0 = 3072
DIN = 2048
NST = 16
DTR = 64
KCV = 4
HFF1 = 4096
LN_EPS = 1e-5
FF0S = 1.0 / 127.0
NTC = QW // 128
NTO = TOK // 128

E2M1_MIDS = [0.25, 0.75, 1.25, 1.75, 2.5, 3.5, 5.0]


def _band_width(decay_logit):
    decay = 1.0 / (1.0 + math.exp(-float(decay_logit)))
    if decay <= 1e-6:
        return 512
    dstar = math.log(1.4e-45) / math.log(decay)
    w = int(math.ceil(1.2 * dstar / 512.0)) * 512
    return max(512, min(T_SEQ, w))


class P:

    RANGES = {"e": (0, 50), "h": (50, 84), "s": (84, 90)}
    SWDGE_PFX = ("wq1st", "f1xst", "f1gst", "cbst", "w0c")

    def __init__(self, nc, stack):
        self.nc = nc
        self.stack = stack
        self.pool = {}
        self.tot = {}
        self.map = {}
        self.counts = {"e": 0, "h": 0, "s": 0}
        self.marks = {}

    @classmethod
    def _cls(cls, name):
        if any(name.startswith(x) for x in cls.SWDGE_PFX):
            return "s"
        if ("ld" in name or "st" in name or
                name in ("c0b", "mfum", "wqs", "wqs2", "wqs3", "wqs4", "wqb", "cbbc")):
            return "h"
        return "e"

    def _ensure(self, name):
        if name not in self.map:
            c = self._cls(name)
            lo, hi = self.RANGES[c]
            pidx = lo + self.counts[c] % (hi - lo)
            self.counts[c] += 1
            if pidx not in self.pool:
                self.pool[pidx] = self.stack.enter_context(self.nc.semaphore("sp%d" % pidx))
                self.tot[pidx] = 0
            self.map[name] = (pidx, self.tot[pidx])
        return self.map[name]

    def s(self, name):
        pidx, _ = self._ensure(name)
        return self.pool[pidx]

    def inc(self, name, k):
        pidx, _ = self._ensure(name)
        self.tot[pidx] += k
        return self.pool[pidx], k

    def v(self, name):
        pidx, _ = self._ensure(name)
        return self.tot[pidx]

    def wge(self, eng, name, val):
        pidx, base = self._ensure(name)
        eng.wait_ge(self.pool[pidx], base + val)

    def mark(self, mname, sname):
        pidx, _ = self._ensure(sname)
        self.marks[mname] = (self.pool[pidx], self.tot[pidx])

    def wait(self, eng, mname):
        h, v = self.marks[mname]
        if v > 0:
            eng.wait_ge(h, v)

    def endphase(self, *mnames):
        self.prev_marks = list(mnames)

    def gate(self, eng):
        for m in getattr(self, "prev_marks", []):
            self.wait(eng, m)


def build_program(W):
    nc = bass.Bass("TRN2", target_bir_lowering=False)
    stack = contextlib.ExitStack()
    p = P(nc, stack)
    EXT = QW + W
    assert EXT % 128 == 0
    NB = (W + 512) // 128

    I = {}
    def din(name, shape, dtype=F32):
        I[name] = nc.dram_tensor(name, shape, dtype, kind="ExternalInput").ap()
    din("x", [EXT, D])
    din("decay_logit", [1, 1]); din("out_scale", [1, 1])
    din("q_scale", [1, D]); din("k_scale", [1, D])
    din("w_gate", [HFF0, D], I32); din("w_up", [HFF0, D], I32); din("w_down", [D, HFF0], I32)
    din("W_in", [2 * DIN, D]); din("conv_w", [DIN, KCV]); din("conv_b", [1, DIN])
    din("W_x_t", [DIN, DTR + 2 * NST]); din("W_dt_t", [DTR, DIN]); din("b_dt", [1, DIN])
    din("A_log", [DIN, NST]); din("Dp", [1, DIN]); din("W_out", [D, DIN])
    din("g1_w", [HFF1, D]); din("u1_w", [HFF1, D]); din("d1_w", [D, HFF1])
    din("s_in", [1, 1]); din("s_in_down", [1, 1])
    din("umask", [1, QW])
    out = nc.dram_tensor("out", [TOK, D], F32, kind="ExternalOutput").ap()

    S = {}
    def scr(name, shape, dtype=F32):
        S[name] = nc.dram_tensor("sc_" + name, shape, dtype, kind="Internal").ap()
    scr("xn_hi", [EXT, D], BF16); scr("xn_lo", [EXT, D], BF16)
    scr("qp_hi", [QW, D], BF16); scr("qp_lo", [QW, D], BF16)
    scr("x1", [QW, D])
    scr("row", [8, 4096])
    scr("wg_b", [HFF0, D], BF16); scr("wu_b", [HFF0, D], BF16); scr("wd_b", [D, HFF0], BF16)
    scr("win_hi", [2 * DIN, D], BF16); scr("win_lo", [2 * DIN, D], BF16)
    scr("wout_hi", [D, DIN], BF16); scr("wout_lo", [D, DIN], BF16)
    scr("wx_hi", [DIN, DTR + 2 * NST], BF16); scr("wx_lo", [DIN, DTR + 2 * NST], BF16)
    scr("wdt_hi", [DTR, DIN], BF16); scr("wdt_lo", [DTR, DIN], BF16)
    scr("uwg", [HFF1, D], BF16); scr("uwu", [HFF1, D], BF16); scr("uwd", [D, HFF1], BF16)
    scr("xc", [DIN, QW]); scr("xc_hi", [DIN, QW], BF16); scr("xc_lo", [DIN, QW], BF16)
    scr("lns", [DIN, QW])
    scr("bc", [2 * NST, QW])
    scr("siluz", [DIN, TOK])
    scr("ysz_hi", [DIN, TOK], BF16); scr("ysz_lo", [DIN, TOK], BF16)
    scr("hh_hi", [QW, HFF0], BF16); scr("hh_lo", [QW, HFF0], BF16)
    scr("xn1f", [QW, D]); scr("xn1_hi", [QW, D], BF16); scr("xn1_lo", [QW, D], BF16)
    scr("ux", [TOK, D], BF16); scr("uh", [TOK, HFF1], BF16)

    sb = {}
    def sbuf(name, shape, dtype=F32):
        sb[name] = stack.enter_context(nc.sbuf_tensor("sb_" + name, shape, dtype))
        return sb[name]
    ps = [stack.enter_context(nc.psum_tensor("ps%d" % i, [128, 512], F32)) for i in range(8)]

    cst = sbuf("cst", [128, 40])
    convw = sbuf("convw", [128, 16 * KCV])
    convb = sbuf("convb", [128, 16])
    bdtn = sbuf("bdtn", [128, 16])
    dpc = sbuf("dpc", [128, 16])
    ealog = sbuf("ealog", [128, 16 * NST])
    state = sbuf("state", [128, 16 * NST])
    astack = contextlib.ExitStack()
    def asbuf(name, shape, dtype=F32):
        sb[name] = astack.enter_context(nc.sbuf_tensor("sb_" + name, shape, dtype))
        return sb[name]
    b_tile = asbuf("b_tile", [128, 512])
    a_mat = asbuf("a_mat", [128, NB + 2])
    wts = [asbuf("wt%d" % k, [128, 512]) for k in range(4)]
    qk_b = asbuf("qk_b", [128, D])

    C_LNDEC, C_NLNDEC, C_OS, C_SIN6, C_SIN, C_SIND, C_GWG, C_GWU, C_GWD, \
        C_SGA, C_SUA, C_SDA, C_LN2, C_SIN6D = range(14)

    eng = {}

    c0stack = contextlib.ExitStack()
    with nc.named_scope("c0"), nc.Block("c0") as blk:
        t0 = c0stack.enter_context(nc.sbuf_tensor("c0_t0", [1, 4096], F32))
        t1 = c0stack.enter_context(nc.sbuf_tensor("c0_t1", [1, 4096], F32))
        io0 = c0stack.enter_context(nc.sbuf_tensor("c0_io", [128, 2144], F32))

        @blk.sync
        def _(sync):
            p.gate(sync)
            sync.dma_start(out=t0[0:1, 0:1], in_=I["decay_logit"]).then_inc(*p.inc("c0ld", 16))
            sync.dma_start(out=t0[0:1, 1:2], in_=I["out_scale"]).then_inc(*p.inc("c0ld", 16))
            sync.dma_start(out=t0[0:1, 2:3], in_=I["s_in"]).then_inc(*p.inc("c0ld", 16))
            sync.dma_start(out=t0[0:1, 3:4], in_=I["s_in_down"]).then_inc(*p.inc("c0ld", 16))
            sync.dma_start(out=t0[0:1, 16:16 + D], in_=I["q_scale"]).then_inc(*p.inc("c0ld", 16))
            sync.dma_start(out=t0[0:1, 16 + D:16 + 2 * D], in_=I["k_scale"]).then_inc(*p.inc("c0ld", 16))
            sync.dma_start(out=convw[:, :].rearrange("p (c k) -> p c k", k=KCV),
                           in_=I["conv_w"].rearrange("(c p) k -> p c k", p=128)).then_inc(*p.inc("c0ld", 16))
            sync.dma_start(out=convb[:, :].rearrange("p (o c) -> p o c", o=1),
                           in_=I["conv_b"].rearrange("o (c p) -> p o c", p=128),
                           allow_slow_non_contiguous=True).then_inc(*p.inc("c0ld", 16))
            sync.dma_start(out=bdtn[:, :].rearrange("p (o c) -> p o c", o=1),
                           in_=I["b_dt"].rearrange("o (c p) -> p o c", p=128),
                           allow_slow_non_contiguous=True).then_inc(*p.inc("c0ld", 16))
            sync.dma_start(out=dpc[:, :].rearrange("p (o c) -> p o c", o=1),
                           in_=I["Dp"].rearrange("o (c p) -> p o c", p=128),
                           allow_slow_non_contiguous=True).then_inc(*p.inc("c0ld", 16))
            sync.dma_start(out=ealog[:, :].rearrange("p (c n) -> p c n", n=NST),
                           in_=I["A_log"].rearrange("(c p) n -> p c n", p=128)).then_inc(*p.inc("c0ld", 16))

        @blk.gpsimd
        def _(g):
            p.gate(g)
            g.iota(t1[0:1, 0:512], [[1, 512]], channel_multiplier=0,
                   allow_small_or_imprecise_dtypes=True)
            g.iota(t1[0:1, 512:512 + NB + 2], [[128, NB + 2]], base=-1, channel_multiplier=0,
                   allow_small_or_imprecise_dtypes=True)
            g.iota(io0[:, 0:1], [[0, 1]], channel_multiplier=1,
                   allow_small_or_imprecise_dtypes=True).then_inc(*p.inc("c0io", 1))
            for k in range(4):
                g.iota(io0[:, 1 + k * 512:1 + (k + 1) * 512], [[-1, 512]], base=128 * k - 1,
                       channel_multiplier=1, allow_small_or_imprecise_dtypes=True)
            g.iota(io0[:, 1 + 4 * 512:1 + 4 * 512 + 1], [[0, 1]], base=0,
                   channel_multiplier=0, allow_small_or_imprecise_dtypes=True).then_inc(*p.inc("c0io", 1))

        @blk.scalar
        def _(se):
            p.gate(se)
            se.wait_ge(p.s("c0ld"), p.v("c0ld"))
            se.activation(out=t0[0:1, 4:5], in_=t0[0:1, 0:1], func=AF.Sigmoid)
            se.drain()
            se.activation(out=t0[0:1, 5:6], in_=t0[0:1, 4:5], func=AF.Ln).then_inc(*p.inc("c0a", 1))
            se.activation(out=ealog[:, :], in_=ealog[:, :], func=AF.Exp).then_inc(*p.inc("c0a", 1))

        @blk.vector
        def _(v):
            p.gate(v)
            p.wge(v, "c0a", 1)
            v.tensor_scalar(t0[0:1, 6:7], t0[0:1, 5:6], -1.0, None, OP.mult)
            v.tensor_tensor(t0[0:1, 16:16 + D], t0[0:1, 16:16 + D], t0[0:1, 16 + D:16 + 2 * D], OP.mult)
            v.tensor_scalar(t1[0:1, 1024:1025], t0[0:1, 2:3], 1.0 / 6.0, None, OP.mult)
            v.drain().then_inc(*p.inc("c0v", 1))
            v.tensor_scalar(bdtn[:, :], bdtn[:, :], -1.0, None, OP.mult).then_inc(*p.inc("c0v", 1))

        @blk.scalar
        def _(se):
            p.gate(se)
            p.wge(se, "c0v", 1)
            p.wge(se, "c0io", 1)
            se.activation(out=t1[0:1, 0:512], in_=t1[0:1, 0:512], func=AF.Exp,
                          scale=t0[0:1, 6:7]).then_inc(*p.inc("c0a", 1))
            se.activation(out=t1[0:1, 512:512 + NB + 2], in_=t1[0:1, 512:512 + NB + 2], func=AF.Exp,
                          scale=t0[0:1, 5:6]).then_inc(*p.inc("c0a", 1))

        @blk.sync
        def _(sync):
            p.gate(sync)
            sync.wait_ge(p.s("c0a"), p.v("c0a"))
            sync.dma_start(out=S["row"][0:1, 0:7], in_=t0[0:1, 0:7]).then_inc(*p.inc("c0st", 16))
            sync.dma_start(out=S["row"][0:1, 16:16 + D], in_=t0[0:1, 16:16 + D]).then_inc(*p.inc("c0st", 16))
            sync.dma_start(out=S["row"][1:2, 0:512 + NB + 2], in_=t1[0:1, 0:512 + NB + 2]).then_inc(*p.inc("c0st", 16))
            sync.dma_start(out=S["row"][1:2, 1024:1025], in_=t1[0:1, 1024:1025]).then_inc(*p.inc("c0st", 16))
            sync.wait_ge(p.s("c0st"), p.v("c0st"))
            sync.dma_start(out=cst[:, C_LNDEC:C_LNDEC + 1],
                           in_=S["row"][0:1, 5:6].partition_broadcast(128)).then_inc(*p.inc("c0b", 16))
            sync.dma_start(out=cst[:, C_NLNDEC:C_NLNDEC + 1],
                           in_=S["row"][0:1, 6:7].partition_broadcast(128)).then_inc(*p.inc("c0b", 16))
            sync.dma_start(out=cst[:, C_OS:C_OS + 1],
                           in_=S["row"][0:1, 1:2].partition_broadcast(128)).then_inc(*p.inc("c0b", 16))
            sync.dma_start(out=cst[:, C_SIN:C_SIN + 1],
                           in_=S["row"][0:1, 2:3].partition_broadcast(128)).then_inc(*p.inc("c0b", 16))
            sync.dma_start(out=cst[:, C_SIND:C_SIND + 1],
                           in_=S["row"][0:1, 3:4].partition_broadcast(128)).then_inc(*p.inc("c0b", 16))
            sync.dma_start(out=cst[:, C_SIN6:C_SIN6 + 1],
                           in_=S["row"][1:2, 1024:1025].partition_broadcast(128)).then_inc(*p.inc("c0b", 16))
            sync.dma_start(out=b_tile[:, :],
                           in_=S["row"][1:2, 0:512].partition_broadcast(128)).then_inc(*p.inc("c0b", 16))
            sync.dma_start(out=io0[:, 1 + 4 * 512 + 1:1 + 4 * 512 + 1 + NB + 2],
                           in_=S["row"][1:2, 512:512 + NB + 2].partition_broadcast(128)).then_inc(*p.inc("c0b", 16))
            sync.dma_start(out=qk_b[:, :],
                           in_=S["row"][0:1, 16:16 + D].partition_broadcast(128)).then_inc(*p.inc("c0b", 16))

        @blk.scalar
        def _(se):
            p.gate(se)
            se.wait_ge(p.s("c0b"), p.v("c0b"))
            p.wge(se, "c0io", 2)
            se.activation(out=io0[:, 0:1], in_=io0[:, 0:1], func=AF.Exp,
                          scale=cst[:, C_LNDEC:C_LNDEC + 1])
            for k in range(4):
                se.activation(out=wts[k][:, :], in_=io0[:, 1 + k * 512:1 + (k + 1) * 512],
                              func=AF.Exp, scale=cst[:, C_LNDEC:C_LNDEC + 1])
            se.drain().then_inc(*p.inc("c0a", 1))

        @blk.vector
        def _(v):
            p.gate(v)
            v.wait_ge(p.s("c0a"), p.v("c0a"))
            v.tensor_scalar(a_mat[:, :], io0[:, 1 + 4 * 512 + 1:1 + 4 * 512 + 1 + NB + 2],
                            io0[:, 0:1], None, OP.mult)
            for k in range(4):
                v.tensor_scalar(io0[:, 1 + k * 512:1 + (k + 1) * 512],
                                io0[:, 1 + k * 512:1 + (k + 1) * 512], -0.5, None, OP.is_gt)
            v.drain()
            for k in range(4):
                v.tensor_tensor(wts[k][:, :], wts[k][:, :], io0[:, 1 + k * 512:1 + (k + 1) * 512], OP.mult)
            v.memset(state[:, :], 0.0)
            v.memset(cst[:, 14:15], -10.0 * math.log(2.0))
            v.memset(cst[:, 15:16], 10.0 * math.log(2.0))
            v.memset(cst[:, 16:17], LN_EPS)
            v.drain().then_inc(*p.inc("c0v", 1))

    c0stack.close()
    p.mark("c0", "c0v")
    p.mark("c0b", "c0b")
    p.endphase("c0", "c0b")

    w0stack = contextlib.ExitStack()
    with nc.named_scope("w0"), nc.Block("w0") as blk:
        wtile = w0stack.enter_context(nc.sbuf_tensor("w0_a", [128, 2 * 4096], F32))
        whi = w0stack.enter_context(nc.sbuf_tensor("w0_hi", [128, 2 * 4096], BF16))
        wlo = w0stack.enter_context(nc.sbuf_tensor("w0_lo", [128, 2 * 4096], BF16))

        SPLITS = [("W_in", "win_hi", "win_lo", 2 * DIN, D),
                  ("W_out", "wout_hi", "wout_lo", D, DIN),
                  ("W_x_t", "wx_hi", "wx_lo", DIN, DTR + 2 * NST),
                  ("W_dt_t", "wdt_hi", "wdt_lo", DTR, DIN)]
        CASTS = [("w_gate", "wg_b", HFF0, D), ("w_up", "wu_b", HFF0, D),
                 ("w_down", "wd_b", D, HFF0)]

        def chunks(rows):
            o = 0
            while o < rows:
                yield o, min(128, rows - o)
                o += 128

        @blk.sync
        def _(sync):
            p.gate(sync)
            i = 0
            for (src, hi, lo, rows, cols) in SPLITS:
                for (o, r) in chunks(rows):
                    if i >= 2:
                        p.wge(sync, "w0v", i - 1)
                    buf = (i % 2) * 4096
                    sync.dma_start(out=wtile[0:r, buf:buf + cols],
                                   in_=I[src][o:o + r, :]).then_inc(*p.inc("w0ld%d" % (i % 2), 16))
                    i += 1

        @blk.vector
        def _(v):
            p.gate(v)
            i = 0
            for (src, hi, lo, rows, cols) in SPLITS:
                for (o, r) in chunks(rows):
                    p.wge(v, "w0ld%d" % (i % 2), 16 * (i // 2 + 1))
                    if i >= 2:
                        p.wge(v, "w0st%d" % (i % 2), 32 * (i // 2))
                    buf = (i % 2) * 4096
                    v.tensor_copy(whi[0:r, buf:buf + cols], wtile[0:r, buf:buf + cols])
                    v.drain()
                    v.tensor_tensor(wlo[0:r, buf:buf + cols], wtile[0:r, buf:buf + cols],
                                    whi[0:r, buf:buf + cols], OP.subtract)
                    v.drain().then_inc(*p.inc("w0v", 1))
                    i += 1

        @blk.scalar
        def _(se):
            p.gate(se)
            i = 0
            for (src, hi, lo, rows, cols) in SPLITS:
                for (o, r) in chunks(rows):
                    p.wge(se, "w0v", i + 1)
                    buf = (i % 2) * 4096
                    se.dma_start(out=S[hi][o:o + r, :], in_=whi[0:r, buf:buf + cols]).then_inc(*p.inc("w0st%d" % (i % 2), 16))
                    se.dma_start(out=S[lo][o:o + r, :], in_=wlo[0:r, buf:buf + cols]).then_inc(*p.inc("w0st%d" % (i % 2), 16))
                    i += 1

        @blk.gpsimd
        def _(g):
            p.gate(g)
            for (src, dst, rows, cols) in CASTS:
                g.dma_start(out=S[dst][:, :], in_=I[src][:, :]).then_inc(*p.inc("w0c", 16))

    w0stack.close()
    p.mark("w0_a", "w0st0")
    p.mark("w0_b", "w0st1")
    p.mark("w0c", "w0c")
    p.endphase("w0_a", "w0_b", "w0c")

    _emit_wquant(nc, p, stack, I, S, sb, cst, ps)

    _emit_ln(nc, p, stack, I, S, sb, src_kind="x", n_tiles=EXT // 128, qp=True,
             qk_b=qk_b, tag="ln0", beps=cst[:, 16:17])

    _emit_attn(nc, p, stack, I, S, sb, ps, cst, b_tile, a_mat, wts, W, NB, C_OS)
    astack.close()

    _emit_ffn0(nc, p, stack, I, S, sb, ps, cst)

    _emit_ln(nc, p, stack, I, S, sb, src_kind="x1", n_tiles=QW // 128, qp=False,
             qk_b=None, tag="ln1", beps=cst[:, 16:17])

    _emit_mamba_front(nc, p, stack, I, S, sb, ps, cst, convw, convb, bdtn)

    _emit_cube(nc, p, stack, I, S, sb, ps, cst, ealog, state, dpc)

    _emit_wout(nc, p, stack, I, S, sb, ps, cst)

    _emit_ffn1(nc, p, stack, I, S, sb, ps, cst,
               (C_SIN, C_SIN6, C_SGA, C_SUA, C_SDA, C_SIND, C_SIN6D))

    with nc.Block("p9") as blk:
        @blk.sync
        def _(sync):
            p.gate(sync)
            p.wait(sync, "p8_a")
            p.wait(sync, "p8_b")
            sync.dma_start(out=out[:, :], in_=S["x1"][WARM:WARM + TOK, :]).then_inc(*p.inc("p9st", 16))
            p.wge(sync, "p9st", 16)

    return nc, stack



def _emit_ln(nc, p, stack, I, S, sb, src_kind, n_tiles, qp, qk_b, tag, beps):
    with nc.named_scope(tag), nc.Block(tag) as blk:
        with contextlib.ExitStack() as ph:
            xt = [ph.enter_context(nc.sbuf_tensor(f"{tag}_x{i}", [128, D], F32)) for i in range(2)]
            xn = [ph.enter_context(nc.sbuf_tensor(f"{tag}_n{i}", [128, D], F32)) for i in range(2)]
            hi = [ph.enter_context(nc.sbuf_tensor(f"{tag}_h{i}", [128, D], BF16)) for i in range(2)]
            lo = [ph.enter_context(nc.sbuf_tensor(f"{tag}_l{i}", [128, D], BF16)) for i in range(2)]
            qh = [ph.enter_context(nc.sbuf_tensor(f"{tag}_qh{i}", [128, D], BF16)) for i in range(2)]
            ql = [ph.enter_context(nc.sbuf_tensor(f"{tag}_ql{i}", [128, D], BF16)) for i in range(2)]
            q3 = ph.enter_context(nc.sbuf_tensor(f"{tag}_q3", [128, D], F32))
            st = ph.enter_context(nc.sbuf_tensor(f"{tag}_st", [128, 24], F32))

            src = I["x"] if src_kind == "x" else S["x1"]
            hid, lod = (("xn_hi", "xn_lo") if src_kind == "x" else ("xn1_hi", "xn1_lo"))
            L, A, V, Q, ST = tag + "ld", tag + "ap", tag + "v", tag + "sq", tag + "st"

            @blk.sync
            def _(sync):
                p.gate(sync)
                if src_kind == "x1":
                    p.wait(sync, "p3_a")
                    p.wait(sync, "p3_b")
                for i in range(n_tiles):
                    if i >= 2:
                        p.wge(sync, A, i - 1)
                    sync.dma_start(out=xt[i % 2][:, :],
                                   in_=src[i * 128:(i + 1) * 128, :]).then_inc(*p.inc(L + str(i % 2), 16))

            nst_tile = [2 + (1 if src_kind == "x1" else 0) +
                        (2 if (qp and i < QW // 128) else 0) for i in range(n_tiles)]
            cum_st = [0]
            for c_ in nst_tile:
                cum_st.append(cum_st[-1] + c_)

            @blk.vector
            def _(v):
                p.gate(v)
                for i in range(n_tiles):
                    b = i % 2
                    p.wge(v, L + str(b), 16 * (i // 2 + 1))
                    if i >= 2:
                        p.wge(v, ST + str(b), 16 * sum(nst_tile[j] for j in range(b, i - 1, 2)))
                    v.bn_stats(st[:, 0:6], xt[b][:, 0:512])
                    v.drain()
                    v.bn_stats(st[:, 6:12], xt[b][:, 512:1024])
                    v.drain()
                    v.bn_aggr(st[:, 12:14], st[:, 0:12])
                    v.drain().then_inc(*p.inc(V, 1))
                    p.wge(v, Q, i + 1)
                    v.reciprocal(st[:, 15:16], st[:, 14:15])
                    v.drain()
                    v.tensor_tensor(st[:, 16:17], st[:, 15:16], st[:, 15:16], OP.mult)
                    v.tensor_scalar(st[:, 17:18], st[:, 13:14], LN_EPS, None, OP.add)
                    v.drain()
                    v.tensor_tensor(st[:, 18:19], st[:, 16:17], st[:, 17:18], OP.mult)
                    v.drain()
                    v.tensor_scalar(st[:, 18:19], st[:, 18:19], -0.5, 1.5, OP.mult, OP.add)
                    v.drain()
                    v.tensor_tensor(st[:, 19:20], st[:, 15:16], st[:, 18:19], OP.mult)
                    v.drain()
                    v.tensor_scalar(xn[b][:, :], xt[b][:, :], st[:, 12:13], st[:, 19:20],
                                    OP.subtract, OP.mult).then_inc(*p.inc(A, 1))
                    v.drain()
                    v.tensor_copy(hi[b][:, :], xn[b][:, :])
                    v.drain()
                    v.tensor_tensor(lo[b][:, :], xn[b][:, :], hi[b][:, :], OP.subtract)
                    if qp and i < QW // 128:
                        v.drain()
                        v.tensor_tensor(q3[:, :], xn[b][:, :], qk_b[:, :], OP.mult)
                        v.drain()
                        v.tensor_copy(qh[b][:, :], q3[:, :])
                        v.drain()
                        v.tensor_tensor(ql[b][:, :], q3[:, :], qh[b][:, :], OP.subtract)
                    v.drain().then_inc(*p.inc(V, 1))

            @blk.scalar
            def _(se):
                p.gate(se)
                for i in range(n_tiles):
                    b = i % 2
                    p.wge(se, V, 2 * i + 1)
                    se.activation(out=st[:, 14:15], in_=st[:, 13:14], func=AF.Sqrt,
                                  bias=beps).then_inc(*p.inc(Q, 1))
                    p.wge(se, V, 2 * i + 2)
                    se.dma_start(out=S[hid][i * 128:(i + 1) * 128, :], in_=hi[b][:, :]).then_inc(*p.inc(ST + str(b), 16))
                    se.dma_start(out=S[lod][i * 128:(i + 1) * 128, :], in_=lo[b][:, :]).then_inc(*p.inc(ST + str(b), 16))
                    if src_kind == "x1":
                        se.dma_start(out=S["xn1f"][i * 128:(i + 1) * 128, :],
                                     in_=xn[b][:, :]).then_inc(*p.inc(ST + str(b), 16))
                    if qp and i < QW // 128:
                        se.dma_start(out=S["qp_hi"][i * 128:(i + 1) * 128, :],
                                     in_=qh[b][:, :]).then_inc(*p.inc(ST + str(b), 16))
                        se.dma_start(out=S["qp_lo"][i * 128:(i + 1) * 128, :],
                                     in_=ql[b][:, :]).then_inc(*p.inc(ST + str(b), 16))

    p.mark(tag + "_a", tag + "st0")
    p.mark(tag + "_b", tag + "st1")
    p.endphase(tag + "_a", tag + "_b")


def _q_stageA(v, qsl, G, NBK, slot, x_ap, C):
    amax = qsl[:, 0 * G * NBK + slot * NBK: 0 * G * NBK + (slot + 1) * NBK]
    v.tensor_reduce(amax.rearrange("p (b o) -> p b o", o=1),
                    x_ap.rearrange("p (b k) -> p b k", k=16),
                    AX.X, OP.max, apply_absolute_value=True)
    v.drain()


def _q_stageB_v1(v, qsl, G, NBK, ng, gs6_col):
    W = ng * NBK
    GN = G * NBK
    amax = qsl[:, 0:W]
    eidx = qsl[:, GN:GN + W]
    tmp = qsl[:, 7 * GN:7 * GN + W]
    v.tensor_scalar(tmp, amax, gs6_col, None, OP.mult)
    v.drain()
    v.tensor_scalar(tmp, tmp, 448.0, None, OP.min)
    v.drain()
    v.tensor_scalar(eidx, tmp, 2.0 ** -9, None, OP.is_ge)
    for k in range(-8, 9):
        v.drain()
        v.scalar_tensor_tensor(eidx, tmp, float(2.0 ** k), eidx, OP.is_ge, OP.add)
    v.drain()


def _q_stageB_se(se, qsl, G, NBK, ng, bn10, bp10):
    W = ng * NBK
    GN = G * NBK
    LN2 = math.log(2.0)
    eidx = qsl[:, GN:GN + W]
    bexp = qsl[:, 2 * GN:2 * GN + W]
    rbexp = qsl[:, 3 * GN:3 * GN + W]
    se.activation(out=bexp, in_=eidx, func=AF.Exp, scale=LN2, bias=bn10)
    se.drain()
    se.activation(out=rbexp, in_=eidx, func=AF.Exp, scale=-LN2, bias=bp10)
    se.drain()


def _q_stageB_v2(v, qsl, G, NBK, ng, gs_col):
    W = ng * NBK
    GN = G * NBK
    bexp = qsl[:, 2 * GN:2 * GN + W]
    rbexp = qsl[:, 3 * GN:3 * GN + W]
    m = qsl[:, 4 * GN:4 * GN + W]
    sc8 = qsl[:, 5 * GN:5 * GN + W]
    rsg = qsl[:, 6 * GN:6 * GN + W]
    tmp = qsl[:, 7 * GN:7 * GN + W]
    v.tensor_scalar(bexp, bexp, 2.0 ** -6, None, OP.max)
    v.drain()
    v.tensor_scalar(rbexp, rbexp, 2.0 ** 6, None, OP.min)
    v.drain()
    v.tensor_tensor(m, tmp, rbexp, OP.mult)
    v.drain()
    v.tensor_scalar(sc8, m, 1.0 / 16.0, None, OP.is_ge)
    for j in range(1, 16):
        v.drain()
        v.scalar_tensor_tensor(sc8, m, float((2 * j + 1) / 16.0), sc8, OP.is_ge, OP.add)
    v.drain()
    v.tensor_tensor(sc8, sc8, bexp, OP.mult)
    v.drain()
    v.tensor_scalar(sc8, sc8, 0.125, None, OP.mult)
    v.drain()
    v.tensor_scalar(tmp, sc8, 1e-30, None, OP.max)
    v.drain()
    v.reciprocal(rsg, tmp)
    v.drain()
    v.tensor_scalar(rsg, rsg, gs_col, None, OP.mult)
    v.drain()


def _q_stageC_se(se, x_ap, ab, sg, C):
    se.activation(out=ab[:, 0:C], in_=x_ap, func=AF.Abs)
    se.drain()
    se.activation(out=sg[:, 0:C], in_=x_ap, func=AF.Sign)
    se.drain()


def _q_stageC_v(v, qsl, G, NBK, slot, ab, sg, qa, qb, u_out, C):
    GN = G * NBK
    sc8 = qsl[:, 5 * GN + slot * NBK:5 * GN + (slot + 1) * NBK]
    rsg = qsl[:, 6 * GN + slot * NBK:6 * GN + (slot + 1) * NBK]
    A = qa[:, 0:C]
    Cq = qb[:, 0:C]
    v.tensor_tensor(A.rearrange("p (b k) -> p b k", k=16),
                    ab[:, 0:C].rearrange("p (b k) -> p b k", k=16),
                    rsg.rearrange("p (b o) -> p b o", o=1).broadcast_to((128, NBK, 16)),
                    OP.mult)
    v.drain()
    v.tensor_scalar(Cq, A, 0.25, None, OP.is_ge)
    for mth in (0.75, 1.25, 1.75):
        v.drain()
        v.scalar_tensor_tensor(Cq, A, mth, Cq, OP.is_ge, OP.add)
    v.drain()
    v.tensor_scalar(Cq, Cq, 0.5, None, OP.mult)
    v.drain()
    v.scalar_tensor_tensor(Cq, A, 2.5, Cq, OP.is_ge, OP.add)
    v.drain()
    v.scalar_tensor_tensor(Cq, A, 3.5, Cq, OP.is_ge, OP.add)
    v.drain()
    v.tensor_scalar(A, A, 5.0, 2.0, OP.is_ge, OP.mult)
    v.drain()
    v.tensor_tensor(A, Cq, A, OP.add)
    v.drain()
    v.tensor_tensor(A, A, sg[:, 0:C], OP.mult)
    v.drain()
    v.tensor_tensor(u_out.rearrange("p (b k) -> p b k", k=16),
                    A.rearrange("p (b k) -> p b k", k=16),
                    sc8.rearrange("p (b o) -> p b o", o=1).broadcast_to((128, NBK, 16)),
                    OP.mult)
    v.drain()


def _emit_wquant(nc, p, stack, I, S, sb, cst, ps):
    C_LNDEC, C_NLNDEC, C_OS, C_SIN6, C_SIN, C_SIND, C_GWG, C_GWU, C_GWD, \
        C_SGA, C_SUA, C_SDA, C_LN2, C_SIN6D = range(14)
    WQ = [("g1_w", "uwg", HFF1, D, C_GWG),
          ("u1_w", "uwu", HFF1, D, C_GWU),
          ("d1_w", "uwd", D, HFF1, C_GWD)]

    wq0stack = contextlib.ExitStack()
    with nc.named_scope("wq0"), nc.Block("wq0") as blk:
        wt = wq0stack.enter_context(nc.sbuf_tensor("wq_a", [128, 2 * 4096], F32))
        mx = wq0stack.enter_context(nc.sbuf_tensor("wq_m", [128, 8], F32))
        sc1 = wq0stack.enter_context(nc.sbuf_tensor("wq_s", [1, 512], F32))
        nchunks = [R // 128 for (_, _, R, _, _) in WQ]

        @blk.sync
        def _(sync):
            p.gate(sync)
            i = 0
            for (src, dst, R, C, cg) in WQ:
                for o in range(0, R, 128):
                    if i >= 2:
                        p.wge(sync, "wqv", i - 1)
                    sync.dma_start(out=wt[:, (i % 2) * 4096:(i % 2) * 4096 + C],
                                   in_=I[src][o:o + 128, :]).then_inc(*p.inc("wqld%d" % (i % 2), 16))
                    i += 1

        @blk.vector
        def _(v):
            p.gate(v)
            i = 0
            for wi, (src, dst, R, C, cg) in enumerate(WQ):
                for j in range(R // 128):
                    p.wge(v, "wqld%d" % (i % 2), 16 * (i // 2 + 1))
                    b = (i % 2) * 4096
                    if j == 0:
                        v.tensor_reduce(mx[:, wi:wi + 1], wt[:, b:b + C], AX.X, OP.max,
                                        apply_absolute_value=True)
                    else:
                        v.tensor_reduce(mx[:, 4:5], wt[:, b:b + C], AX.X, OP.max,
                                        apply_absolute_value=True)
                        v.drain()
                        v.tensor_tensor(mx[:, wi:wi + 1], mx[:, wi:wi + 1], mx[:, 4:5], OP.max)
                    v.drain().then_inc(*p.inc("wqv", 1))
                    i += 1

        @blk.scalar
        def _(se):
            p.gate(se)
            p.wge(se, "wqv", sum(nchunks))
            se.dma_start(out=S["row"][2:3, 0:384].rearrange("o (c pp) -> pp (o c)", pp=128),
                         in_=mx[:, 0:3], allow_slow_non_contiguous=True).then_inc(*p.inc("wqs", 16))

        @blk.sync
        def _(sync):
            p.gate(sync)
            p.wge(sync, "wqs", 16)
            sync.dma_start(out=sc1[0:1, 0:384], in_=S["row"][2:3, 0:384]).then_inc(*p.inc("wqs2", 16))
            sync.dma_start(out=sc1[0:1, 400:401], in_=I["s_in"]).then_inc(*p.inc("wqs2", 16))
            sync.dma_start(out=sc1[0:1, 401:402], in_=I["s_in_down"]).then_inc(*p.inc("wqs2", 16))

        @blk.vector
        def _(v):
            p.gate(v)
            v.memset(sc1[0:1, 416:448], 0.0)
            v.drain()
            p.wge(v, "wqs2", 48)
            for wi in range(3):
                v.tensor_reduce(sc1[0:1, 416 + wi:417 + wi], sc1[0:1, wi * 128:(wi + 1) * 128],
                                AX.X, OP.max)
                v.drain()
            v.reciprocal(sc1[0:1, 420:423], sc1[0:1, 416:419])
            v.drain()
            v.tensor_scalar(sc1[0:1, 424:427], sc1[0:1, 420:423], 2688.0, None, OP.mult)
            v.drain()
            v.tensor_scalar(sc1[0:1, 428:431], sc1[0:1, 424:427], 1.0 / 6.0, None, OP.mult)
            v.drain()
            v.reciprocal(sc1[0:1, 432:434], sc1[0:1, 400:402])
            v.drain()
            v.tensor_scalar(sc1[0:1, 436:439], sc1[0:1, 416:419], 1.0 / 2688.0, None, OP.mult)
            v.drain()
            v.tensor_scalar(sc1[0:1, 436:438], sc1[0:1, 436:438], sc1[0:1, 432:433], None, OP.mult)
            v.drain()
            v.tensor_scalar(sc1[0:1, 438:439], sc1[0:1, 438:439], sc1[0:1, 433:434], None, OP.mult)
            v.drain()
            v.tensor_scalar(sc1[0:1, 440:441], sc1[0:1, 401:402], 1.0 / 6.0, None, OP.mult)
            v.drain().then_inc(*p.inc("wqv2", 1))

        @blk.scalar
        def _(se):
            p.gate(se)
            p.wge(se, "wqv2", 1)
            se.dma_start(out=S["row"][3:4, 416:448], in_=sc1[0:1, 416:448]).then_inc(*p.inc("wqs4", 16))

        @blk.sync
        def _(sync):
            p.gate(sync)
            p.wge(sync, "wqs4", 16)
            for (col, off) in [(C_GWG, 424), (C_GWU, 425), (C_GWD, 426),
                               (C_GWG + 16, 428), (C_GWU + 16, 429), (C_GWD + 16, 430),
                               (C_SGA, 436), (C_SUA, 437), (C_SDA, 438), (C_SIN6D, 440)]:
                sync.dma_start(out=cst[:, col:col + 1],
                               in_=S["row"][3:4, off:off + 1].partition_broadcast(128)).then_inc(*p.inc("wqb", 16))

    wq0stack.close()
    p.mark("wqscales", "wqb")
    p.endphase("wqscales")

    G = 4
    calls = []
    for (srcn, dstn, R, C, cg) in WQ:
        ch = [(srcn, dstn, o, C, cg) for o in range(0, R, 128)]
        for gs in range(0, len(ch), G):
            calls.append(ch[gs:gs + G])
    elad, cdone, exps, absg = {}, {}, {}, {}
    vc = sec = 0
    qglob = 0
    for g, grp in enumerate(calls):
        vc += 1
        elad[g] = vc
        sec += 1
        exps[g] = sec
        for k in range(len(grp)):
            sec += 1
            absg[qglob + k] = sec
            vc += 1
            cdone[qglob + k] = vc
        qglob += len(grp)
    ncalls = qglob

    with nc.named_scope("wq1"), nc.Block("wq1") as blk:
        with contextlib.ExitStack() as ph:
            xb = ph.enter_context(nc.sbuf_tensor("wq1_xb", [128, G * 4096], F32))
            ab = ph.enter_context(nc.sbuf_tensor("wq1_ab", [128, 4096], F32))
            sg = ph.enter_context(nc.sbuf_tensor("wq1_sg", [128, 4096], F32))
            qa = ph.enter_context(nc.sbuf_tensor("wq1_qa", [128, 4096], F32))
            qb = ph.enter_context(nc.sbuf_tensor("wq1_qb", [128, 4096], F32))
            qsl = ph.enter_context(nc.sbuf_tensor("wq1_qs", [128, 8 * G * 256], F32))
            uo = ph.enter_context(nc.sbuf_tensor("wq1_u", [128, 2 * 4096], BF16))

            @blk.sync
            def _(sync):
                p.gate(sync)
                p.wait(sync, "wqscales")
                q = 0
                for grp in calls:
                    for k, (srcn, dstn, o, C, cg) in enumerate(grp):
                        if q >= G:
                            p.wge(sync, "wq1se", absg[q - G])
                        sync.dma_start(out=xb[:, (q % G) * 4096:(q % G) * 4096 + C],
                                       in_=I[srcn][o:o + 128, :]).then_inc(*p.inc("wq1ld%d" % (q % 4), 16))
                        q += 1

            @blk.vector
            def _(v):
                p.gate(v)
                q = 0
                for g, grp in enumerate(calls):
                    C = grp[0][3]
                    NBK = C // 16
                    ng = len(grp)
                    cg = grp[0][4]
                    for k in range(ng):
                        p.wge(v, "wq1ld%d" % ((q + k) % 4), 16 * ((q + k) // 4 + 1))
                        _q_stageA(v, qsl, G, NBK, k, xb[:, ((q + k) % G) * 4096:((q + k) % G) * 4096 + C], C)
                    _q_stageB_v1(v, qsl, G, NBK, ng, cst[:, cg + 16:cg + 17])
                    p.inc("wq1v", 1)
                    v.drain().then_inc(p.s("wq1v"), 1)
                    p.wge(v, "wq1se", exps[g])
                    _q_stageB_v2(v, qsl, G, NBK, ng, cst[:, cg:cg + 1])
                    for k in range(ng):
                        p.wge(v, "wq1se", absg[q + k])
                        if q + k >= 2:
                            p.wge(v, "wq1st%d" % ((q + k) % 2), 16 * ((q + k) // 2))
                        _q_stageC_v(v, qsl, G, NBK, k, ab, sg, qa, qb,
                                    uo[:, ((q + k) % 2) * 4096:((q + k) % 2) * 4096 + C], C)
                        p.inc("wq1v", 1)
                        v.drain().then_inc(p.s("wq1v"), 1)
                    q += ng

            @blk.scalar
            def _(se):
                p.gate(se)
                q = 0
                for g, grp in enumerate(calls):
                    C = grp[0][3]
                    NBK = C // 16
                    ng = len(grp)
                    p.wge(se, "wq1v", elad[g])
                    _q_stageB_se(se, qsl, G, NBK, ng, cst[:, 14:15], cst[:, 15:16])
                    p.inc("wq1se", 1)
                    se.drain().then_inc(p.s("wq1se"), 1)
                    for k in range(ng):
                        if q + k >= 1:
                            p.wge(se, "wq1v", cdone[q + k - 1])
                        _q_stageC_se(se, xb[:, ((q + k) % G) * 4096:((q + k) % G) * 4096 + C], ab, sg, C)
                        p.inc("wq1se", 1)
                        se.drain().then_inc(p.s("wq1se"), 1)
                    q += ng

            @blk.gpsimd
            def _(g_):
                p.gate(g_)
                q = 0
                for grp in calls:
                    for (srcn, dstn, o, C, cg) in grp:
                        p.wge(g_, "wq1v", cdone[q])
                        g_.dma_start(out=S[dstn][o:o + 128, :],
                                     in_=uo[:, (q % 2) * 4096:(q % 2) * 4096 + C]).then_inc(*p.inc("wq1st%d" % (q % 2), 16))
                        q += 1

    p.mark("wq_a", "wq1st0")
    p.mark("wq_b", "wq1st1")
    p.endphase("wq_a", "wq_b")


def _emit_attn(nc, p, stack, I, S, sb, ps, cst, b_tile, a_mat, wts, W, NB, C_OS):
    qtiles = []
    t0 = 0
    while t0 < QW:
        qt = min(512, QW - t0)
        qtiles.append((t0, qt))
        t0 += qt

    for qi, (t0, qt) in enumerate(qtiles):
        NBq = (W + qt) // 128
        NHG = (NBq + 3) // 4
        ntc = qt // 128
        tag = "at%d" % qi
        with nc.named_scope(tag), nc.Block(tag) as blk:
            with contextlib.ExitStack() as ph:
                sw_hi = ph.enter_context(nc.sbuf_tensor(tag + "_swh", [128, NBq * 512], BF16))
                sw_lo = ph.enter_context(nc.sbuf_tensor(tag + "_swl", [128, NBq * 512], BF16))
                kf = [[ph.enter_context(nc.sbuf_tensor(tag + "_kf%d%s" % (i, h), [128, 8 * 512], BF16))
                       for h in "hl"] for i in range(2)]
                qpf = [ph.enter_context(nc.sbuf_tensor(tag + "_qp%s" % h, [128, 8 * 512], BF16))
                       for h in "hl"]
                vb = [ph.enter_context(nc.sbuf_tensor(tag + "_vb%s" % h, [128, NBq * 512], BF16))
                      for h in "hl"]
                sw32 = ph.enter_context(nc.sbuf_tensor(tag + "_s32", [128, 512], F32))
                xre = ph.enter_context(nc.sbuf_tensor(tag + "_xre", [128, 2 * 512], F32))
                x1t = ph.enter_context(nc.sbuf_tensor(tag + "_x1t", [128, 2 * 512], F32))

                KLD, QLD, VLD, SC, SWP, SW, RT, EV, XLD, ST = [tag + s for s in
                    ("kld", "qld", "vld", "sc", "swp", "sw", "rt", "ev", "xld", "st")]

                @blk.sync
                def _(sync):
                    p.gate(sync)
                    if qi == 0:
                        p.wait(sync, "ln0_a")
                        p.wait(sync, "ln0_b")
                        p.wait(sync, "c0")
                    else:
                        p.wait(sync, "at%d_a" % (qi - 1))
                        p.wait(sync, "at%d_b" % (qi - 1))
                    for hn, dst in (("qp_hi", qpf[0]), ("qp_lo", qpf[1])):
                        for vc in range(8):
                            sync.dma_start(out=dst[:, vc * 512:vc * 512 + qt],
                                           in_=S[hn][t0:t0 + qt, vc * 128:(vc + 1) * 128],
                                           transpose=True).then_inc(*p.inc(QLD, 16))
                    for hg in range(NHG):
                        if hg >= 2:
                            p.wge(sync, SC, 4 * (hg - 1))
                        s0 = t0 + 512 * hg
                        nsb = min(512, (t0 + NBq * 128) - s0)
                        for hn, dst in (("xn_hi", kf[hg % 2][0]), ("xn_lo", kf[hg % 2][1])):
                            for vc in range(8):
                                sync.dma_start(out=dst[:, vc * 512:vc * 512 + nsb],
                                               in_=S[hn][s0:s0 + nsb, vc * 128:(vc + 1) * 128],
                                               transpose=True).then_inc(*p.inc(KLD + str(hg % 2), 16))
                    for vh in range(2):
                        if vh == 1:
                            p.wge(sync, RT, ntc)
                        for hn, dst in (("xn_hi", vb[0]), ("xn_lo", vb[1])):
                            sync.dma_start(
                                out=dst[:, :].rearrange("p (j v) -> p j v", v=512),
                                in_=S[hn][t0:t0 + NBq * 128, vh * 512:(vh + 1) * 512]
                                    .rearrange("(j p) v -> p j v", p=128),
                            ).then_inc(*p.inc(VLD, 16))
                        for tc in range(ntc):
                            idx = vh * ntc + tc
                            if idx >= 2:
                                p.wge(sync, EV, idx - 1)
                            sync.dma_start(out=xre[:, (idx % 2) * 512:(idx % 2) * 512 + 512],
                                           in_=I["x"][t0 + tc * 128:t0 + tc * 128 + 128,
                                                      vh * 512:(vh + 1) * 512]).then_inc(*p.inc(XLD + str(idx % 2), 16))

                @blk.tensor
                def _(tensor):
                    p.gate(tensor)
                    for j in range(NBq):
                        hg, jj = j // 4, j % 4
                        p.wge(tensor, KLD + str(hg % 2), 16 * 16 * (hg // 2 + 1))
                        if j == 0:
                            p.wge(tensor, QLD, 16 * 16)
                        if j >= 2:
                            p.wge(tensor, SWP, j - 1)
                        pb = ps[j % 2]
                        nmm = 0
                        for (kb, qb) in ((0, 0), (0, 1), (1, 0)):
                            for vc in range(8):
                                nmm += 1
                                mm = tensor.matmul(pb[:, 0:qt],
                                                   kf[hg % 2][kb][:, vc * 512 + jj * 128:vc * 512 + jj * 128 + 128],
                                                   qpf[qb][:, vc * 512:vc * 512 + qt],
                                                   start=(nmm == 1), stop=(nmm == 24))
                        mm.then_inc(*p.inc(SC, 1))
                    for vh in range(2):
                        p.wge(tensor, VLD, 32 * (vh + 1))
                        for tc in range(ntc):
                            if vh == 1:
                                p.wge(tensor, EV, ntc + tc)
                            pr = ps[2 + tc]
                            nmm = 0
                            nj = min(tc + W // 128 + 1, NBq)
                            if vh == 0:
                                p.wge(tensor, SW, nj)
                            for j in range(tc, nj):
                                for (sb_, vbb) in ((0, 0), (0, 1), (1, 0)):
                                    nmm += 1
                                    src_sw = sw_hi if sb_ == 0 else sw_lo
                                    mm = tensor.matmul(
                                        pr[:, 0:512],
                                        src_sw[:, j * 512 + tc * 128:j * 512 + tc * 128 + 128],
                                        vb[vbb][:, j * 512:(j + 1) * 512],
                                        start=(nmm == 1), stop=(nmm == 3 * (nj - tc)))
                            mm.then_inc(*p.inc(RT, 1))

                @blk.vector
                def _(v):
                    p.gate(v)
                    if qi == 0:
                        p.wait(v, "c0")
                    for j in range(NBq):
                        p.wge(v, SC, j + 1)
                        pb = ps[j % 2]
                        if j <= 3:
                            v.tensor_tensor(sw32[:, 0:qt], pb[:, 0:qt], wts[j][:, 0:qt], OP.mult)
                        else:
                            v.scalar_tensor_tensor(sw32[:, 0:qt], pb[:, 0:qt], a_mat[:, j:j + 1],
                                                   b_tile[:, 0:qt], OP.mult, OP.mult)
                        p.inc(SWP, 1)
                        v.drain().then_inc(p.s(SWP), 1)
                        v.tensor_copy(sw_hi[:, j * 512:j * 512 + qt], sw32[:, 0:qt])
                        v.drain()
                        v.tensor_tensor(sw_lo[:, j * 512:j * 512 + qt], sw32[:, 0:qt],
                                        sw_hi[:, j * 512:j * 512 + qt], OP.subtract)
                        p.inc(SW, 1)
                        v.drain().then_inc(p.s(SW), 1)
                    for vh in range(2):
                        for tc in range(ntc):
                            idx = vh * ntc + tc
                            p.wge(v, RT, idx + 1)
                            p.wge(v, XLD + str(idx % 2), 16 * (idx // 2 + 1))
                            if idx >= 2:
                                p.wge(v, ST + str(idx % 2), 16 * (idx // 2))
                            v.scalar_tensor_tensor(x1t[:, (idx % 2) * 512:(idx % 2) * 512 + 512],
                                                   ps[2 + tc][:, 0:512], cst[:, C_OS:C_OS + 1],
                                                   xre[:, (idx % 2) * 512:(idx % 2) * 512 + 512],
                                                   OP.mult, OP.add)
                            p.inc(EV, 1)
                            v.drain().then_inc(p.s(EV), 1)

                @blk.scalar
                def _(se):
                    p.gate(se)
                    for i in range(2 * ntc):
                        p.wge(se, EV, i + 1)
                        vh, tc = i // ntc, i % ntc
                        se.dma_start(out=S["x1"][t0 + tc * 128:t0 + tc * 128 + 128,
                                                 vh * 512:(vh + 1) * 512],
                                     in_=x1t[:, (i % 2) * 512:(i % 2) * 512 + 512]
                                     ).then_inc(*p.inc(ST + str(i % 2), 16))

        p.mark("at%d_a" % qi, ST + "0")
        p.mark("at%d_b" % qi, ST + "1")
        p.endphase("at%d_a" % qi, "at%d_b" % qi)
    p.mark("p2a", "at%dst0" % (len(qtiles) - 1))
    p.mark("p2b", "at%dst1" % (len(qtiles) - 1))


def _emit_ffn0(nc, p, stack, I, S, sb, ps, cst):
    with nc.named_scope("f0a"), nc.Block("f0a") as blk:
        with contextlib.ExitStack() as ph:
            xnf = [ph.enter_context(nc.sbuf_tensor("f0_xn%s" % h, [128, 8 * QW], BF16)) for h in "hl"]
            wgf = [ph.enter_context(nc.sbuf_tensor("f0_wg%d" % i, [128, 8 * 512], BF16)) for i in range(2)]
            wuf = [ph.enter_context(nc.sbuf_tensor("f0_wu%d" % i, [128, 8 * 512], BF16)) for i in range(2)]
            sg = ph.enter_context(nc.sbuf_tensor("f0_sg", [128, 2 * 512], F32))
            hh = ph.enter_context(nc.sbuf_tensor("f0_hh", [128, 2 * 512], F32))
            hhh = ph.enter_context(nc.sbuf_tensor("f0_hhh", [128, 2 * 512], BF16))
            hhl = ph.enter_context(nc.sbuf_tensor("f0_hhl", [128, 2 * 512], BF16))
            NHS = HFF0 // 512

            @blk.sync
            def _(sync):
                p.gate(sync)
                p.wait(sync, "p2a")
                p.wait(sync, "p2b")
                p.wait(sync, "w0c")
                for hn, dst in (("xn_hi", xnf[0]), ("xn_lo", xnf[1])):
                    for vc in range(8):
                        sync.dma_start(out=dst[:, vc * QW:(vc + 1) * QW],
                                       in_=S[hn][0:QW, vc * 128:(vc + 1) * 128],
                                       transpose=True).then_inc(*p.inc("f0xld", 16))
                for hs_ in range(NHS):
                    if hs_ >= 2:
                        p.wge(sync, "f0mm", 2 * NTC * (hs_ - 1))
                    for wn, dst in (("wg_b", wgf[hs_ % 2]), ("wu_b", wuf[hs_ % 2])):
                        for vc in range(8):
                            sync.dma_start(out=dst[:, vc * 512:(vc + 1) * 512],
                                           in_=S[wn][hs_ * 512:(hs_ + 1) * 512, vc * 128:(vc + 1) * 128],
                                           transpose=True).then_inc(*p.inc("f0wld%d" % (hs_ % 2), 16))

            @blk.tensor
            def _(tensor):
                p.gate(tensor)
                p.wge(tensor, "f0xld", 16 * 16)
                for hs_ in range(NHS):
                    p.wge(tensor, "f0wld%d" % (hs_ % 2), 16 * 16 * (hs_ // 2 + 1))
                    for tc in range(NTC):
                        idx = hs_ * NTC + tc
                        if idx >= 2:
                            p.wge(tensor, "f0hv", idx - 1)
                            p.wge(tensor, "f0se", idx - 1)
                        pg, pu = ps[2 * (idx % 2)], ps[2 * (idx % 2) + 1]
                        for (dst, wf) in ((pg, wgf[hs_ % 2]), (pu, wuf[hs_ % 2])):
                            nmm = 0
                            for hb in range(2):
                                xf = xnf[hb]
                                for vc in range(8):
                                    nmm += 1
                                    mm = tensor.matmul(dst[:, 0:512],
                                                       xf[:, vc * QW + tc * 128:vc * QW + tc * 128 + 128],
                                                       wf[:, vc * 512:(vc + 1) * 512],
                                                       start=(nmm == 1), stop=(nmm == 16))
                            mm.then_inc(*p.inc("f0mm", 1))

            @blk.scalar
            def _(se):
                p.gate(se)
                for hs_ in range(NHS):
                    for tc in range(NTC):
                        idx = hs_ * NTC + tc
                        p.wge(se, "f0mm", 2 * idx + 1)
                        se.activation(out=sg[:, (idx % 2) * 512:(idx % 2) * 512 + 512],
                                      in_=ps[2 * (idx % 2)][:, 0:512], func=AF.Silu,
                                      scale=FF0S)
                        se.drain().then_inc(p.s("f0se"), 1)
                        p.inc("f0se", 1)
                        p.wge(se, "f0hv", idx + 1)
                        se.dma_start(out=S["hh_hi"][tc * 128:tc * 128 + 128, hs_ * 512:(hs_ + 1) * 512],
                                     in_=hhh[:, (idx % 2) * 512:(idx % 2) * 512 + 512]).then_inc(*p.inc("f0st%d" % (idx % 2), 16))
                        se.dma_start(out=S["hh_lo"][tc * 128:tc * 128 + 128, hs_ * 512:(hs_ + 1) * 512],
                                     in_=hhl[:, (idx % 2) * 512:(idx % 2) * 512 + 512]).then_inc(*p.inc("f0st%d" % (idx % 2), 16))

            @blk.vector
            def _(v):
                p.gate(v)
                for hs_ in range(NHS):
                    for tc in range(NTC):
                        idx = hs_ * NTC + tc
                        p.wge(v, "f0mm", 2 * idx + 2)
                        p.wge(v, "f0se", idx + 1)
                        if idx >= 2:
                            p.wge(v, "f0st%d" % (idx % 2), 32 * (idx // 2))
                        b = (idx % 2) * 512
                        v.scalar_tensor_tensor(hh[:, b:b + 512], ps[2 * (idx % 2) + 1][:, 0:512],
                                               FF0S, sg[:, b:b + 512], OP.mult, OP.mult)
                        v.drain()
                        v.tensor_copy(hhh[:, b:b + 512], hh[:, b:b + 512])
                        v.drain()
                        v.tensor_tensor(hhl[:, b:b + 512], hh[:, b:b + 512], hhh[:, b:b + 512],
                                        OP.subtract)
                        p.inc("f0hv", 1)
                        v.drain().then_inc(p.s("f0hv"), 1)

    p.mark("f0hh_a", "f0st0")
    p.mark("f0hh_b", "f0st1")
    p.endphase("f0hh_a", "f0hh_b")

    with nc.named_scope("f0b"), nc.Block("f0b") as blk:
        with contextlib.ExitStack() as ph:
            hhf = [[ph.enter_context(nc.sbuf_tensor("f0b_h%d%s" % (i, h), [128, QW], BF16))
                    for h in "hl"] for i in range(12)]
            wdf = [ph.enter_context(nc.sbuf_tensor("f0b_w%d" % i, [128, 1024], BF16))
                   for i in range(12)]
            x1t = ph.enter_context(nc.sbuf_tensor("f0b_x1", [128, 2 * 512], F32))
            x1o = ph.enter_context(nc.sbuf_tensor("f0b_x1o", [128, 2 * 512], F32))

            for half in range(2):
                HB = half * 12
                tagl, tagm, tage, tags = ("f0b%dld" % half, "f0b%dmm" % half,
                                          "f0b%dev" % half, "f0b%dst" % half)

                @blk.sync
                def _(sync, half=half, HB=HB, tagl=tagl, tags=tags):
                    p.gate(sync)
                    if half == 0:
                        p.wait(sync, "f0hh_a")
                        p.wait(sync, "f0hh_b")
                    else:
                        p.wge(sync, "f0b0st0", 16 * NTC)
                        p.wge(sync, "f0b0st1", 16 * NTC)
                    for i in range(12):
                        hc = HB + i
                        for hn, dst in (("hh_hi", hhf[i][0]), ("hh_lo", hhf[i][1])):
                            sync.dma_start(out=dst[:, 0:QW],
                                           in_=S[hn][0:QW, hc * 128:(hc + 1) * 128],
                                           transpose=True).then_inc(*p.inc(tagl, 16))
                        sync.dma_start(out=wdf[i][:, :],
                                       in_=S["wd_b"][0:D, hc * 128:(hc + 1) * 128],
                                       transpose=True).then_inc(*p.inc(tagl, 16))
                    for tc in range(NTC):
                        for dh in range(2):
                            idx = tc * 2 + dh
                            if idx >= 2:
                                p.wge(sync, tage, idx - 1)
                            sync.dma_start(out=x1t[:, (idx % 2) * 512:(idx % 2) * 512 + 512],
                                           in_=S["x1"][tc * 128:tc * 128 + 128,
                                                       dh * 512:(dh + 1) * 512]).then_inc(*p.inc(tagl + "x%d" % (idx % 2), 16))

                @blk.tensor
                def _(tensor, half=half, tagl=tagl, tagm=tagm, tage=tage):
                    p.gate(tensor)
                    p.wge(tensor, tagl, 16 * 36)
                    for tc in range(NTC):
                        for dh in range(2):
                            idx = tc * 2 + dh
                            if idx >= 4:
                                p.wge(tensor, tage, idx - 3)
                            pr = ps[4 + (idx % 4)]
                            nmm = 0
                            for i in range(12):
                                for hb in range(2):
                                    nmm += 1
                                    mm = tensor.matmul(pr[:, 0:512],
                                                       hhf[i][hb][:, tc * 128:tc * 128 + 128],
                                                       wdf[i][:, dh * 512:(dh + 1) * 512],
                                                       start=(nmm == 1), stop=(nmm == 24))
                            mm.then_inc(*p.inc(tagm, 1))

                @blk.vector
                def _(v, tagm=tagm, tage=tage, tagl=tagl, tags=tags):
                    p.gate(v)
                    for tc in range(NTC):
                        for dh in range(2):
                            idx = tc * 2 + dh
                            p.wge(v, tagm, idx + 1)
                            p.wge(v, tagl + "x%d" % (idx % 2), 16 * (idx // 2 + 1))
                            if idx >= 2:
                                p.wge(v, tags + str(idx % 2), 16 * (idx // 2))
                            b = (idx % 2) * 512
                            v.scalar_tensor_tensor(x1o[:, b:b + 512], ps[4 + (idx % 4)][:, 0:512],
                                                   FF0S, x1t[:, b:b + 512], OP.mult, OP.add)
                            p.inc(tage, 1)
                            v.drain().then_inc(p.s(tage), 1)

                @blk.scalar
                def _(se, tage=tage, tags=tags, half=half):
                    p.gate(se)
                    for tc in range(NTC):
                        for dh in range(2):
                            idx = tc * 2 + dh
                            p.wge(se, tage, idx + 1)
                            se.dma_start(out=S["x1"][tc * 128:tc * 128 + 128, dh * 512:(dh + 1) * 512],
                                         in_=x1o[:, (idx % 2) * 512:(idx % 2) * 512 + 512]
                                         ).then_inc(*p.inc("f0b%dst%d" % (half, idx % 2), 16))

            p.mark("f0bh%d_a" % half, "f0b%dst0" % half)
            p.mark("f0bh%d_b" % half, "f0b%dst1" % half)
            p.endphase("f0bh%d_a" % half, "f0bh%d_b" % half)

    p.mark("p3_a", "f0b1st0")
    p.mark("p3_b", "f0b1st1")
    p.endphase("p3_a", "p3_b")


MF_TT = [(0, 512), (512, 512), (1024, 512), (1536, 512), (2048, 384)]


def _emit_mamba_front(nc, p, stack, I, S, sb, ps, cst, convw, convb, bdtn):
    with nc.named_scope("mfa"), nc.Block("mfa") as blk:
        with contextlib.ExitStack() as ph:
            xnf = [ph.enter_context(nc.sbuf_tensor("mf_xn%s" % h, [128, 8 * QW], BF16)) for h in "hl"]
            wif = [[ph.enter_context(nc.sbuf_tensor("mf_wi%d%s" % (i, h), [128, 8 * 512], BF16))
                    for h in "hl"] for i in range(2)]
            xcp = [ph.enter_context(nc.sbuf_tensor("mf_xcp%d" % i, [128, 8 + QW], F32)) for i in range(4)]
            xct = ph.enter_context(nc.sbuf_tensor("mf_xct", [128, QW], F32))
            xch = ph.enter_context(nc.sbuf_tensor("mf_xch", [128, QW], BF16))
            xcl = ph.enter_context(nc.sbuf_tensor("mf_xcl", [128, QW], BF16))
            szt = ph.enter_context(nc.sbuf_tensor("mf_sz", [128, 2 * 512], F32))
            umb = ph.enter_context(nc.sbuf_tensor("mf_um", [128, QW], F32))

            @blk.sync
            def _(sync):
                p.gate(sync)
                p.wait(sync, "ln1_a")
                p.wait(sync, "ln1_b")
                p.wait(sync, "w0_a")
                p.wait(sync, "w0_b")
                sync.dma_start(out=umb[:, :],
                               in_=I["umask"][0:1, :].partition_broadcast(128)).then_inc(*p.inc("mfum", 16))
                for hn, dst in (("xn1_hi", xnf[0]), ("xn1_lo", xnf[1])):
                    for vc in range(8):
                        sync.dma_start(out=dst[:, vc * QW:(vc + 1) * QW],
                                       in_=S[hn][0:QW, vc * 128:(vc + 1) * 128],
                                       transpose=True).then_inc(*p.inc("mfxld", 16))
                for grp in range(8):
                    if grp >= 2:
                        p.wge(sync, "mfmm", 20 * (grp - 1))
                    for hn, dst in (("win_hi", wif[grp % 2][0]), ("win_lo", wif[grp % 2][1])):
                        for vc in range(8):
                            sync.dma_start(out=dst[:, vc * 512:(vc + 1) * 512],
                                           in_=S[hn][grp * 512:(grp + 1) * 512, vc * 128:(vc + 1) * 128],
                                           transpose=True).then_inc(*p.inc("mfwld%d" % (grp % 2), 16))

            @blk.tensor
            def _(tensor):
                p.gate(tensor)
                p.wge(tensor, "mfxld", 16 * 16)
                for grp in range(8):
                    p.wge(tensor, "mfwld%d" % (grp % 2), 16 * 16 * (grp // 2 + 1))
                    for gi in range(4):
                        for ti, (tt0, tl) in enumerate(MF_TT):
                            idx = (grp * 4 + gi) * 5 + ti
                            if idx >= 4:
                                p.wge(tensor, "mfev", idx - 3)
                            pb = ps[idx % 4]
                            nmm = 0
                            for (kb, qb) in ((0, 0), (0, 1), (1, 0)):
                                for vc in range(8):
                                    nmm += 1
                                    mm = tensor.matmul(
                                        pb[:, 0:tl],
                                        wif[grp % 2][kb][:, vc * 512 + gi * 128:vc * 512 + gi * 128 + 128],
                                        xnf[qb][:, vc * QW + tt0:vc * QW + tt0 + tl],
                                        start=(nmm == 1), stop=(nmm == 24))
                            mm.then_inc(*p.inc("mfmm", 1))

            @blk.vector
            def _(v):
                p.gate(v)
                p.wge(v, "mfum", 16)
                for grp in range(8):
                    for gi in range(4):
                        dc = grp * 4 + gi
                        is_xc = dc < 16
                        if is_xc:
                            v.memset(xcp[gi][:, 0:8], 0.0)
                        for ti, (tt0, tl) in enumerate(MF_TT):
                            idx = dc * 5 + ti
                            p.wge(v, "mfmm", idx + 1)
                            if is_xc:
                                v.drain()
                                v.tensor_copy(xcp[gi][:, 3 + tt0:3 + tt0 + tl], ps[idx % 4][:, 0:tl])
                                p.inc("mfev", 1)
                                v.drain().then_inc(p.s("mfev"), 1)
                            else:
                                p.wge(v, "mfsz", (dc - 16) * 5 + ti + 1)
                                p.inc("mfev", 1)
                                v.nop().then_inc(p.s("mfev"), 1)
                        if is_xc:
                            p.wge(v, "mfst", 48 * dc)
                            v.tensor_scalar(xct[:, :], xcp[gi][:, 0:QW], convw[:, dc * 4:dc * 4 + 1],
                                            None, OP.mult)
                            for k in range(1, 4):
                                v.drain()
                                v.scalar_tensor_tensor(xct[:, :], xcp[gi][:, k:k + QW],
                                                       convw[:, dc * 4 + k:dc * 4 + k + 1],
                                                       xct[:, :], OP.mult, OP.add)
                            p.inc("mfcv", 1)
                            v.drain().then_inc(p.s("mfcv"), 1)
                            p.wge(v, "mfsil", dc + 1)
                            v.tensor_tensor(xct[:, :], xct[:, :], umb[:, :], OP.mult)
                            v.drain()
                            v.tensor_copy(xch[:, :], xct[:, :])
                            v.drain()
                            v.tensor_tensor(xcl[:, :], xct[:, :], xch[:, :], OP.subtract)
                            p.inc("mfxc", 1)
                            v.drain().then_inc(p.s("mfxc"), 1)

            @blk.scalar
            def _(se):
                p.gate(se)
                nsz = 0
                for grp in range(8):
                    for gi in range(4):
                        dc = grp * 4 + gi
                        is_xc = dc < 16
                        if is_xc:
                            p.wge(se, "mfcv", dc + 1)
                            se.activation(out=xct[:, :], in_=xct[:, :], func=AF.Silu,
                                          bias=convb[:, dc:dc + 1])
                            se.drain().then_inc(p.s("mfsil"), 1)
                            p.inc("mfsil", 1)
                            p.wge(se, "mfxc", dc + 1)
                            se.dma_start(out=S["xc"][dc * 128:(dc + 1) * 128, :],
                                         in_=xct[:, :]).then_inc(*p.inc("mfst", 16))
                            se.dma_start(out=S["xc_hi"][dc * 128:(dc + 1) * 128, :],
                                         in_=xch[:, :]).then_inc(*p.inc("mfst", 16))
                            se.dma_start(out=S["xc_lo"][dc * 128:(dc + 1) * 128, :],
                                         in_=xcl[:, :]).then_inc(*p.inc("mfst", 16))
                        else:
                            zc = dc - 16
                            for ti, (tt0, tl) in enumerate(MF_TT):
                                idx = dc * 5 + ti
                                o0, o1 = max(tt0, WARM), tt0 + tl
                                p.wge(se, "mfmm", idx + 1)
                                if o1 > o0:
                                    b = (nsz % 2) * 512
                                    if nsz >= 2:
                                        p.wge(se, "mfzst%d" % (nsz % 2), 16 * (nsz // 2))
                                    se.activation(out=szt[:, b:b + (o1 - o0)],
                                                  in_=ps[idx % 4][:, o0 - tt0:o1 - tt0], func=AF.Silu)
                                    se.drain()
                                    se.dma_start(out=S["siluz"][zc * 128:(zc + 1) * 128, o0 - WARM:o1 - WARM],
                                                 in_=szt[:, b:b + (o1 - o0)]).then_inc(*p.inc("mfzst%d" % (nsz % 2), 16))
                                    nsz += 1
                                se.drain().then_inc(p.s("mfsz"), 1)
                                p.inc("mfsz", 1)

    p.mark("mfa", "mfst")
    p.mark("mfaz", "mfzst0")
    p.mark("mfaz2", "mfzst1")
    p.endphase("mfa", "mfaz", "mfaz2")

    ph2 = contextlib.ExitStack()
    drh = ph2.enter_context(nc.sbuf_tensor("mf_drh", [64, QW], BF16))
    drl = ph2.enter_context(nc.sbuf_tensor("mf_drl", [64, QW], BF16))
    with nc.named_scope("mfb"), nc.Block("mfb") as blk:
        with contextlib.ExitStack() as ph:
            wxf = [ph.enter_context(nc.sbuf_tensor("mf_wx%s" % h, [128, 16 * 96], BF16)) for h in "hl"]
            xcf = [[ph.enter_context(nc.sbuf_tensor("mf_xcf%d%s" % (i, h), [128, QW], BF16))
                    for h in "hl"] for i in range(2)]
            dr3 = ph.enter_context(nc.sbuf_tensor("mf_dr3", [96, QW], F32))

            @blk.sync
            def _(sync):
                p.gate(sync)
                p.wait(sync, "mfa")
                for hn, dst in (("wx_hi", wxf[0]), ("wx_lo", wxf[1])):
                    for dc in range(16):
                        sync.dma_start(out=dst[:, dc * 96:(dc + 1) * 96],
                                       in_=S[hn][dc * 128:(dc + 1) * 128, 0:96]
                                       ).then_inc(*p.inc("mbwld", 16))
                for dc in range(16):
                    if dc >= 2:
                        p.wge(sync, "mbmm", 5 * (dc - 1))
                    for hn, dst in (("xc_hi", xcf[dc % 2][0]), ("xc_lo", xcf[dc % 2][1])):
                        sync.dma_start(out=dst[:, :],
                                       in_=S[hn][dc * 128:(dc + 1) * 128, :]).then_inc(*p.inc("mbxld%d" % (dc % 2), 16))

            @blk.tensor
            def _(tensor):
                p.gate(tensor)
                p.wge(tensor, "mbwld", 16 * 32)
                for dc in range(16):
                    p.wge(tensor, "mbxld%d" % (dc % 2), 32 * (dc // 2 + 1))
                    for ti, (tt0, tl) in enumerate(MF_TT):
                        nmm = 0
                        for (kb, qb) in ((0, 0), (0, 1), (1, 0)):
                            nmm += 1
                            mm = tensor.matmul(ps[ti][0:96, 0:tl],
                                               wxf[kb][:, dc * 96:(dc + 1) * 96],
                                               xcf[dc % 2][qb][:, tt0:tt0 + tl],
                                               start=(dc == 0 and nmm == 1),
                                               stop=(dc == 15 and nmm == 3))
                        mm.then_inc(*p.inc("mbmm", 1))

            @blk.vector
            def _(v):
                p.gate(v)
                p.wge(v, "mbmm", 5 * 16)
                for ti, (tt0, tl) in enumerate(MF_TT):
                    v.tensor_copy(dr3[0:96, tt0:tt0 + tl], ps[ti][0:96, 0:tl])
                v.drain()
                v.tensor_copy(drh[:, :], dr3[0:64, :])
                v.drain()
                v.tensor_tensor(drl[:, :], dr3[0:64, :], drh[:, :], OP.subtract)
                v.drain()
                v.tensor_scalar(dr3[64:80, :], dr3[64:80, :], -1.0, None, OP.mult)
                v.drain().then_inc(p.s("mbev"), 1)
                p.inc("mbev", 1)

            @blk.scalar
            def _(se):
                p.gate(se)
                p.wge(se, "mbev", 1)
                se.dma_start(out=S["bc"][0:32, :], in_=dr3[64:96, :]).then_inc(*p.inc("mbst", 16))

    p.mark("mfb", "mbst")
    p.endphase("mfb")

    with nc.named_scope("mfc"), nc.Block("mfc") as blk:
        with contextlib.ExitStack() as ph:
            wdtf = [ph.enter_context(nc.sbuf_tensor("mf_wdt%s" % h, [64, 16 * 128], BF16)) for h in "hl"]
            sgt = [ph.enter_context(nc.sbuf_tensor("mf_sgt%d" % i, [128, 512], F32)) for i in range(2)]
            lnt = [ph.enter_context(nc.sbuf_tensor("mf_lnt%d" % i, [128, 512], F32)) for i in range(2)]

            @blk.sync
            def _(sync):
                p.gate(sync)
                p.wait(sync, "mfb")
                for hn, dst in (("wdt_hi", wdtf[0]), ("wdt_lo", wdtf[1])):
                    sync.dma_start(out=dst[:, 0:DIN],
                                   in_=S[hn][0:DTR, 0:DIN]).then_inc(*p.inc("mcwld", 16))

            @blk.tensor
            def _(tensor):
                p.gate(tensor)
                p.wge(tensor, "mcwld", 16 * 2)
                p.wait(tensor, "mfb")
                for dc in range(16):
                    for ti, (tt0, tl) in enumerate(MF_TT):
                        idx = dc * 5 + ti
                        if idx >= 2:
                            p.wge(tensor, "mcsg", idx - 1)
                        pb = ps[5 + idx % 2]
                        nmm = 0
                        for (kb, qb) in ((0, 0), (0, 1), (1, 0)):
                            nmm += 1
                            drq = drh if qb == 0 else drl
                            mm = tensor.matmul(pb[:, 0:tl],
                                               wdtf[kb][0:64, dc * 128:(dc + 1) * 128],
                                               drq[0:64, tt0:tt0 + tl],
                                               start=(nmm == 1), stop=(nmm == 3))
                        mm.then_inc(*p.inc("mcmm", 1))

            @blk.scalar
            def _(se):
                p.gate(se)
                NIDX = 16 * len(MF_TT)

                def lnstore(j):
                    jd, jt = j // 5, j % 5
                    jt0, jtl = MF_TT[jt]
                    p.wge(se, "mcgv", j + 1)
                    if j >= 2:
                        p.wge(se, "mcst%d" % (j % 2), 16 * (j // 2))
                    se.activation(out=lnt[j % 2][:, 0:jtl], in_=sgt[j % 2][:, 0:jtl], func=AF.Ln)
                    se.drain().then_inc(p.s("mcln"), 1)
                    p.inc("mcln", 1)
                    se.dma_start(out=S["lns"][jd * 128:(jd + 1) * 128, jt0:jt0 + jtl],
                                 in_=lnt[j % 2][:, 0:jtl]).then_inc(*p.inc("mcst%d" % (j % 2), 16))

                for dc in range(16):
                    for ti, (tt0, tl) in enumerate(MF_TT):
                        idx = dc * 5 + ti
                        if idx >= 2:
                            p.wge(se, "mcgv", idx - 1)
                        p.wge(se, "mcmm", idx + 1)
                        se.activation(out=sgt[idx % 2][:, 0:tl], in_=ps[5 + idx % 2][:, 0:tl],
                                      func=AF.Sigmoid, scale=-1.0, bias=bdtn[:, dc:dc + 1])
                        se.drain().then_inc(p.s("mcsg"), 1)
                        p.inc("mcsg", 1)
                        if idx >= 1:
                            lnstore(idx - 1)
                lnstore(NIDX - 1)

            @blk.vector
            def _(v):
                p.gate(v)
                for dc in range(16):
                    for ti, (tt0, tl) in enumerate(MF_TT):
                        idx = dc * 5 + ti
                        b = idx % 2
                        p.wge(v, "mcsg", idx + 1)
                        v.tensor_scalar(sgt[b][:, 0:tl], sgt[b][:, 0:tl], 1e-38, None, OP.max)
                        p.inc("mcgv", 1)
                        v.drain().then_inc(p.s("mcgv"), 1)

    ph2.close()
    p.mark("mfc_a", "mcst0")
    p.mark("mfc_b", "mcst1")
    p.endphase("mfc_a", "mfc_b", "mfaz", "mfaz2")


CUBE_SEGS = [(0, 608), (608, 608), (1216, 608), (1824, 608)]
CUBE_SL = 608


def _cseg(j):
    si = j // 16
    s0, sl = CUBE_SEGS[si]
    o0, o1 = max(s0, WARM), s0 + sl
    return si, s0, sl, o0, o1 - o0


def _emit_cube(nc, p, stack, I, S, sb, ps, cst, ealog, state, dpc):
    SL = CUBE_SL
    NG = 16 * len(CUBE_SEGS)
    with nc.named_scope("cb"), nc.Block("cb") as blk:
        with contextlib.ExitStack() as ph:
            Bb = ph.enter_context(nc.sbuf_tensor("cb_B", [128, NST * SL], F32))
            Cb = ph.enter_context(nc.sbuf_tensor("cb_C", [128, NST * SL], F32))
            lnst = [ph.enter_context(nc.sbuf_tensor("cb_ln%d" % i, [128, SL], F32)) for i in range(3)]
            xct = [ph.enter_context(nc.sbuf_tensor("cb_xc%d" % i, [128, SL], F32)) for i in range(3)]
            szt = [ph.enter_context(nc.sbuf_tensor("cb_sz%d" % i, [128, SL], F32)) for i in range(3)]
            ut = ph.enter_context(nc.sbuf_tensor("cb_u", [128, SL], F32))
            itmp = ph.enter_context(nc.sbuf_tensor("cb_it", [128, NST], F32))
            da = ph.enter_context(nc.sbuf_tensor("cb_da", [128, NST * SL], F32))
            bx = ph.enter_context(nc.sbuf_tensor("cb_bx", [128, NST * SL], F32))
            yh = [ph.enter_context(nc.sbuf_tensor("cb_yh%d" % i, [128, SL], BF16)) for i in range(2)]
            yl = [ph.enter_context(nc.sbuf_tensor("cb_yl%d" % i, [128, SL], BF16)) for i in range(2)]

            def grp(t, n0, n1, w):
                return t[:, n0 * SL:n1 * SL].rearrange("p (n t) -> p n t", n=n1 - n0)[:, :, 0:w]

            @blk.sync
            def _(sync):
                p.gate(sync)
                p.wait(sync, "mfc_a")
                p.wait(sync, "mfc_b")
                p.wait(sync, "mfaz")
                p.wait(sync, "mfaz2")
                for si, (s0, sl) in enumerate(CUBE_SEGS):
                    if si > 0:
                        p.wge(sync, "cbdc", 16 * si)
                    for n in range(NST):
                        sync.dma_start(out=Bb[:, n * SL:n * SL + sl],
                                       in_=S["bc"][n:n + 1, s0:s0 + sl].partition_broadcast(128)
                                       ).then_inc(*p.inc("cbbc", 16))
                        sync.dma_start(out=Cb[:, n * SL:n * SL + sl],
                                       in_=S["bc"][NST + n:NST + n + 1, s0:s0 + sl].partition_broadcast(128)
                                       ).then_inc(*p.inc("cbbc", 16))
                    o0, o1 = max(s0, WARM), s0 + sl
                    for dc in range(16):
                        g = si * 16 + dc
                        if g >= 3:
                            p.wge(sync, "cbdc", g - 2)
                        sync.dma_start(out=lnst[g % 3][:, 0:sl],
                                       in_=S["lns"][dc * 128:(dc + 1) * 128, s0:s0 + sl]
                                       ).then_inc(*p.inc("cbld%d" % (g % 3), 16))
                        sync.dma_start(out=xct[g % 3][:, 0:sl],
                                       in_=S["xc"][dc * 128:(dc + 1) * 128, s0:s0 + sl]
                                       ).then_inc(*p.inc("cbld%d" % (g % 3), 16))
                        sync.dma_start(out=szt[g % 3][:, 0:o1 - o0],
                                       in_=S["siluz"][dc * 128:(dc + 1) * 128, o0 - WARM:o1 - WARM]
                                       ).then_inc(*p.inc("cbld%d" % (g % 3), 16))

            @blk.scalar
            def _(se):
                p.gate(se)
                for g in range(NG):
                    si, s0, sl, o0, ol = _cseg(g)
                    dc = g % 16
                    p.wge(se, "cbld%d" % (g % 3), 48 * (g // 3 + 1))
                    if g >= 1:
                        p.wge(se, "cbsc", g)
                    for n in range(NST):
                        se.activation(out=da[:, n * SL:n * SL + sl], in_=lnst[g % 3][:, 0:sl],
                                      func=AF.Exp, scale=ealog[:, dc * NST + n:dc * NST + n + 1])
                    p.inc("cbse", 1)
                    se.drain().then_inc(p.s("cbse"), 1)

            @blk.vector
            def _(v):
                p.gate(v)
                da3 = da[:, :].rearrange("p (n t) -> p n t", n=NST)
                bx3 = bx[:, :].rearrange("p (n t) -> p n t", n=NST)
                it3 = itmp[:, 1:NST].rearrange("p (n o) -> p n o", o=1)
                for g in range(NG):
                    si, s0, sl, o0, ol = _cseg(g)
                    dc = g % 16
                    st3 = state[:, dc * NST:(dc + 1) * NST].rearrange("p (n o) -> p n o", o=1)
                    if dc == 0:
                        p.wge(v, "cbbc", 32 * 16 * (si + 1))
                    p.wge(v, "cbld%d" % (g % 3), 48 * (g // 3 + 1))
                    v.tensor_tensor(ut[:, 0:sl], lnst[g % 3][:, 0:sl], xct[g % 3][:, 0:sl], OP.mult)
                    v.drain()
                    v.tensor_tensor(grp(bx, 0, NST, sl),
                                    ut[:, 0:sl].unsqueeze(1).broadcast_to((128, NST, sl)),
                                    grp(Bb, 0, NST, sl), OP.mult)
                    v.drain()
                    p.wge(v, "cbse", g + 1)
                    v.tensor_tensor(it3, da3[:, 1:NST, 0:1], st3[:, 1:NST, :], OP.mult)
                    v.drain()
                    v.tensor_tensor(bx3[:, 1:NST, 0:1], bx3[:, 1:NST, 0:1], it3, OP.add)
                    v.drain()
                    v.tensor_scalar(da3[:, 1:NST, 0:1], da3[:, 1:NST, 0:1], 0.0, None, OP.mult)
                    v.drain()
                    v.tensor_tensor_scan(bx[:, :], da[:, :], bx[:, :],
                                         state[:, dc * NST:dc * NST + 1], OP.mult, OP.add)
                    v.drain()
                    v.tensor_copy(st3, bx3[:, :, sl - 1:sl])
                    p.inc("cbsc", 1)
                    v.drain().then_inc(p.s("cbsc"), 1)
                    v.tensor_tensor(bx[:, :], bx[:, :], Cb[:, :], OP.mult)
                    v.drain()
                    for h in (8, 4, 2, 1):
                        v.tensor_tensor(grp(bx, 0, h, sl), grp(bx, 0, h, sl),
                                        grp(bx, h, 2 * h, sl), OP.add)
                        v.drain()
                    v.scalar_tensor_tensor(bx[:, 0:sl], xct[g % 3][:, 0:sl], dpc[:, dc:dc + 1],
                                           bx[:, 0:sl], OP.mult, OP.add)
                    v.drain()
                    if g >= 2:
                        p.wge(v, "cbst%d" % (g % 2), 32 * (g // 2))
                    v.tensor_tensor(ut[:, 0:ol], bx[:, o0 - s0:o0 - s0 + ol],
                                    szt[g % 3][:, 0:ol], OP.mult)
                    v.drain()
                    v.tensor_copy(yh[g % 2][:, 0:ol], ut[:, 0:ol])
                    v.drain()
                    v.tensor_tensor(yl[g % 2][:, 0:ol], ut[:, 0:ol], yh[g % 2][:, 0:ol],
                                    OP.subtract)
                    p.inc("cbdc", 1)
                    v.drain().then_inc(p.s("cbdc"), 1)

            @blk.gpsimd
            def _(gp):
                p.gate(gp)
                for g in range(NG):
                    si, s0, sl, o0, ol = _cseg(g)
                    dc = g % 16
                    p.wge(gp, "cbdc", g + 1)
                    gp.dma_start(out=S["ysz_hi"][dc * 128:(dc + 1) * 128, o0 - WARM:o0 - WARM + ol],
                                 in_=yh[g % 2][:, 0:ol]).then_inc(*p.inc("cbst%d" % (g % 2), 16))
                    gp.dma_start(out=S["ysz_lo"][dc * 128:(dc + 1) * 128, o0 - WARM:o0 - WARM + ol],
                                 in_=yl[g % 2][:, 0:ol]).then_inc(*p.inc("cbst%d" % (g % 2), 16))

    p.mark("cb_a", "cbst0")
    p.mark("cb_b", "cbst1")
    p.endphase("cb_a", "cb_b")


def _emit_wout(nc, p, stack, I, S, sb, ps, cst):
    for half in range(2):
        tag = "wo%d" % half
        with nc.named_scope(tag), nc.Block(tag) as blk:
            with contextlib.ExitStack() as ph:
                yf = [[ph.enter_context(nc.sbuf_tensor(tag + "_y%d%s" % (i, h), [128, TOK], BF16))
                       for h in "hl"] for i in range(8)]
                wf = [[ph.enter_context(nc.sbuf_tensor(tag + "_w%d%s" % (i, h), [128, 1024], BF16))
                       for h in "hl"] for i in range(8)]
                x1t = ph.enter_context(nc.sbuf_tensor(tag + "_x1", [128, 2 * 512], F32))
                x2t = ph.enter_context(nc.sbuf_tensor(tag + "_x2", [128, 2 * 512], F32))
                L, M, E, ST = tag + "ld", tag + "mm", tag + "ev", tag + "st"

                @blk.sync
                def _(sync):
                    p.gate(sync)
                    if half == 0:
                        p.wait(sync, "cb_a")
                        p.wait(sync, "cb_b")
                    else:
                        p.wge(sync, "wo0st0", 16 * NTO)
                        p.wge(sync, "wo0st1", 16 * NTO)
                    for i in range(8):
                        dc = half * 8 + i
                        for hn, dst in (("ysz_hi", yf[i][0]), ("ysz_lo", yf[i][1])):
                            sync.dma_start(out=dst[:, :],
                                           in_=S[hn][dc * 128:(dc + 1) * 128, :]).then_inc(*p.inc(L, 16))
                        for hn, dst in (("wout_hi", wf[i][0]), ("wout_lo", wf[i][1])):
                            sync.dma_start(out=dst[:, :],
                                           in_=S[hn][0:D, dc * 128:(dc + 1) * 128],
                                           transpose=True).then_inc(*p.inc(L, 16))
                    for tc in range(NTO):
                        for dh in range(2):
                            idx = tc * 2 + dh
                            if idx >= 2:
                                p.wge(sync, E, idx - 1)
                            sync.dma_start(out=x1t[:, (idx % 2) * 512:(idx % 2) * 512 + 512],
                                           in_=S["x1"][WARM + tc * 128:WARM + tc * 128 + 128,
                                                       dh * 512:(dh + 1) * 512]).then_inc(*p.inc(L + "x%d" % (idx % 2), 16))

                @blk.tensor
                def _(tensor):
                    p.gate(tensor)
                    p.wge(tensor, L, 16 * 32)
                    for tc in range(NTO):
                        for dh in range(2):
                            idx = tc * 2 + dh
                            if idx >= 4:
                                p.wge(tensor, E, idx - 3)
                            pr = ps[idx % 4]
                            nmm = 0
                            for i in range(8):
                                for (yb, wb) in ((0, 0), (0, 1), (1, 0)):
                                    nmm += 1
                                    mm = tensor.matmul(pr[:, 0:512],
                                                       yf[i][yb][:, tc * 128:tc * 128 + 128],
                                                       wf[i][wb][:, dh * 512:(dh + 1) * 512],
                                                       start=(nmm == 1), stop=(nmm == 24))
                            mm.then_inc(*p.inc(M, 1))

                @blk.vector
                def _(v):
                    p.gate(v)
                    for tc in range(NTO):
                        for dh in range(2):
                            idx = tc * 2 + dh
                            p.wge(v, M, idx + 1)
                            p.wge(v, L + "x%d" % (idx % 2), 16 * (idx // 2 + 1))
                            if idx >= 2:
                                p.wge(v, ST + str(idx % 2), 16 * (idx // 2))
                            b = (idx % 2) * 512
                            v.tensor_tensor(x2t[:, b:b + 512], ps[idx % 4][:, 0:512],
                                            x1t[:, b:b + 512], OP.add)
                            p.inc(E, 1)
                            v.drain().then_inc(p.s(E), 1)

                @blk.scalar
                def _(se):
                    p.gate(se)
                    for tc in range(NTO):
                        for dh in range(2):
                            idx = tc * 2 + dh
                            p.wge(se, E, idx + 1)
                            se.dma_start(out=S["x1"][WARM + tc * 128:WARM + tc * 128 + 128,
                                                     dh * 512:(dh + 1) * 512],
                                         in_=x2t[:, (idx % 2) * 512:(idx % 2) * 512 + 512]
                                         ).then_inc(*p.inc(ST + str(idx % 2), 16))

                p.mark("woh%d_a" % half, ST + "0")
                p.mark("woh%d_b" % half, ST + "1")
                p.endphase("woh%d_a" % half, "woh%d_b" % half)

    p.mark("p7_a", "wo1st0")
    p.mark("p7_b", "wo1st1")
    p.endphase("p7_a", "p7_b")


def _emit_ffn1(nc, p, stack, I, S, sb, ps, cst, cols):
    C_SIN, C_SIN6, C_SGA, C_SUA, C_SDA, C_SIND, C_SIN6D = cols

    G = 4
    NQX = NTO
    elad, cdone, exps, absg = {}, {}, {}, {}
    vc = sec = 0
    for g in range(NQX // G):
        vc += 1; elad[g] = vc
        sec += 1; exps[g] = sec
        for k in range(G):
            sec += 1; absg[g * G + k] = sec
            vc += 1; cdone[g * G + k] = vc

    with nc.named_scope("f1x"), nc.Block("f1x") as blk:
        with contextlib.ExitStack() as ph:
            xb = ph.enter_context(nc.sbuf_tensor("f1x_xb", [128, G * D], F32))
            ab = ph.enter_context(nc.sbuf_tensor("f1x_ab", [128, D], F32))
            sg = ph.enter_context(nc.sbuf_tensor("f1x_sg", [128, D], F32))
            qa = ph.enter_context(nc.sbuf_tensor("f1x_qa", [128, D], F32))
            qb = ph.enter_context(nc.sbuf_tensor("f1x_qb", [128, D], F32))
            qsl = ph.enter_context(nc.sbuf_tensor("f1x_qs", [128, 8 * G * 64], F32))
            uo = [ph.enter_context(nc.sbuf_tensor("f1x_u%d" % i, [128, D], BF16)) for i in range(2)]
            NBK = D // 16

            @blk.sync
            def _(sync):
                p.gate(sync)
                p.wait(sync, "ln1_a")
                p.wait(sync, "ln1_b")
                p.wait(sync, "wqscales")
                for tc in range(NQX):
                    if tc >= G:
                        p.wge(sync, "f1xse", absg[tc - G])
                    sync.dma_start(out=xb[:, (tc % G) * D:(tc % G) * D + D],
                                   in_=S["xn1f"][WARM + tc * 128:WARM + tc * 128 + 128, :]
                                   ).then_inc(*p.inc("f1xld%d" % (tc % 4), 16))

            @blk.vector
            def _(v):
                p.gate(v)
                for g in range(NQX // G):
                    for k in range(G):
                        tc = g * G + k
                        p.wge(v, "f1xld%d" % (tc % 4), 16 * (tc // 4 + 1))
                        _q_stageA(v, qsl, G, NBK, k, xb[:, (tc % G) * D:(tc % G) * D + D], D)
                    _q_stageB_v1(v, qsl, G, NBK, G, cst[:, C_SIN6:C_SIN6 + 1])
                    p.inc("f1xv", 1)
                    v.drain().then_inc(p.s("f1xv"), 1)
                    p.wge(v, "f1xse", exps[g])
                    _q_stageB_v2(v, qsl, G, NBK, G, cst[:, C_SIN:C_SIN + 1])
                    for k in range(G):
                        tc = g * G + k
                        p.wge(v, "f1xse", absg[tc])
                        if tc >= 2:
                            p.wge(v, "f1xst%d" % (tc % 2), 16 * (tc // 2))
                        _q_stageC_v(v, qsl, G, NBK, k, ab, sg, qa, qb, uo[tc % 2][:, :], D)
                        p.inc("f1xv", 1)
                        v.drain().then_inc(p.s("f1xv"), 1)

            @blk.scalar
            def _(se):
                p.gate(se)
                for g in range(NQX // G):
                    p.wge(se, "f1xv", elad[g])
                    _q_stageB_se(se, qsl, G, NBK, G, cst[:, 14:15], cst[:, 15:16])
                    p.inc("f1xse", 1)
                    se.drain().then_inc(p.s("f1xse"), 1)
                    for k in range(G):
                        tc = g * G + k
                        if tc >= 1:
                            p.wge(se, "f1xv", cdone[tc - 1])
                        _q_stageC_se(se, xb[:, (tc % G) * D:(tc % G) * D + D], ab, sg, D)
                        p.inc("f1xse", 1)
                        se.drain().then_inc(p.s("f1xse"), 1)

            @blk.gpsimd
            def _(g_):
                p.gate(g_)
                for tc in range(NQX):
                    p.wge(g_, "f1xv", cdone[tc])
                    g_.dma_start(out=S["ux"][tc * 128:(tc + 1) * 128, :],
                                 in_=uo[tc % 2][:, :]).then_inc(*p.inc("f1xst%d" % (tc % 2), 16))

    p.mark("f1x_a", "f1xst0")
    p.mark("f1x_b", "f1xst1")
    p.endphase("f1x_a", "f1x_b")

    G = 4
    NQH = 8 * NTO
    elad, cdone, exps, absg = {}, {}, {}, {}
    vc = sec = 0
    for g in range(NQH // G):
        vc += 1; elad[g] = vc
        sec += 1; exps[g] = sec
        for k in range(G):
            sec += 1; absg[g * G + k] = sec
            vc += 1; cdone[g * G + k] = vc

    with nc.named_scope("f1g"), nc.Block("f1g") as blk:
        with contextlib.ExitStack() as ph:
            uxf = ph.enter_context(nc.sbuf_tensor("f1g_ux", [128, 8 * TOK], BF16))
            wgf = [ph.enter_context(nc.sbuf_tensor("f1g_wg%d" % i, [128, 8 * 512], BF16)) for i in range(2)]
            wuf = [ph.enter_context(nc.sbuf_tensor("f1g_wu%d" % i, [128, 8 * 512], BF16)) for i in range(2)]
            sgt = [ph.enter_context(nc.sbuf_tensor("f1g_sg%d" % i, [128, 512], F32)) for i in range(2)]
            hht = ph.enter_context(nc.sbuf_tensor("f1g_hh", [128, G * 512], F32))
            ab = ph.enter_context(nc.sbuf_tensor("f1g_ab", [128, 512], F32))
            sgn = ph.enter_context(nc.sbuf_tensor("f1g_sn", [128, 512], F32))
            qa = ph.enter_context(nc.sbuf_tensor("f1g_qa", [128, 512], F32))
            qb = ph.enter_context(nc.sbuf_tensor("f1g_qb", [128, 512], F32))
            qsl = ph.enter_context(nc.sbuf_tensor("f1g_qs", [128, 8 * G * 32], F32))
            uh = [ph.enter_context(nc.sbuf_tensor("f1g_uh%d" % i, [128, 512], BF16)) for i in range(2)]
            NBK = 512 // 16

            @blk.sync
            def _(sync):
                p.gate(sync)
                p.wait(sync, "f1x_a")
                p.wait(sync, "f1x_b")
                p.wait(sync, "wq_a")
                p.wait(sync, "wq_b")
                for hc in range(8):
                    sync.dma_start(out=uxf[:, hc * TOK:(hc + 1) * TOK],
                                   in_=S["ux"][0:TOK, hc * 128:(hc + 1) * 128],
                                   transpose=True).then_inc(*p.inc("f1gxl", 16))
                for os_ in range(8):
                    if os_ >= 2:
                        p.wge(sync, "f1gmm", 2 * NTO * (os_ - 1))
                    for wn, dst in (("uwg", wgf[os_ % 2]), ("uwu", wuf[os_ % 2])):
                        for hc in range(8):
                            sync.dma_start(out=dst[:, hc * 512:(hc + 1) * 512],
                                           in_=S[wn][os_ * 512:(os_ + 1) * 512, hc * 128:(hc + 1) * 128],
                                           transpose=True).then_inc(*p.inc("f1gwl%d" % (os_ % 2), 16))

            @blk.tensor
            def _(tensor):
                p.gate(tensor)
                p.wge(tensor, "f1gxl", 16 * 8)
                for os_ in range(8):
                    p.wge(tensor, "f1gwl%d" % (os_ % 2), 16 * 16 * (os_ // 2 + 1))
                    for tc in range(NTO):
                        idx = os_ * NTO + tc
                        if idx >= 2:
                            p.wge(tensor, "f1gsg", idx - 1)
                        if idx >= 2:
                            p.wge(tensor, "f1ghh", idx - 1)
                        pg, pu = ps[2 * (idx % 2)], ps[2 * (idx % 2) + 1]
                        for (dst, wf) in ((pg, wgf[os_ % 2]), (pu, wuf[os_ % 2])):
                            for hc in range(8):
                                mm = tensor.matmul(dst[:, 0:512],
                                                   uxf[:, hc * TOK + tc * 128:hc * TOK + tc * 128 + 128],
                                                   wf[:, hc * 512:(hc + 1) * 512],
                                                   start=(hc == 0), stop=(hc == 7))
                            mm.then_inc(*p.inc("f1gmm", 1))

            @blk.vector
            def _(v):
                p.gate(v)
                for g in range(NQH // G):
                    for k in range(G):
                        idx = g * G + k
                        p.wge(v, "f1gmm", 2 * idx + 2)
                        p.wge(v, "f1gsg", idx + 1)
                        if idx >= G:
                            p.wge(v, "f1hse", absg[idx - G])
                        v.scalar_tensor_tensor(hht[:, (idx % G) * 512:(idx % G) * 512 + 512],
                                               ps[2 * (idx % 2) + 1][:, 0:512],
                                               cst[:, C_SUA:C_SUA + 1], sgt[idx % 2][:, :],
                                               OP.mult, OP.mult)
                        p.inc("f1ghh", 1)
                        v.drain().then_inc(p.s("f1ghh"), 1)
                        _q_stageA(v, qsl, G, NBK, k, hht[:, (idx % G) * 512:(idx % G) * 512 + 512], 512)
                    _q_stageB_v1(v, qsl, G, NBK, G, cst[:, C_SIN6D:C_SIN6D + 1])
                    p.inc("f1hv", 1)
                    v.drain().then_inc(p.s("f1hv"), 1)
                    p.wge(v, "f1hse", exps[g])
                    _q_stageB_v2(v, qsl, G, NBK, G, cst[:, C_SIND:C_SIND + 1])
                    for k in range(G):
                        idx = g * G + k
                        p.wge(v, "f1hse", absg[idx])
                        if idx >= 2:
                            p.wge(v, "f1gst%d" % (idx % 2), 16 * (idx // 2))
                        _q_stageC_v(v, qsl, G, NBK, k, ab, sgn, qa, qb, uh[idx % 2][:, :], 512)
                        p.inc("f1hv", 1)
                        v.drain().then_inc(p.s("f1hv"), 1)

            @blk.scalar
            def _(se):
                p.gate(se)
                for g in range(NQH // G):
                    for k in range(G):
                        idx = g * G + k
                        p.wge(se, "f1gmm", 2 * idx + 1)
                        if idx >= 2:
                            p.wge(se, "f1ghh", idx - 1)
                        se.activation(out=sgt[idx % 2][:, :], in_=ps[2 * (idx % 2)][:, 0:512],
                                      func=AF.Silu, scale=cst[:, C_SGA:C_SGA + 1])
                        se.drain().then_inc(p.s("f1gsg"), 1)
                        p.inc("f1gsg", 1)
                    p.wge(se, "f1hv", elad[g])
                    _q_stageB_se(se, qsl, G, NBK, G, cst[:, 14:15], cst[:, 15:16])
                    p.inc("f1hse", 1)
                    se.drain().then_inc(p.s("f1hse"), 1)
                    for k in range(G):
                        idx = g * G + k
                        p.wge(se, "f1ghh", idx + 1)
                        if idx >= 1:
                            p.wge(se, "f1hv", cdone[idx - 1])
                        _q_stageC_se(se, hht[:, (idx % G) * 512:(idx % G) * 512 + 512], ab, sgn, 512)
                        p.inc("f1hse", 1)
                        se.drain().then_inc(p.s("f1hse"), 1)

            @blk.gpsimd
            def _(g_):
                p.gate(g_)
                for os_ in range(8):
                    for tc in range(NTO):
                        idx = os_ * NTO + tc
                        p.wge(g_, "f1hv", cdone[idx])
                        g_.dma_start(out=S["uh"][tc * 128:(tc + 1) * 128, os_ * 512:(os_ + 1) * 512],
                                     in_=uh[idx % 2][:, :]).then_inc(*p.inc("f1gst%d" % (idx % 2), 16))

    p.mark("f1g_a", "f1gst0")
    p.mark("f1g_b", "f1gst1")
    p.endphase("f1g_a", "f1g_b")

    for half in range(2):
        tag = "f1d%d" % half
        with nc.named_scope(tag), nc.Block(tag) as blk:
            with contextlib.ExitStack() as ph:
                uhf = [ph.enter_context(nc.sbuf_tensor(tag + "_uh%d" % i, [128, TOK], BF16))
                       for i in range(16)]
                wdf = [ph.enter_context(nc.sbuf_tensor(tag + "_wd%d" % i, [128, 1024], BF16))
                       for i in range(16)]
                x1t = ph.enter_context(nc.sbuf_tensor(tag + "_x1", [128, 2 * 512], F32))
                x2t = ph.enter_context(nc.sbuf_tensor(tag + "_x2", [128, 2 * 512], F32))
                L, M, E, ST = tag + "ld", tag + "mm", tag + "ev", tag + "st"

                @blk.sync
                def _(sync):
                    p.gate(sync)
                    p.wait(sync, "f1g_a")
                    p.wait(sync, "f1g_b")
                    if half == 0:
                        p.wait(sync, "p7_a")
                        p.wait(sync, "p7_b")
                    else:
                        p.wge(sync, "f1d0st0", 16 * NTO)
                        p.wge(sync, "f1d0st1", 16 * NTO)
                    for i in range(16):
                        oc = half * 16 + i
                        sync.dma_start(out=uhf[i][:, :],
                                       in_=S["uh"][0:TOK, oc * 128:(oc + 1) * 128],
                                       transpose=True).then_inc(*p.inc(L, 16))
                        sync.dma_start(out=wdf[i][:, :],
                                       in_=S["uwd"][0:D, oc * 128:(oc + 1) * 128],
                                       transpose=True).then_inc(*p.inc(L, 16))
                    for tc in range(NTO):
                        for dh in range(2):
                            idx = tc * 2 + dh
                            if idx >= 2:
                                p.wge(sync, E, idx - 1)
                            sync.dma_start(out=x1t[:, (idx % 2) * 512:(idx % 2) * 512 + 512],
                                           in_=S["x1"][WARM + tc * 128:WARM + tc * 128 + 128,
                                                       dh * 512:(dh + 1) * 512]).then_inc(*p.inc(L + "x%d" % (idx % 2), 16))

                @blk.tensor
                def _(tensor):
                    p.gate(tensor)
                    p.wge(tensor, L, 16 * 32)
                    for tc in range(NTO):
                        for dh in range(2):
                            idx = tc * 2 + dh
                            if idx >= 4:
                                p.wge(tensor, E, idx - 3)
                            pr = ps[4 + idx % 4]
                            for i in range(16):
                                mm = tensor.matmul(pr[:, 0:512],
                                                   uhf[i][:, tc * 128:tc * 128 + 128],
                                                   wdf[i][:, dh * 512:(dh + 1) * 512],
                                                   start=(i == 0), stop=(i == 15))
                            mm.then_inc(*p.inc(M, 1))

                @blk.vector
                def _(v):
                    p.gate(v)
                    for tc in range(NTO):
                        for dh in range(2):
                            idx = tc * 2 + dh
                            p.wge(v, M, idx + 1)
                            p.wge(v, L + "x%d" % (idx % 2), 16 * (idx // 2 + 1))
                            if idx >= 2:
                                p.wge(v, ST + str(idx % 2), 16 * (idx // 2))
                            b = (idx % 2) * 512
                            v.scalar_tensor_tensor(x2t[:, b:b + 512], ps[4 + idx % 4][:, 0:512],
                                                   cst[:, C_SDA:C_SDA + 1], x1t[:, b:b + 512],
                                                   OP.mult, OP.add)
                            p.inc(E, 1)
                            v.drain().then_inc(p.s(E), 1)

                @blk.scalar
                def _(se):
                    p.gate(se)
                    for tc in range(NTO):
                        for dh in range(2):
                            idx = tc * 2 + dh
                            p.wge(se, E, idx + 1)
                            se.dma_start(out=S["x1"][WARM + tc * 128:WARM + tc * 128 + 128,
                                                     dh * 512:(dh + 1) * 512],
                                         in_=x2t[:, (idx % 2) * 512:(idx % 2) * 512 + 512]
                                         ).then_inc(*p.inc("f1d%dst%d" % (half, idx % 2), 16))

                p.mark("f1dh%d_a" % half, "f1d%dst0" % half)
                p.mark("f1dh%d_b" % half, "f1d%dst1" % half)
                p.endphase("f1dh%d_a" % half, "f1dh%d_b" % half)

    p.mark("p8_a", "f1d1st0")
    p.mark("p8_b", "f1d1st1")



def _stage_inputs(inputs, W):
    EXT = QW + W
    x = np.asarray(inputs["L0_attn_x"]).reshape(T_SEQ, D)
    maps = []
    for c in range(NCORES):
        g0 = c * TOK - WARM
        xl = np.zeros((EXT, D), dtype=np.float32)
        s0, s1 = max(0, g0), min(T_SEQ, g0 + EXT)
        xl[s0 - g0:s1 - g0, :] = x[s0:s1, :]
        um = np.zeros((1, QW), dtype=np.float32)
        um[0, max(0, -g0):] = 1.0
        m = {
            "x": xl,
            "decay_logit": np.asarray(inputs["L0_attn_decay_logit"], np.float32).reshape(1, 1),
            "out_scale": np.asarray(inputs["L0_attn_out_scale"], np.float32).reshape(1, 1),
            "q_scale": np.asarray(inputs["L0_attn_q_scale"], np.float32).reshape(1, D),
            "k_scale": np.asarray(inputs["L0_attn_k_scale"], np.float32).reshape(1, D),
            "w_gate": np.asarray(inputs["L0_ffn_w_gate"], np.int32),
            "w_up": np.asarray(inputs["L0_ffn_w_up"], np.int32),
            "w_down": np.asarray(inputs["L0_ffn_w_down"], np.int32),
            "W_in": np.asarray(inputs["L1_attn_W_in"], np.float32),
            "conv_w": np.asarray(inputs["L1_attn_conv_w"], np.float32),
            "conv_b": np.asarray(inputs["L1_attn_conv_b"], np.float32).reshape(1, DIN),
            "W_x_t": np.ascontiguousarray(np.asarray(inputs["L1_attn_W_x"], np.float32).T),
            "W_dt_t": np.ascontiguousarray(np.asarray(inputs["L1_attn_W_dt"], np.float32).T),
            "b_dt": np.asarray(inputs["L1_attn_b_dt"], np.float32).reshape(1, DIN),
            "A_log": np.asarray(inputs["L1_attn_A_log"], np.float32),
            "Dp": np.asarray(inputs["L1_attn_Dp"], np.float32).reshape(1, DIN),
            "W_out": np.asarray(inputs["L1_attn_W_out"], np.float32),
            "g1_w": np.asarray(inputs["L1_ffn_gate_w"], np.float32),
            "u1_w": np.asarray(inputs["L1_ffn_up_w"], np.float32),
            "d1_w": np.asarray(inputs["L1_ffn_down_w"], np.float32),
            "s_in": np.asarray(inputs["L1_ffn_s_in"], np.float32).reshape(1, 1),
            "s_in_down": np.asarray(inputs["L1_ffn_s_in_down"], np.float32).reshape(1, 1),
            "umask": um,
        }
        maps.append(m)
    return maps


def kernel(**inputs):
    W = _band_width(np.asarray(inputs["L0_attn_decay_logit"]).reshape(-1)[0])
    nc, stack = build_program(W)
    try:
        maps = _stage_inputs(inputs, W)
        res = run_bass_kernel_spmd(nc, maps, core_ids=list(range(NCORES)))
    finally:
        stack.close()
    y = np.concatenate([np.asarray(r["out"]) for r in res.results], axis=0)
    return y.reshape(1, T_SEQ, D).astype(np.float32)
